# version 03
# baseline (speedup 1.0000x reference)
import math
import numpy as np
import ml_dtypes
from contextlib import ExitStack

import concourse.bass as bass
import concourse.mybir as mybir
from concourse.bass_utils import run_bass_kernel_spmd

F32 = mybir.dt.float32
BF16 = mybir.dt.bfloat16
AF = mybir.ActivationFunctionType
ALU = mybir.AluOpType
AX = mybir.AxisListType

B, T, D = 2, 4096, 4096
H, KVH, HD = 32, 8, 128
NT, TS = 8, 128
OWN = NT * TS
NFC = D // 128
EPS = 1e-5
ISQHD = 1.0 / math.sqrt(HD)
NBOUND = -1.0 / HD
GC1 = 0.7978845608028654
GC2 = 0.044715
KVROWS = 2052
VST = 132


def _inv_freq():
    THETA, FACTOR, HI_FF, LO_FF, ORIG_MAX = 500000.0, 8.0, 4.0, 1.0, 8192
    inv = 1.0 / THETA ** (np.arange(0, HD, 2, dtype=np.float64) / HD)
    wavelen = 2.0 * np.pi / inv
    low_wl = ORIG_MAX / LO_FF
    high_wl = ORIG_MAX / HI_FF
    smooth = (ORIG_MAX / wavelen - LO_FF) / (HI_FF - LO_FF)
    scaled = np.where(wavelen > low_wl, inv / FACTOR, inv)
    mid = (wavelen <= low_wl) & (wavelen >= high_wl)
    scaled = np.where(mid, (1 - smooth) * inv / FACTOR + smooth * inv, scaled)
    return scaled.astype(np.float32)


def _host_tables(r):
    inv = _inv_freq()
    j = np.arange(NT)[:, None]
    s = np.arange(TS)[None, :]
    pos = (512 * j + 4 * s + r).reshape(-1).astype(np.float32)
    fr = pos[None, :] * inv[:, None]
    cos = np.concatenate([np.cos(fr), np.cos(fr)], axis=0).astype(np.float32)
    sin = np.concatenate([np.sin(fr), np.sin(fr)], axis=0).astype(np.float32)
    sp = np.arange(TS)[:, None]
    dm = np.zeros((TS, 4 * TS), dtype=np.float32)
    for rr in range(4):
        dm[:, rr * TS:(rr + 1) * TS] = (sp < s) | ((sp == s) & (rr <= r))
    return cos, sin, dm.astype(ml_dtypes.bfloat16)


def _rot_matrix():
    R = np.zeros((HD, HD), dtype=np.float32)
    for dp in range(64):
        R[dp + 64, dp] = -1.0
    for dp in range(64, 128):
        R[dp - 64, dp] = 1.0
    return R.astype(ml_dtypes.bfloat16)


class Stream:
    __slots__ = ("h", "step", "n", "name")

    def __init__(self, h, step, name):
        self.h, self.step, self.n, self.name = h, step, 0, name


class Prog:
    def __init__(self):
        self.q = {e: [] for e in "tvsyg"}
        self.hw = {e: {} for e in "tvsyg"}
        self.all_sems = []
        self.boot = None
        self.chain = {}

    def op(self, e, fn, waits=(), inc=None, step=None):
        ch = self.chain.get(e)
        if ch is not None:
            waits = tuple(waits) + ((ch, ch.n),)
            if inc is None:
                inc = ch
        ws = []
        for w in waits:
            if w is None:
                continue
            sem, val = w
            if self.hw[e].get(sem, 0) < val:
                self.hw[e][sem] = val
                ws.append((sem, val))
        st = step if step is not None else (inc.step if inc is not None else 0)
        ev = None
        if inc is not None:
            inc.n += st
            ev = (inc, inc.n)
        self.q[e].append((ws, fn, inc, st))
        return ev

    def emit(self, nc, block):
        def mk(e):
            def body(eng):
                for (ws, fn, inc, st) in self.q[e]:
                    for (sem, val) in ws:
                        eng.wait_ge(sem.h, val)
                    ins = fn(eng)
                    if inc is not None:
                        ins.then_inc(inc.h, st)
            return body
        block.tensor(mk("t"))
        block.vector(mk("v"))
        block.scalar(mk("s"))
        block.sync(mk("y"))
        block.gpsimd(mk("g"))


def build_program(debug=False, stop_after=None):
    nc = bass.Bass("TRN2", target_bir_lowering=False)

    def din(name, shape, dt=F32):
        return nc.dram_tensor(name, shape, dt, kind="ExternalInput").ap()

    x_d = din("x_own", [OWN, D])
    wq_d = din("i_wq", [D, D])
    wk_d = din("i_wk", [D, KVH * HD])
    wv_d = din("i_wv", [D, KVH * HD])
    wo_d = din("i_wo", [D, D])
    fWqT_d = din("i_fWqT", [D, 64])
    fWrT_d = din("i_fWrT", [64, 48])
    akT_d = din("i_akT", [64, 1024])
    av_d = din("i_av", [1024, D])
    WvdT_d = din("i_WvdT", [D, 256])
    WvuT_d = din("i_WvuT", [256, D])
    Wu1T_d = din("i_Wu1T", [D, 1024])
    bu1_d = din("i_bu1c", [128, 8])
    wu2c_d = din("i_wu2c", [128, 8])
    bu2_d = din("i_bu2", [1, 1])
    gw1_d = din("i_gw1", [1, 1])
    gb_d = din("i_gb", [1, 1])
    lrel_d = din("i_lrel", [1, 1024])
    cos_d = din("c_cos", [HD, OWN])
    sin_d = din("c_sin", [HD, OWN])
    id_bf_d = din("c_idbf", [128, 128], BF16)
    id_f32_d = din("c_idf32", [128, 128], F32)
    ones_bf_d = din("c_ones", [128, 128], BF16)
    rot_d = din("c_rot", [128, 128], BF16)
    dmask_d = din("c_dmask", [128, 512], BF16)
    cscal_d = din("c_scal", [128, 8], F32)
    out_d = nc.dram_tensor("o_out", [OWN, D], F32, kind="ExternalOutput").ap()

    qspill_d = nc.dram_tensor("z_qspill", [D, OWN], BF16).ap()
    kvin_d = nc.dram_tensor("z_kvin", [KVROWS, OWN], BF16).ap()
    kvg_d = nc.dram_tensor("z_kvg", [4 * KVROWS, OWN], BF16).ap()
    ao_d = nc.dram_tensor("z_ao", [OWN, D], F32).ap()
    xp_d = nc.dram_tensor("z_xp", [OWN, D], F32).ap()
    arin_d = nc.dram_tensor("z_arin", [1, 16], F32).ap()
    arout_d = nc.dram_tensor("z_arout", [1, 16], F32).ap()

    P = Prog()
    ctx = ExitStack()

    def sem(name, step):
        s = Stream(ctx.enter_context(nc.semaphore(name)), step, name)
        P.all_sems.append(s)
        return s

    def sem2(name):
        return [sem(name + "0", 16), sem(name + "1", 16)]

    sx2 = sem2("sx"); swk2 = sem2("swk"); swv = sem("swv", 16)
    swq2 = sem2("swq"); skv = sem("skv", 16)
    skh2 = sem2("skh"); svh2 = sem2("svh"); sqr2 = sem2("sqr")
    sao = sem("sao", 16); sxp = sem("sxp", 16); sxr2 = sem2("sxr")
    swo2 = sem2("swo"); sav = sem("sav", 16); swu2 = sem2("swu")
    scst = sem("scst", 16); skm = sem("skm", 16)
    swvd = sem("swvd", 16); swvu = sem("swvu", 16)
    sout = sem("sout", 16); sar = sem("sar", 16)
    ssp = sem2("ssp"); svb = sem2("svb"); sat = sem2("sat"); soo = sem2("soo")
    st_ = sem("st", 1); sv_ = sem("sv", 1); ss_ = sem("ss", 1); cc_ = sem("cc", 1)
    P.chain = {"v": sv_, "s": ss_}

    def sb(stack, name, shape, dt, side=None):
        kw = {"side": side} if side else {}
        return stack.enter_context(nc.sbuf_tensor(name, shape, dt, **kw))

    def ps(stack, name, shape, dt=F32):
        return stack.enter_context(nc.psum_tensor(name, shape, dt))

    def dy(stream, out, in_, waits=(), **kw):
        return P.op("y", lambda e: e.dma_start(out, in_, **kw), waits, inc=stream)

    def dg(stream, out, in_, waits=(), **kw):
        return P.op("g", lambda e: e.dma_start(out, in_, **kw), waits, inc=stream)

    def dg_split(stream, out3, in3, waits=(), parts=8):
        C = out3.shape[1]
        step = C // parts
        ev = None
        for i in range(parts):
            ev = dg(stream, out3[:, i * step:(i + 1) * step, :],
                    in3[:, i * step:(i + 1) * step, :], waits=waits)
        return ev

    def t_op(fn, waits=(), inc=False):
        return P.op("t", fn, waits, inc=st_ if inc else None)

    def v_op(fn, waits=(), inc=False):
        return P.op("v", fn, waits, inc=sv_ if inc else None)

    def s_op(fn, waits=(), inc=False):
        return P.op("s", fn, waits, inc=ss_ if inc else None)

    def now(s):
        return (s, s.n)

    id_bf = sb(ctx, "id_bf", [128, 128], BF16)
    id_f32 = sb(ctx, "id_f32", [128, 128], F32)
    ones_bf = sb(ctx, "ones_bf", [128, 128], BF16)
    rot_bf = sb(ctx, "rot_bf", [128, 128], BF16)
    dmask = sb(ctx, "dmask", [128, 512], BF16)
    cos_t = sb(ctx, "cos_t", [HD, OWN], F32)
    sin_t = sb(ctx, "sin_t", [HD, OWN], F32)
    WqT = sb(ctx, "WqT", [128, NFC * 64], BF16)
    WrT = sb(ctx, "WrT", [64, 48], BF16)
    akT_raw = sb(ctx, "akT_raw", [64, 1024], BF16)
    akT_aug = sb(ctx, "akT_aug", [128, 1024], BF16)
    rkT_aug = sb(ctx, "rkT_aug", [128, 1024], BF16)
    wu2c = sb(ctx, "wu2c", [128, 8], BF16)
    bu1c = sb(ctx, "bu1c", [128, 8], F32)
    bu2_sb = sb(ctx, "bu2_sb", [1, 1], F32)
    gw1_sb = sb(ctx, "gw1_sb", [1, 1], F32)
    gb_sb = sb(ctx, "gb_sb", [1, 1], F32)
    MHC = sb(ctx, "MHC", [128, 32], F32)
    qm2 = sb(ctx, "qm2", [1, 64], F32)
    qmx = sb(ctx, "qmx", [1, 32], F32)
    kn2 = sb(ctx, "kn2", [1, 16], F32)
    knx = sb(ctx, "knx", [1, 8], F32)
    km4 = sb(ctx, "km4", [1, 32], F32)
    mrow = sb(ctx, "mrow", [1, 32], F32)
    mrow_bf = sb(ctx, "mrow_bf", [1, 32], BF16)
    stats = [sb(ctx, f"stats{i}", [128, 16], F32) for i in range(2)]
    rstd2_k = sb(ctx, "rstd2_k", [128, 8], F32)
    nmr2_k = sb(ctx, "nmr2_k", [128, 8], F32)
    sums_ao = sb(ctx, "sums_ao", [128, 8], F32)
    sumsq_ao = sb(ctx, "sumsq_ao", [128, 8], F32)
    lv_all = sb(ctx, "lv_all", [128, 8], F32)
    lffn_all = sb(ctx, "lffn_all", [128, 8], F32)
    gs_all = sb(ctx, "gs_all", [128, 8], F32)
    gcol = sb(ctx, "gcol", [128, 1], F32)
    rcol = sb(ctx, "rcol", [128, 1], F32)
    lvrow = sb(ctx, "lvrow", [1, 1024], F32)
    sigrow = sb(ctx, "sigrow", [1, 1024], F32)
    growA = sb(ctx, "growA", [1, 1024], F32)
    growB = sb(ctx, "growB", [1, 1024], F32)
    arsb = sb(ctx, "arsb", [1, 16], F32)
    negbig = sb(ctx, "negbig", [128, 1], F32)
    q_aug = sb(ctx, "q_aug", [128, 1024], BF16)
    rq_aug = sb(ctx, "rq_aug", [128, 1024], BF16)
    p5dbg = sb(ctx, "p5dbg", [128, 64], F32)

    cscal = sb(ctx, "cscal", [128, 8], F32)
    CVALS = [EPS, 1.0, -0.5, 0.125, 1.0 / math.sqrt(48.0), ISQHD, GC1, 0.0]
    for i, vv in enumerate(CVALS):
        nc.const_aps.register(F32, vv, cscal[:, i:i + 1])

    e_id = dy(scst, id_bf[:, :], id_bf_d)
    dy(scst, cscal[:, :], cscal_d)
    dy(scst, id_f32[:, :], id_f32_d)
    dy(scst, ones_bf[:, :], ones_bf_d)
    dy(scst, rot_bf[:, :], rot_d)
    dy(scst, dmask[:, :], dmask_d)
    dy(scst, cos_t[:, :], cos_d)
    dy(scst, sin_t[:, :], sin_d)
    dy(scst, bu1c[:, :], bu1_d)
    dy(scst, bu2_sb[:, :], bu2_d)
    dy(scst, gw1_sb[:, :], gw1_d)
    dy(scst, gb_sb[:, :], gb_d)
    dg_split(scst, WqT[:, :].rearrange("p (c q) -> p c q", c=NFC),
       fWqT_d.rearrange("(c p) q -> p c q", p=128))
    dg(scst, WrT[:, :], fWrT_d)
    dg(scst, akT_raw[:, :], akT_d)
    dg(scst, akT_aug[64:65, :], lrel_d)
    dg(scst, rkT_aug[48:49, :], lrel_d)
    e_wu2 = dg(scst, wu2c[:, :], wu2c_d)
    e_cst = now(scst)

    e_ms1 = v_op(lambda e: e.memset(q_aug[64:96, :], 1.0), waits=(e_cst,), inc=True)
    v_op(lambda e: e.memset(rq_aug[32:64, :], 1.0), inc=True)
    e_ms2 = v_op(lambda e: e.memset(negbig[:, :], -1e30), inc=True)

    e_ak = s_op(lambda e: e.activation(akT_aug[0:64, :], akT_raw[:, :], AF.Copy,
                                       scale=0.125), waits=(e_cst,), inc=True)

    with nc.psum_tensor("p_rk", [48, 1024], F32) as p_rk:
        for half in range(2):
            t_op(lambda e, h=half: e.matmul(p_rk[:, h * 512:(h + 1) * 512],
                                            WrT[:, :], akT_raw[:, h * 512:(h + 1) * 512],
                                            start=True, stop=True),
                 waits=(e_cst,), inc=(half == 1))
        e_rk_mm = now(st_)
        e_rk = s_op(lambda e: e.activation(rkT_aug[0:48, :], p_rk[:, :], AF.Copy,
                                           scale=1.0 / math.sqrt(48.0)),
                    waits=(e_rk_mm,), inc=True)

    if stop_after == "p0":
        do = nc.dram_tensor("dbg_p0", [128, 1024], BF16, kind="ExternalOutput").ap()
        dy(sout, do, rkT_aug[:, :], waits=(e_rk, e_ak, e_ms1, e_ms2, e_cst))
        P.op("y", lambda e: e.nop(), waits=(now(sout),))
        with nc.Block() as block:
            P.emit(nc, block)
        ctx.close()
        return nc

    p1 = ExitStack()
    x_sb = [sb(p1, f"x_sb{i}", [128, D], F32) for i in range(2)]
    ln_tm = sb(p1, "ln_tm", [128, D], BF16)
    ln_fm = sb(p1, "ln_fm", [128, NFC * OWN], BF16)
    scrA = sb(p1, "scrA", [128, 512], F32)
    scrB = sb(p1, "scrB", [128, 512], F32)
    q0_bf = [sb(p1, f"q0_bf{i}", [128, 512], BF16) for i in range(2)]
    qf_bf = [sb(p1, f"qf_bf{i}", [128, 512], BF16) for i in range(2)]
    qsq_bf = [sb(p1, f"qsq_bf{i}", [128, 512], BF16) for i in range(2)]

    p1a = ExitStack()
    p_tr = [ps(p1a, f"p_tr{i}", [128, 512], BF16) for i in range(2)]

    ev_x_free = [None, None]
    ev_tr_evac = {0: None, 1: None}
    ev_lnfm = None
    last_stats = [None, None]
    last_tr_of_tile = None
    for j in range(NT):
        sl = j % 2
        stt_t = stats[sl]
        e_x = dy(sx2[sl], x_sb[sl][:, :], x_d[j * 128:(j + 1) * 128, :],
                 waits=(ev_x_free[sl],))
        e_sum = v_op(lambda e, sl=sl, stt_t=stt_t: e.reduce_sum(
            stt_t[:, 0:1], x_sb[sl][:, :], axis=AX.X),
            waits=(e_x, last_stats[sl]), inc=True)
        e_sq = s_op(lambda e, sl=sl, stt_t=stt_t: e.activation(
            ln_tm[:, :], x_sb[sl][:, :], AF.Square, accum_out=stt_t[:, 1:2]),
            waits=(e_x, last_stats[sl], last_tr_of_tile), inc=True)
        v_op(lambda e, stt_t=stt_t: e.tensor_scalar(stt_t[:, 2:3], stt_t[:, 0:1],
                                                    1.0 / D, None, ALU.mult),
             waits=(e_sum,))
        v_op(lambda e, stt_t=stt_t: e.tensor_scalar(stt_t[:, 3:4], stt_t[:, 1:2],
                                                    1.0 / D, None, ALU.mult),
             waits=(e_sq,))
        v_op(lambda e, stt_t=stt_t: e.tensor_tensor(stt_t[:, 4:5], stt_t[:, 2:3],
                                                    stt_t[:, 2:3], ALU.mult))
        e_var = v_op(lambda e, stt_t=stt_t: e.tensor_sub(stt_t[:, 5:6], stt_t[:, 3:4],
                                                         stt_t[:, 4:5]), inc=True)
        s_op(lambda e, stt_t=stt_t: e.activation(stt_t[:, 6:7], stt_t[:, 5:6], AF.Ln,
                                                 bias=EPS), waits=(e_var,))
        e_rstd = s_op(lambda e, stt_t=stt_t: e.activation(stt_t[:, 7:8], stt_t[:, 6:7],
                                                          AF.Exp, scale=-0.5), inc=True)
        e_nmr = v_op(lambda e, stt_t=stt_t: e.tensor_scalar(stt_t[:, 8:9], stt_t[:, 2:3],
                                                            stt_t[:, 7:8], -1.0,
                                                            ALU.mult, ALU.mult),
                     waits=(e_rstd,), inc=True)
        e_ln = s_op(lambda e, sl=sl, stt_t=stt_t: e.activation(
            ln_tm[:, :], x_sb[sl][:, :], AF.Identity, bias=stt_t[:, 8:9],
            scale=stt_t[:, 7:8]), waits=(e_nmr,), inc=True)
        ev_x_free[sl] = e_ln
        last_stats[sl] = e_ln
        lnfm3 = ln_fm[:, :].rearrange("p (fc t) -> p fc t", fc=NFC)
        for g in range(8):
            par = g % 2
            for k in range(4):
                fc = g * 4 + k
                t_op(lambda e, fc=fc, par=par, k=k: e.transpose(
                        p_tr[par][:, k * 128:(k + 1) * 128],
                        ln_tm[:, fc * 128:(fc + 1) * 128], id_bf[:, :]),
                     waits=(e_ln, e_id, ev_tr_evac[par]), inc=(k == 3))
            e_trg = now(st_)
            e_ev = v_op(lambda e, g=g, j=j, par=par, lnfm3=lnfm3: e.tensor_copy(
                            lnfm3[:, g * 4:(g + 1) * 4, j * 128:(j + 1) * 128],
                            p_tr[par][:, :].rearrange("p (k t) -> p k t", k=4)),
                        waits=(e_trg,), inc=True)
            ev_tr_evac[par] = e_ev
            ev_lnfm = e_ev
        last_tr_of_tile = now(st_)
    p1a.close()

    q0_used = [None, None]
    qf_used = [None, None]
    qsq_used = [None, None]

    def rope_block(p_src, e_mm, head, half, is_k, p_rot, p_qn, war):
        par = half
        c0 = half * 512
        e_q0 = s_op(lambda e, par=par: e.activation(q0_bf[par][:, :], p_src[:, :],
                                                    AF.Copy),
                    waits=(e_mm, q0_used[par], war.get("rot")), inc=True)
        e_rot = t_op(lambda e, par=par: e.matmul(p_rot[:, :], rot_bf[:, :],
                                                 q0_bf[par][:, :],
                                                 start=True, stop=True),
                     waits=(e_q0, war.get("rotv")), inc=True)
        e_t1 = v_op(lambda e, par=par, c0=c0: e.tensor_mul(
            scrA[:, :], q0_bf[par][:, :], cos_t[:, c0:c0 + 512]),
            waits=(e_q0,), inc=True)
        q0_used[par] = e_t1
        e_t2 = v_op(lambda e, c0=c0: e.tensor_mul(scrB[:, :], p_rot[:, :],
                                                  sin_t[:, c0:c0 + 512]),
                    waits=(e_rot,), inc=True)
        war["rotv"] = e_t2
        e_qf = v_op(lambda e, par=par: e.tensor_add(qf_bf[par][:, :], scrA[:, :],
                                                    scrB[:, :]),
                    waits=(e_t1, e_t2) + tuple(qf_used[par] or ()), inc=True)
        e_sq = s_op(lambda e, par=par: e.activation(qsq_bf[par][:, :], qf_bf[par][:, :],
                                                    AF.Square),
                    waits=(e_qf, qsq_used[par]), inc=True)
        e_qn = t_op(lambda e, par=par: e.matmul(p_qn[:, :], ones_bf[:, 0:1],
                                                qsq_bf[par][:, :], start=True, stop=True),
                    waits=(e_sq, war.get("qn")), inc=True)
        qsq_used[par] = e_qn
        col = head * 2 + half
        dstn = kn2 if is_k else qm2
        e_red = v_op(lambda e, col=col, dstn=dstn: e.reduce_max(
            dstn[0:1, col:col + 1], p_qn[:, :], axis=AX.X),
            waits=(e_qn,), inc=True)
        war["qn"] = e_red
        if is_k:
            e_w = dy(ssp[par], kvin_d[head * 128:(head + 1) * 128, c0:c0 + 512],
                     qf_bf[par][:, :], waits=(e_qf,))
        else:
            e_w = dy(ssp[par], qspill_d[head * 128:(head + 1) * 128, c0:c0 + 512],
                     qf_bf[par][:, :], waits=(e_qf,))
        qf_used[par] = [e_sq, e_w]
        war["rot"] = e_q0
        return e_w, e_sq

    p1b = ExitStack()
    wkcol = [sb(p1b, f"wkcol{i}", [128, NFC * 128], BF16) for i in range(2)]
    p_pj = [ps(p1b, f"p_pj{i}", [128, 512]) for i in range(2)]
    p_rot1 = ps(p1b, "p_rot1", [128, 512])
    p_qn1 = ps(p1b, "p_qn1", [1, 512])

    war1 = {}
    ev_wk_free = [None, None]
    ev_pj_free = [None, None]
    wk_r = wk_d.rearrange("(fc p) n -> p fc n", p=128)
    for kvh in range(KVH):
        sl = kvh % 2
        e_wk = dg_split(swk2[sl], wkcol[sl][:, :].rearrange("p (c m) -> p c m", c=NFC),
                  wk_r[:, :, kvh * 128:(kvh + 1) * 128], waits=(ev_wk_free[sl],))
        last_mm = None
        for half in range(2):
            par = half
            for fc in range(NFC):
                t_op(lambda e, fc=fc, par=par, sl=sl, half=half: e.matmul(
                        p_pj[par][:, :], wkcol[sl][:, fc * 128:(fc + 1) * 128],
                        ln_fm[:, fc * OWN + half * 512: fc * OWN + half * 512 + 512],
                        start=(fc == 0), stop=(fc == NFC - 1)),
                     waits=(e_wk, ev_lnfm, ev_pj_free[par]), inc=(fc == NFC - 1))
            e_mm = now(st_)
            last_mm = e_mm
            e_w, e_q0c = rope_block(p_pj[par], e_mm, kvh, half, True,
                                    p_rot1, p_qn1, war1)
            ev_pj_free[par] = war1["rot"]
        ev_wk_free[sl] = last_mm
    e_knx = v_op(lambda e: e.reduce_max(knx[0:1, :], kn2[0:1, :].rearrange(
        "p (h a) -> p h a", a=2), axis=AX.X), inc=True)
    dg(skv, kvin_d[2048:2049, 0:8], knx[0:1, :].rearrange("p (a c) -> p a c", a=1),
       waits=(e_knx,))
    e_p1b_s = now(ss_)
    e_p1b_v = now(sv_)
    p1b.close()

    p1c = ExitStack()
    wvh = sb(p1c, "wvh", [128, NFC * 512], BF16)
    v_bf = [sb(p1c, f"v_bf{i}", [128, 512], BF16) for i in range(2)]
    p_pv = [ps(p1c, f"p_pv{i}", [128, 512]) for i in range(2)]
    wv_r = wv_d.rearrange("(fc p) n -> p fc n", p=128)
    ev_wv_free = None
    ev_pv_free = [e_p1b_s, e_p1b_v]
    ev_vbf_free = [None, None]
    for vhalf in range(2):
        e_wv = dg_split(swv, wvh[:, :].rearrange("p (c m) -> p c m", c=NFC),
                  wv_r[:, :, vhalf * 512:(vhalf + 1) * 512],
                  waits=(ev_wv_free, e_p1b_s, e_p1b_v))
        last = None
        for j in range(NT):
            par = j % 2
            for fc in range(NFC):
                t_op(lambda e, fc=fc, par=par, j=j: e.matmul(
                        p_pv[par][:, :],
                        ln_fm[:, fc * OWN + j * 128: fc * OWN + j * 128 + 128],
                        wvh[:, fc * 512:(fc + 1) * 512],
                        start=(fc == 0), stop=(fc == NFC - 1)),
                     waits=(e_wv, ev_lnfm, ev_pv_free[par]), inc=(fc == NFC - 1))
            e_mm = now(st_)
            last = e_mm
            e_ev = s_op(lambda e, par=par: e.activation(v_bf[par][:, :], p_pv[par][:, :],
                                                        AF.Copy),
                        waits=(e_mm, ev_vbf_free[par]), inc=True)
            ev_pv_free[par] = e_ev
            e_vw = dy(svb[par], kvin_d[1024 + j * 128:1024 + (j + 1) * 128,
                                  vhalf * 512:(vhalf + 1) * 512],
                      v_bf[par][:, :], waits=(e_ev,))
            ev_vbf_free[par] = e_vw
        ev_wv_free = last
    e_p1c_s = now(ss_)
    p1c.close()

    p1d = ExitStack()
    wqcol = [sb(p1d, f"wqcol{i}", [128, NFC * 128], BF16) for i in range(2)]
    p_pj2 = [ps(p1d, f"p_pj2{i}", [128, 512]) for i in range(2)]
    p_rot2 = ps(p1d, "p_rot2", [128, 512])
    p_qn2 = ps(p1d, "p_qn2", [1, 512])
    wq_r = wq_d.rearrange("(fc p) n -> p fc n", p=128)
    war2 = {}
    ev_wq_free = [None, None]
    ev_pj2_free = [e_p1c_s, e_p1c_s]
    for h in range(H):
        sl = h % 2
        e_wq = dg_split(swq2[sl], wqcol[sl][:, :].rearrange("p (c m) -> p c m", c=NFC),
                  wq_r[:, :, h * 128:(h + 1) * 128], waits=(ev_wq_free[sl], e_p1c_s))
        last_mm = None
        for half in range(2):
            par = half
            for fc in range(NFC):
                t_op(lambda e, fc=fc, par=par, sl=sl, half=half: e.matmul(
                        p_pj2[par][:, :], wqcol[sl][:, fc * 128:(fc + 1) * 128],
                        ln_fm[:, fc * OWN + half * 512: fc * OWN + half * 512 + 512],
                        start=(fc == 0), stop=(fc == NFC - 1)),
                     waits=(e_wq, ev_lnfm, ev_pj2_free[par]), inc=(fc == NFC - 1))
            e_mm = now(st_)
            last_mm = e_mm
            e_w, e_q0c = rope_block(p_pj2[par], e_mm, h, half, False,
                                    p_rot2, p_qn2, war2)
            ev_pj2_free[par] = war2["rot"]
        ev_wq_free[sl] = last_mm
    e_qmx = v_op(lambda e: e.reduce_max(qmx[0:1, :], qm2[0:1, :].rearrange(
        "p (h a) -> p h a", a=2), axis=AX.X), inc=True)
    e_p1_t = now(st_)
    e_p1_s = now(ss_)
    e_p1_v = now(sv_)
    p1d.close()
    p1.close()

    if stop_after == "p1":
        dq = nc.dram_tensor("dbg_qspill", [D, OWN], BF16, kind="ExternalOutput").ap()
        dk = nc.dram_tensor("dbg_kvin", [KVROWS, OWN], BF16, kind="ExternalOutput").ap()
        dy(sout, dq, qspill_d,
           waits=(now(ssp[0]), now(ssp[1]), now(svb[0]), now(svb[1]), now(skv),
                  e_p1_t, e_p1_s, e_p1_v))
        dy(sout, dk, kvin_d)
        P.op("y", lambda e: e.nop(), waits=(now(sout),))
        with nc.Block() as block:
            P.emit(nc, block)
        ctx.close()
        return nc

    P.op("g", lambda e: e.collective_compute(
        "AllGather", ALU.bypass,
        replica_groups=[[0, 1, 2, 3], [4, 5, 6, 7]],
        ins=[kvin_d.opt()], outs=[kvg_d.opt()]),
        waits=(now(skv), now(ssp[0]), now(ssp[1]), now(svb[0]), now(svb[1])),
        inc=cc_)
    e_ag = now(cc_)

    kvg_rr = kvg_d.rearrange("(rr rows) c -> rows rr c", rows=KVROWS)
    e_km = dg(skm, km4[0:1, :].rearrange("p (rr c) -> p rr c", rr=4),
              kvg_rr[2048:2049, :, 0:8], waits=(e_ag,))
    e_k1 = v_op(lambda e: e.tensor_max(km4[0:1, 0:8], km4[0:1, 8:16], km4[0:1, 0:8]),
                waits=(e_km,), inc=True)
    e_k2 = v_op(lambda e: e.tensor_max(km4[0:1, 16:24], km4[0:1, 24:32],
                                       km4[0:1, 16:24]), waits=(e_km,), inc=True)
    e_k3 = v_op(lambda e: e.tensor_max(km4[0:1, 0:8], km4[0:1, 0:8], km4[0:1, 16:24]),
                waits=(e_k1, e_k2), inc=True)
    e_ks = s_op(lambda e: e.activation(km4[0:1, 0:8], km4[0:1, 0:8], AF.Sqrt),
                waits=(e_k3,), inc=True)
    e_qs = s_op(lambda e: e.activation(qmx[0:1, :], qmx[0:1, :], AF.Sqrt),
                waits=(e_qmx,), inc=True)
    mrow3 = mrow[0:1, :].rearrange("p (h a) -> p h a", a=4)
    km3 = km4[0:1, 0:8].rearrange("p (h a) -> p h a", a=1)
    for sub in range(4):
        v_op(lambda e, sub=sub: e.tensor_copy(mrow3[:, :, sub:sub + 1], km3),
             waits=(e_ks,))
    e_mr = v_op(lambda e: e.scalar_tensor_tensor(mrow[0:1, :], qmx[0:1, :], NBOUND,
                                                 mrow[0:1, :], ALU.mult, ALU.mult),
                waits=(e_qs,), inc=True)
    e_mrb = s_op(lambda e: e.activation(mrow_bf[0:1, :], mrow[0:1, :], AF.Copy),
                 waits=(e_mr,), inc=True)
    with nc.psum_tensor("p_mhc", [128, 32], F32) as p_mhc:
        t_op(lambda e: e.matmul(p_mhc[:, :], ones_bf[0:1, :], mrow_bf[0:1, :],
                                start=True, stop=True),
             waits=(e_mrb, e_p1_s, e_p1_v), inc=True)
        e_mh1 = now(st_)
        e_mhc = s_op(lambda e: e.activation(MHC[:, :], p_mhc[:, :], AF.Copy),
                     waits=(e_mh1,), inc=True)

    sAO = ExitStack()
    AO_fm = sb(sAO, "AO_fm", [128, H * OWN], BF16)
    p3 = ExitStack()
    K_h = [sb(p3, f"K_h{i}", [128, 4 * OWN], BF16) for i in range(2)]
    V_h = [sb(p3, f"V_h{i}", [128, 32 * VST], BF16) for i in range(2)]
    Q_h = [sb(p3, f"Q_h{i}", [128, OWN], BF16) for i in range(2)]
    PT = sb(p3, "PT", [128, NT * 512], BF16)
    AOq = [sb(p3, f"AOq{i}", [128, 128], BF16) for i in range(2)]
    rl_sb = [sb(p3, f"rl_sb{i}", [128, 1], F32) for i in range(2)]
    p_S = [ps(p3, f"p_S{i}", [128, 512]) for i in range(2)]
    p_AV = [ps(p3, f"p_AV{i}", [128, 129]) for i in range(2)]
    p_AOT = [ps(p3, f"p_AOT{i}", [128, 128], BF16) for i in range(2)]

    ev_vones = []
    for i in range(2):
        vh3 = V_h[i][:, :].rearrange("p (m c) -> p m c", c=VST)
        ev_vones.append(v_op(lambda e, vh3=vh3: e.memset(vh3[:, :, 128:129], 1.0),
                             waits=(e_p1_t, e_p1_s, now(swq2[0]), now(swq2[1]),
                                    now(sx2[0]), now(sx2[1])), inc=True))

    ev_kh_free = [None, None]
    ev_vh_free = [None, None]
    ev_qh_free = [None, None]
    exp_ev = {0: None, 1: None}
    av_war = {0: None, 1: None}
    aot_war = {0: None, 1: None}
    aoq_war = {0: None, 1: None}
    rl_war = {0: None, 1: None}
    for kvh in range(KVH):
        ks = kvh % 2
        e_kh = e_vh = None
        for rr in range(4):
            e_kh = dy(skh2[ks], K_h[ks][:, rr * OWN:(rr + 1) * OWN],
                      kvg_d[rr * KVROWS + kvh * 128: rr * KVROWS + (kvh + 1) * 128, :],
                      waits=(e_ag, ev_kh_free[ks], e_p1_t, e_p1_s, e_p1_v))
            vdst = V_h[ks][:, :].rearrange("p (m c) -> p m c", c=VST)
            vsrc = kvg_d[rr * KVROWS + 1024: rr * KVROWS + 2048,
                         kvh * 128:(kvh + 1) * 128].rearrange(
                             "(jj s) e -> s jj e", s=128)
            e_vh = dy(svh2[ks], vdst[:, rr * 8:(rr + 1) * 8, 0:128], vsrc,
                      waits=(e_ag, ev_vh_free[ks], ev_vones[ks]))
        for qh4 in range(4):
            h = kvh * 4 + qh4
            qs = h % 2
            e_qh = dy(sqr2[qs], Q_h[qs][:, :], qspill_d[h * 128:(h + 1) * 128, :],
                      waits=(now(ssp[0]), now(ssp[1]), ev_qh_free[qs]))
            last_t_of_head = None
            for j in range(NT):
                avp = j % 2
                for i in range(j + 1):
                    spar = i % 2
                    for rr in range(4):
                        t_op(lambda e, i=i, rr=rr, j=j, spar=spar, ks=ks, qs=qs: e.matmul(
                                p_S[spar][:, rr * 128:(rr + 1) * 128],
                                K_h[ks][:, rr * OWN + i * 128: rr * OWN + i * 128 + 128],
                                Q_h[qs][:, j * 128:(j + 1) * 128],
                                start=True, stop=True),
                             waits=(e_kh, e_qh, exp_ev[spar], e_mhc),
                             inc=(rr == 3))
                    e_sc = now(st_)
                    e_exp = s_op(lambda e, i=i, spar=spar, h=h: e.activation(
                                    PT[:, i * 512:(i + 1) * 512], p_S[spar][:, :],
                                    AF.Exp, bias=MHC[:, h:h + 1], scale=ISQHD),
                                 waits=(e_sc, e_mhc), inc=True)
                    exp_ev[spar] = e_exp
                    if i == j:
                        e_ready = v_op(lambda e, i=i: e.tensor_mul(
                                        PT[:, i * 512:(i + 1) * 512],
                                        PT[:, i * 512:(i + 1) * 512], dmask[:, :]),
                                       waits=(e_exp,), inc=True)
                    else:
                        e_ready = e_exp
                    for rr in range(4):
                        t_op(lambda e, i=i, rr=rr, j=j, avp=avp, ks=ks: e.matmul(
                                p_AV[avp][:, :],
                                PT[:, i * 512 + rr * 128: i * 512 + rr * 128 + 128],
                                V_h[ks][:, (rr * 8 + i) * VST: (rr * 8 + i) * VST + 129],
                                start=(i == 0 and rr == 0), stop=(i == j and rr == 3)),
                             waits=(e_ready, e_vh, av_war[avp]),
                             inc=(i == j and rr == 3))
                e_avd = now(st_)
                last_t_of_head = e_avd
                e_rl = v_op(lambda e, avp=avp: e.reciprocal(rl_sb[avp][:, :],
                                                            p_AV[avp][:, 128:129]),
                            waits=(e_avd, rl_war[avp]), inc=True)
                e_ao = s_op(lambda e, avp=avp: e.activation(
                                AOq[avp][:, :], p_AV[avp][:, 0:128], AF.Copy,
                                scale=rl_sb[avp][:, :]),
                            waits=(e_rl, aoq_war[avp]), inc=True)
                av_war[avp] = e_ao
                rl_war[avp] = e_ao
                e_tr = t_op(lambda e, avp=avp: e.transpose(p_AOT[avp][:, :],
                                                           AOq[avp][:, :], id_bf[:, :]),
                            waits=(e_ao, aot_war[avp]), inc=True)
                aoq_war[avp] = now(st_)
                e_fm = v_op(lambda e, avp=avp, h=h, j=j: e.tensor_copy(
                                AO_fm[:, h * OWN + j * 128: h * OWN + j * 128 + 128],
                                p_AOT[avp][:, :]),
                            waits=(now(st_),), inc=True)
                aot_war[avp] = e_fm
            ev_qh_free[qs] = last_t_of_head
        ev_kh_free[ks] = last_t_of_head
        ev_vh_free[ks] = last_t_of_head
    e_attn_done = now(sv_)
    e_p3_t = now(st_)
    e_p3_s = now(ss_)
    p3.close()

    p4 = ExitStack()
    wosl = [sb(p4, f"wosl{i}", [128, NFC * 512], BF16) for i in range(2)]
    aost = [sb(p4, f"aost{i}", [128, 512], F32) for i in range(2)]
    sink4 = sb(p4, "sink4", [128, 512], F32)
    p_o2 = [ps(p4, f"p_o2{i}", [128, 512]) for i in range(2)]
    wo_r = wo_d.rearrange("(fc p) n -> p fc n", p=128)
    ev_wo_free = [None, None]
    ev_o2_free = [None, None]
    ev_aost_free = [None, None]
    acc_war = [None, None]
    for oc in range(8):
        ws = oc % 2
        e_wo = dg_split(swo2[ws], wosl[ws][:, :].rearrange("p (c m) -> p c m", c=NFC),
                  wo_r[:, :, oc * 512:(oc + 1) * 512],
                  waits=(ev_wo_free[ws], e_p3_t, e_p3_s, e_attn_done))
        last = None
        for j in range(NT):
            par = j % 2
            stt_t = stats[par]
            for fc in range(NFC):
                t_op(lambda e, fc=fc, par=par, j=j, ws=ws: e.matmul(
                        p_o2[par][:, :],
                        AO_fm[:, fc * OWN + j * 128: fc * OWN + j * 128 + 128],
                        wosl[ws][:, fc * 512:(fc + 1) * 512],
                        start=(fc == 0), stop=(fc == NFC - 1)),
                     waits=(e_wo, e_attn_done, ev_o2_free[par]), inc=(fc == NFC - 1))
            e_mm = now(st_)
            last = e_mm
            e_ev = s_op(lambda e, par=par, stt_t=stt_t: e.activation(
                            aost[par][:, :], p_o2[par][:, :], AF.Copy,
                            accum_out=stt_t[:, 9:10]),
                        waits=(e_mm, ev_aost_free[par], acc_war[par]), inc=True)
            ev_o2_free[par] = e_ev
            if oc == 0:
                v_op(lambda e, j=j, stt_t=stt_t: e.tensor_copy(
                    sums_ao[:, j:j + 1], stt_t[:, 9:10]), waits=(e_ev,))
            else:
                v_op(lambda e, j=j, stt_t=stt_t: e.tensor_add(
                    sums_ao[:, j:j + 1], sums_ao[:, j:j + 1], stt_t[:, 9:10]),
                    waits=(e_ev,))
            e_sq = s_op(lambda e, par=par, stt_t=stt_t: e.activation(
                            sink4[:, :], aost[par][:, :], AF.Square,
                            accum_out=stt_t[:, 10:11]), inc=True)
            if oc == 0:
                e_ac = v_op(lambda e, j=j, stt_t=stt_t: e.tensor_copy(
                    sumsq_ao[:, j:j + 1], stt_t[:, 10:11]), waits=(e_sq,), inc=True)
            else:
                e_ac = v_op(lambda e, j=j, stt_t=stt_t: e.tensor_add(
                    sumsq_ao[:, j:j + 1], sumsq_ao[:, j:j + 1], stt_t[:, 10:11]),
                    waits=(e_sq,), inc=True)
            acc_war[par] = e_ac
            e_w = dy(sat[par], ao_d[j * 128:(j + 1) * 128, oc * 512:(oc + 1) * 512],
                     aost[par][:, :], waits=(e_sq,))
            ev_aost_free[par] = e_w
        ev_wo_free[ws] = last
    e_p4_v = now(sv_)
    e_p4_t = now(st_)
    e_p4_s = now(ss_)
    p4.close()
    sAO.close()

    sHF = ExitStack()
    h_fm = sb(sHF, "h_fm", [128, NFC * OWN], BF16, side="right")
    p5 = ExitStack()
    xp_sb = sb(p5, "xp_sb", [128, D], F32)
    x2_sb = [sb(p5, f"x2_sb{i}", [128, D], F32) for i in range(2)]
    ao_rd = sb(p5, "ao_rd", [128, D], F32)
    adj_sb = sb(p5, "adj_sb", [128, D], F32)
    ln2_tm = sb(p5, "ln2_tm", [128, D], BF16)
    p_t5 = [ps(p5, f"p_t5{i}", [128, 512], BF16) for i in range(2)]

    ev_x2_free = [None, None]
    ev_ao_free = None
    ev_tr5 = {0: None, 1: None}
    ev_hfm = None
    ev_xp_w = None
    last_stats5 = [None, None]
    last_tr5 = None
    prev_xp = None
    prev_ln2 = None
    for j in range(NT):
        sl = j % 2
        stt_t = stats[sl]
        e_x = dy(sx2[sl], x2_sb[sl][:, :], x_d[j * 128:(j + 1) * 128, :],
                 waits=(ev_x2_free[sl], e_p4_t, e_p4_s, e_p4_v))
        e_a = dy(sao, ao_rd[:, :], ao_d[j * 128:(j + 1) * 128, :],
                 waits=(now(sat[0]), now(sat[1]), ev_ao_free, e_p4_t, e_p4_s,
                        e_p4_v))
        v_op(lambda e, j=j, stt_t=stt_t: e.tensor_scalar(
            stt_t[:, 2:3], sums_ao[:, j:j + 1], 1.0 / D, None, ALU.mult),
            waits=(e_p4_v, last_stats5[sl]))
        v_op(lambda e, j=j, stt_t=stt_t: e.tensor_scalar(
            stt_t[:, 3:4], sumsq_ao[:, j:j + 1], 1.0 / D, None, ALU.mult))
        v_op(lambda e, stt_t=stt_t: e.tensor_tensor(stt_t[:, 4:5], stt_t[:, 2:3],
                                                    stt_t[:, 2:3], ALU.mult))
        e_var = v_op(lambda e, stt_t=stt_t: e.tensor_sub(stt_t[:, 5:6], stt_t[:, 3:4],
                                                         stt_t[:, 4:5]), inc=True)
        s_op(lambda e, stt_t=stt_t: e.activation(stt_t[:, 6:7], stt_t[:, 5:6], AF.Ln,
                                                 bias=EPS),
             waits=(e_var, last_stats5[sl]))
        e_rstd = s_op(lambda e, stt_t=stt_t: e.activation(stt_t[:, 7:8], stt_t[:, 6:7],
                                                          AF.Exp, scale=-0.5), inc=True)
        e_nmr = v_op(lambda e, stt_t=stt_t: e.tensor_scalar(
            stt_t[:, 8:9], stt_t[:, 2:3], stt_t[:, 7:8], -1.0, ALU.mult, ALU.mult),
            waits=(e_rstd,), inc=True)
        v_op(lambda e, j=j, stt_t=stt_t: e.tensor_copy(
            p5dbg[:, j * 8:j * 8 + 7], stt_t[:, 2:9]), waits=(e_nmr,))
        e_adj = s_op(lambda e, sl=sl, stt_t=stt_t: e.activation(
            adj_sb[:, :], x2_sb[sl][:, :], AF.Identity, bias=stt_t[:, 8:9]),
            waits=(e_x, e_nmr, prev_xp), inc=True)
        e_xp = v_op(lambda e, stt_t=stt_t: e.scalar_tensor_tensor(
            xp_sb[:, :], ao_rd[:, :], stt_t[:, 7:8], adj_sb[:, :],
            ALU.mult, ALU.add, accum_out=stt_t[:, 11:12]),
            waits=(e_a, e_adj, prev_ln2, ev_xp_w), inc=True)
        prev_xp = e_xp
        ev_x2_free[sl] = e_xp
        ev_ao_free = e_xp
        ev_xp_w = dy(sxp, xp_d[j * 128:(j + 1) * 128, :], xp_sb[:, :], waits=(e_xp,))
        e_sq2 = s_op(lambda e, stt_t=stt_t: e.activation(
            ln2_tm[:, :], xp_sb[:, :], AF.Square, accum_out=stt_t[:, 12:13]),
            waits=(e_xp, last_tr5), inc=True)
        v_op(lambda e, stt_t=stt_t: e.tensor_scalar(stt_t[:, 2:3], stt_t[:, 11:12],
                                                    1.0 / D, None, ALU.mult),
             waits=(e_xp,))
        v_op(lambda e, stt_t=stt_t: e.tensor_scalar(stt_t[:, 3:4], stt_t[:, 12:13],
                                                    1.0 / D, None, ALU.mult),
             waits=(e_sq2,))
        v_op(lambda e, stt_t=stt_t: e.tensor_tensor(stt_t[:, 4:5], stt_t[:, 2:3],
                                                    stt_t[:, 2:3], ALU.mult))
        e_var2 = v_op(lambda e, stt_t=stt_t: e.tensor_sub(stt_t[:, 5:6], stt_t[:, 3:4],
                                                          stt_t[:, 4:5]), inc=True)
        s_op(lambda e, stt_t=stt_t: e.activation(stt_t[:, 6:7], stt_t[:, 5:6], AF.Ln,
                                                 bias=EPS), waits=(e_var2,))
        e_rstd2 = s_op(lambda e, j=j, stt_t=stt_t: e.activation(
            rstd2_k[:, j:j + 1], stt_t[:, 6:7], AF.Exp, scale=-0.5), inc=True)
        e_nmr2 = v_op(lambda e, j=j, stt_t=stt_t: e.tensor_scalar(
            nmr2_k[:, j:j + 1], stt_t[:, 2:3], rstd2_k[:, j:j + 1], -1.0,
            ALU.mult, ALU.mult), waits=(e_rstd2,), inc=True)
        e_ln2 = s_op(lambda e, j=j: e.activation(ln2_tm[:, :], xp_sb[:, :], AF.Identity,
                                                 bias=nmr2_k[:, j:j + 1],
                                                 scale=rstd2_k[:, j:j + 1]),
                     waits=(e_nmr2,), inc=True)
        prev_ln2 = e_ln2
        v_op(lambda e, stt_t=stt_t: e.tensor_scalar(stt_t[:, 13:14], stt_t[:, 5:6],
                                                    EPS, None, ALU.add),
             waits=(e_var2,))
        e_rec = v_op(lambda e, stt_t=stt_t: e.reciprocal(stt_t[:, 14:15],
                                                         stt_t[:, 13:14]), inc=True)
        e_vh2 = v_op(lambda e, stt_t=stt_t: e.tensor_tensor(
            stt_t[:, 15:16], stt_t[:, 5:6], stt_t[:, 14:15], ALU.mult),
            waits=(e_rec,), inc=True)
        e_lv = s_op(lambda e, j=j, stt_t=stt_t: e.activation(
            lv_all[:, j:j + 1], stt_t[:, 15:16], AF.Ln, bias=1.0),
            waits=(e_vh2,), inc=True)
        last_stats5[sl] = e_lv
        hfm3 = h_fm[:, :].rearrange("p (fc t) -> p fc t", fc=NFC)
        for g in range(8):
            par = g % 2
            for k in range(4):
                fc = g * 4 + k
                t_op(lambda e, fc=fc, par=par, k=k: e.transpose(
                        p_t5[par][:, k * 128:(k + 1) * 128],
                        ln2_tm[:, fc * 128:(fc + 1) * 128], id_bf[:, :]),
                     waits=(e_ln2, ev_tr5[par]), inc=(k == 3))
            e_trg = now(st_)
            e_ev = v_op(lambda e, g=g, j=j, par=par, hfm3=hfm3: e.tensor_copy(
                            hfm3[:, g * 4:(g + 1) * 4, j * 128:(j + 1) * 128],
                            p_t5[par][:, :].rearrange("p (k t) -> p k t", k=4)),
                        waits=(e_trg,), inc=True)
            ev_tr5[par] = e_ev
            ev_hfm = e_ev
        last_tr5 = now(st_)
    e_lv_done = now(ss_)

    with nc.psum_tensor("p_lvT", [1, 128], F32) as p_lvT:
        e_lvc = None
        for j in range(NT):
            e_t = t_op(lambda e, j=j: e.transpose(p_lvT[:, :], lv_all[:, j:j + 1],
                                                  id_f32[:, :]),
                       waits=(e_lv_done, e_lvc), inc=True)
            e_lvc = v_op(lambda e, j=j: e.tensor_copy(
                lvrow[0:1, j * 128:(j + 1) * 128], p_lvT[0:1, :]),
                waits=(now(st_),), inc=True)
    v_op(lambda e: e.memset(arsb[0:1, :], 0.0), waits=(e_lvc,))
    e_sumlv = v_op(lambda e: e.reduce_sum(arsb[0:1, 0:1], lvrow[0:1, :], axis=AX.X),
                   inc=True)
    e_arw = dy(sar, arin_d, arsb[0:1, :], waits=(e_sumlv,))
    e_p5_t = now(st_)
    e_p5_s = now(ss_)
    e_p5_v = now(sv_)
    p5.close()

    a1 = ExitStack()
    wu1c = [sb(a1, f"wu1c{i}", [128, NFC * 128], BF16) for i in range(2)]
    gg = sb(a1, "gg", [128, 8 * OWN], BF16)
    gb_s = sb(a1, "gb_s", [128, 512], F32)
    sqg = sb(a1, "sqg", [128, 512], F32)
    x3g = sb(a1, "x3g", [128, 512], F32)
    thg = sb(a1, "thg", [128, 512], F32)
    p_g = [ps(a1, f"p_g{i}", [128, 512]) for i in range(2)]
    p_le = [ps(a1, f"p_le{i}", [1, 512]) for i in range(2)]
    wu1_r = Wu1T_d.rearrange("(fc p) n -> p fc n", p=128)
    ev_wu_free = [None, None]
    ev_g_free = [None, None]
    prev_ggv = None
    prev_x3 = None
    for uc in range(8):
        ws = uc % 2
        e_wu = dg_split(swu2[ws], wu1c[ws][:, :].rearrange("p (c m) -> p c m", c=NFC),
                  wu1_r[:, :, uc * 128:(uc + 1) * 128],
                  waits=(ev_wu_free[ws], e_p5_t, e_p5_s, e_p5_v, now(sxp)))
        last = None
        for half in range(2):
            par = half
            for fc in range(NFC):
                t_op(lambda e, fc=fc, par=par, ws=ws, half=half: e.matmul(
                        p_g[par][:, :], wu1c[ws][:, fc * 128:(fc + 1) * 128],
                        h_fm[:, fc * OWN + half * 512: fc * OWN + half * 512 + 512],
                        start=(fc == 0), stop=(fc == NFC - 1)),
                     waits=(e_wu, ev_hfm, ev_g_free[par], e_p5_v),
                     inc=(fc == NFC - 1))
            e_mm = now(st_)
            last = e_mm
            e_gb = s_op(lambda e, par=par, uc=uc: e.activation(
                gb_s[:, :], p_g[par][:, :], AF.Identity, bias=bu1c[:, uc:uc + 1]),
                waits=(e_mm, e_cst, prev_ggv), inc=True)
            ev_g_free[par] = e_gb
            e_sqg = s_op(lambda e: e.activation(sqg[:, :], gb_s[:, :], AF.Square),
                         waits=(prev_x3,), inc=True)
            e_x3 = v_op(lambda e: e.tensor_mul(x3g[:, :], sqg[:, :], gb_s[:, :]),
                        waits=(e_sqg,), inc=True)
            prev_x3 = e_x3
            e_ti = v_op(lambda e: e.scalar_tensor_tensor(thg[:, :], x3g[:, :], GC2,
                                                         gb_s[:, :], ALU.mult, ALU.add),
                        waits=(e_x3,), inc=True)
            e_th = s_op(lambda e: e.activation(thg[:, :], thg[:, :], AF.Tanh, scale=GC1),
                        waits=(e_ti,), inc=True)
            e_t5v = v_op(lambda e: e.tensor_scalar(thg[:, :], thg[:, :], 0.5, 0.5,
                                                   ALU.mult, ALU.add),
                         waits=(e_th,), inc=True)
            e_ggv = v_op(lambda e, uc=uc, half=half: e.tensor_mul(
                            gg[:, uc * OWN + half * 512: uc * OWN + half * 512 + 512],
                            thg[:, :], gb_s[:, :]),
                         waits=(e_t5v,), inc=True)
            prev_ggv = e_ggv
            t_op(lambda e, uc=uc, half=half: e.matmul(
                    p_le[half][:, :], wu2c[:, uc:uc + 1],
                    gg[:, uc * OWN + half * 512: uc * OWN + half * 512 + 512],
                    start=(uc == 0), stop=(uc == 7)),
                 waits=(e_ggv, e_wu2), inc=(uc == 7))
        ev_wu_free[ws] = last
    e_led = now(st_)
    for half in range(2):
        s_op(lambda e, half=half: e.activation(
            sigrow[0:1, half * 512:(half + 1) * 512], p_le[half][0:1, :],
            AF.Sigmoid, bias=bu2_sb[0:1, :]),
            waits=(e_led, e_cst), inc=(half == 1))
    e_sig = now(ss_)
    e_a1_t = now(st_)
    e_a1_v = now(sv_)
    a1.close()

    sA2 = ExitStack()
    wT = sb(sA2, "wT", [128, 8 * OWN], BF16)
    w_bf = sb(sA2, "w_bf", [128, 1024], BF16)
    srf = sb(sA2, "srf", [128, 1024], F32)
    mask_u8 = sb(sA2, "mask_u8", [128, 1024], mybir.dt.uint8)
    alm = sb(sA2, "alm", [128, 1024], F32)
    top8 = sb(sA2, "top8", [128, 8], F32)
    a2p = ExitStack()
    p_qy = ps(a2p, "p_qy", [64, 512])
    p_rq = ps(a2p, "p_rq", [48, 512])
    p_sr = ps(a2p, "p_sr", [128, 1024])
    p_al = ps(a2p, "p_al", [128, 1024])
    p_wt = [ps(a2p, f"p_wt{i}", [128, 512], BF16) for i in range(2)]

    e_qy_ev = None
    e_rq_ev = None
    ev_sr_free = None
    ev_al_free = None
    ev_wt_war = {0: None, 1: None}
    prev_wx = None
    prev_sel = None
    last_wtr = None
    for half in range(2):
        for fc in range(NFC):
            t_op(lambda e, fc=fc, half=half: e.matmul(
                    p_qy[:, :], WqT[:, fc * 64:(fc + 1) * 64],
                    h_fm[:, fc * OWN + half * 512: fc * OWN + half * 512 + 512],
                    start=(fc == 0), stop=(fc == NFC - 1)),
                 waits=(e_cst, ev_hfm, e_qy_ev, e_sig, e_a1_v), inc=(fc == NFC - 1))
        e_qmm = now(st_)
        e_qy_ev = s_op(lambda e, half=half: e.activation(
            q_aug[0:64, half * 512:(half + 1) * 512], p_qy[:, :], AF.Copy),
            waits=(e_qmm, e_ms1), inc=True)
        t_op(lambda e, half=half: e.matmul(p_rq[:, :], WrT[:, :],
                                           q_aug[0:64, half * 512:(half + 1) * 512],
                                           start=True, stop=True),
             waits=(e_qy_ev, e_rq_ev), inc=True)
        e_rqm = now(st_)
        e_rq_ev = s_op(lambda e, half=half: e.activation(
            rq_aug[0:48, half * 512:(half + 1) * 512], p_rq[:, :], AF.Copy),
            waits=(e_rqm,), inc=True)
        for jj in range(4):
            j = half * 4 + jj
            for hh in range(2):
                t_op(lambda e, jj=jj, half=half, hh=hh: e.matmul(
                        p_sr[:, hh * 512:(hh + 1) * 512],
                        rq_aug[0:49, half * 512 + jj * 128: half * 512 + jj * 128 + 128],
                        rkT_aug[0:49, hh * 512:(hh + 1) * 512],
                        start=True, stop=True),
                     waits=(e_rq_ev, e_rk, ev_sr_free), inc=(hh == 1))
            e_srm = now(st_)
            e_srf = s_op(lambda e: e.activation(srf[:, :], p_sr[:, :], AF.Copy),
                         waits=(e_srm, prev_sel), inc=True)
            ev_sr_free = e_srf
            e_t8 = v_op(lambda e: e.max(out=top8[:, :], in_=srf[:, :]),
                        waits=(e_srf,), inc=True)
            e_msk = v_op(lambda e: e.tensor_scalar(mask_u8[:, :], srf[:, :],
                                                   top8[:, 7:8], None, ALU.is_ge),
                         waits=(e_t8,), inc=True)
            for hh in range(2):
                t_op(lambda e, jj=jj, half=half, hh=hh: e.matmul(
                        p_al[:, hh * 512:(hh + 1) * 512],
                        q_aug[0:65, half * 512 + jj * 128: half * 512 + jj * 128 + 128],
                        akT_aug[0:65, hh * 512:(hh + 1) * 512],
                        start=True, stop=True),
                     waits=(e_qy_ev, e_ak, ev_al_free), inc=(hh == 1))
            e_almm = now(st_)
            e_sel = v_op(lambda e: e.select(alm[:, :], mask_u8[:, :], p_al[:, :],
                                            negbig[:, :].to_broadcast([128, 1024])),
                         waits=(e_almm, e_msk, e_ms2, prev_wx), inc=True)
            prev_sel = e_sel
            ev_al_free = e_sel
            e_nm = v_op(lambda e: e.reduce_max(stats[0][:, 9:10], alm[:, :], axis=AX.X,
                                               negate=True),
                        waits=(e_sel,), inc=True)
            e_wx = s_op(lambda e, j=j: e.activation(w_bf[:, :], alm[:, :], AF.Exp,
                                                    bias=stats[0][:, 9:10],
                                                    accum_out=lffn_all[:, j:j + 1]),
                        waits=(e_nm, last_wtr), inc=True)
            prev_wx = e_wx
            wt3 = wT[:, :].rearrange("p (sc t) -> p sc t", sc=8)
            for g in range(2):
                par = g
                for k in range(4):
                    sc = g * 4 + k
                    t_op(lambda e, sc=sc, par=par, k=k: e.transpose(
                            p_wt[par][:, k * 128:(k + 1) * 128],
                            w_bf[:, sc * 128:(sc + 1) * 128], id_bf[:, :]),
                         waits=(e_wx, ev_wt_war[par]), inc=(k == 3))
                e_trg = now(st_)
                e_evw = v_op(lambda e, g=g, j=j, par=par, wt3=wt3: e.tensor_copy(
                                wt3[:, g * 4:(g + 1) * 4, j * 128:(j + 1) * 128],
                                p_wt[par][:, :].rearrange("p (k t) -> p k t", k=4)),
                             waits=(e_trg,), inc=True)
                ev_wt_war[par] = e_evw
            last_wtr = now(st_)
    e_wT_done = now(sv_)
    e_a2_s = now(ss_)
    e_a2_t = now(st_)
    a2p.close()
    sHF.close()

    sKeep = ExitStack()
    WvuT = sb(sKeep, "WvuT", [128, 2 * D], BF16, side="right")
    a2_all = sb(sKeep, "a2_all", [128, 2048], BF16, side="right")
    a3 = ExitStack()
    av_sb = sb(a3, "av_sb", [128, 8 * D], BF16)
    WvdT = sb(a3, "WvdT", [128, NFC * 256], BF16)
    aux1 = sb(a3, "aux1", [128, NFC * 512], BF16)
    a3p = ExitStack()
    p_x1 = [ps(a3p, f"p_x1{i}", [128, 512]) for i in range(2)]
    p_a2 = [ps(a3p, f"p_a2{i}", [128, 512]) for i in range(2)]

    av_r = av_d.rearrange("(sc p) n -> p sc n", p=128)
    for sc in range(8):
        dg(sav, av_sb[:, sc * D:(sc + 1) * D], av_r[:, sc, :],
           waits=(e_a2_t, e_a2_s, e_wT_done))
    e_wvd = dg_split(swvd, WvdT[:, :].rearrange("p (c m) -> p c m", c=NFC),
               WvdT_d.rearrange("(c p) n -> p c n", p=128))
    e_wvu = dg(swvu, WvuT[:, :].rearrange("p (c m) -> p c m", c=2),
               WvuT_d.rearrange("(c p) n -> p c n", p=128))

    ev_x1_free = [None, None]
    ev_a2_free = [None, None]
    for half in range(2):
        for hc in range(NFC):
            par = hc % 2
            for sc in range(8):
                t_op(lambda e, hc=hc, sc=sc, half=half, par=par: e.matmul(
                        p_x1[par][:, :],
                        av_sb[:, sc * D + hc * 128: sc * D + hc * 128 + 128],
                        wT[:, sc * OWN + half * 512: sc * OWN + half * 512 + 512],
                        start=(sc == 0), stop=(sc == 7)),
                     waits=(now(sav), e_wT_done, ev_x1_free[par]), inc=(sc == 7))
            e_mm = now(st_)
            e_ev = s_op(lambda e, hc=hc, par=par: e.activation(
                            aux1[:, hc * 512:(hc + 1) * 512], p_x1[par][:, :], AF.Copy),
                        waits=(e_mm,), inc=True)
            ev_x1_free[par] = e_ev
        e_aux1 = now(ss_)
        for vbc in range(2):
            par = vbc
            for hc in range(NFC):
                t_op(lambda e, vbc=vbc, hc=hc, par=par: e.matmul(
                        p_a2[par][:, :],
                        WvdT[:, hc * 256 + vbc * 128: hc * 256 + vbc * 128 + 128],
                        aux1[:, hc * 512:(hc + 1) * 512],
                        start=(hc == 0), stop=(hc == NFC - 1)),
                     waits=(e_wvd, e_aux1, ev_a2_free[par]), inc=(hc == NFC - 1))
            e_mm = now(st_)
            e_ev = s_op(lambda e, half=half, vbc=vbc, par=par: e.activation(
                            a2_all[:, half * 1024 + vbc * 512:
                                   half * 1024 + vbc * 512 + 512],
                            p_a2[par][:, :], AF.Copy),
                        waits=(e_mm,), inc=True)
            ev_a2_free[par] = e_ev
    e_a2_done = now(ss_)
    e_a3_t = now(st_)
    a3p.close()
    a3.close()
    sA2.close()

    P.op("g", lambda e: e.collective_compute(
        "AllReduce", ALU.add, replica_groups=[list(range(8))],
        ins=[arin_d.opt()], outs=[arout_d.opt()]),
        waits=(now(sar),), inc=cc_)
    e_arr = now(cc_)
    e_arl = dg(sar, arsb[0:1, :], arout_d, waits=(e_arr,))
    v_op(lambda e: e.tensor_scalar(growA[0:1, 0:1], arsb[0:1, 0:1], 0.008192, None,
                                   ALU.add), waits=(e_arl,))
    v_op(lambda e: e.reciprocal(growA[0:1, 1:2], growA[0:1, 0:1]))
    e_hi = v_op(lambda e: e.tensor_scalar(growA[0:1, 2:3], growA[0:1, 1:2], 4096.0,
                                          None, ALU.mult), inc=True)
    v_op(lambda e: e.tensor_scalar(growB[0:1, :], sigrow[0:1, :], 2.5, None, ALU.mult),
         waits=(e_sig,))
    v_op(lambda e: e.scalar_tensor_tensor(growA[0:1, :], lvrow[0:1, :], growA[0:1, 2:3],
                                          growB[0:1, :], ALU.mult, ALU.add),
         waits=(e_hi, e_sumlv))
    v_op(lambda e: e.tensor_scalar_max(growA[0:1, :], growA[0:1, :], 0.0))
    v_op(lambda e: e.tensor_scalar_min(growA[0:1, :], growA[0:1, :], 5.0))
    v_op(lambda e: e.tensor_scalar(growA[0:1, :], growA[0:1, :], 2.0 / 3.0, -1.0 / 3.0,
                                   ALU.mult, ALU.add))
    v_op(lambda e: e.tensor_scalar_max(growA[0:1, :], growA[0:1, :], 0.0))
    v_op(lambda e: e.tensor_scalar_min(growA[0:1, :], growA[0:1, :], 1.0))
    e_tg = v_op(lambda e: e.tensor_scalar(growA[0:1, :], growA[0:1, :], gw1_sb[0:1, :],
                                          gb_sb[0:1, :], ALU.mult, ALU.add),
                waits=(e_cst,), inc=True)
    e_gt = s_op(lambda e: e.activation(growB[0:1, :], growA[0:1, :], AF.Sigmoid),
                waits=(e_tg, e_a2_done), inc=True)
    v_op(lambda e: e.tensor_scalar(growA[0:1, :], growB[0:1, :], 0.05, None, ALU.is_ge),
         waits=(e_gt,))
    e_gate = v_op(lambda e: e.tensor_mul(growB[0:1, :], growB[0:1, :], growA[0:1, :]),
                  inc=True)
    with nc.psum_tensor("p_gT", [128, 1], F32) as p_gT:
        ev_gTc = None
        for j in range(NT):
            t_op(lambda e, j=j: e.transpose(p_gT[:, :],
                                            growB[0:1, j * 128:(j + 1) * 128],
                                            id_f32[0:1, 0:1]),
                 waits=(e_gate, ev_gTc, e_a3_t, e_a2_done), inc=True)
            e_gtt = now(st_)
            e_gc = v_op(lambda e: e.tensor_copy(gcol[:, :], p_gT[:, :]),
                        waits=(e_gtt,), inc=True)
            ev_gTc = e_gc
            v_op(lambda e, j=j: e.reciprocal(rcol[:, :], lffn_all[:, j:j + 1]),
                 waits=(e_a2_s,))
            v_op(lambda e, j=j: e.tensor_tensor(gs_all[:, j:j + 1], gcol[:, :],
                                                rcol[:, :], ALU.mult))
    e_gs = now(sv_)

    a4 = ExitStack()
    xp2 = [sb(a4, f"xp2{i}", [128, D], F32) for i in range(2)]
    hh_sb = sb(a4, "hh_sb", [128, D], F32)
    ffn_sb = sb(a4, "ffn_sb", [128, D], F32)
    adj2 = sb(a4, "adj2", [128, D], F32)
    outsb = [sb(a4, f"outsb{i}", [128, D], F32) for i in range(2)]
    sink2 = sb(a4, "sink2", [128, D], BF16)
    a4p = ExitStack()
    p_ao4 = [ps(a4p, f"p_ao4{i}", [128, 512]) for i in range(2)]

    ev_xp2_free = [None, None]
    ev_out_free = [None, None]
    ev_ao4_free = [None, None]
    prev_hh_users = None
    prev_sq3 = None
    prev_fin = None
    last_stats4 = [None, None]
    for j in range(NT):
        sl = j % 2
        half = j // 4
        jj = j % 4
        stt_t = stats[sl]
        e_xl = dy(sxr2[sl], xp2[sl][:, :], xp_d[j * 128:(j + 1) * 128, :],
                  waits=(now(sxp), ev_xp2_free[sl], e_a3_t, e_a2_done, e_gs))
        e_hh = s_op(lambda e, sl=sl, j=j: e.activation(
            hh_sb[:, :], xp2[sl][:, :], AF.Identity, bias=nmr2_k[:, j:j + 1],
            scale=rstd2_k[:, j:j + 1]),
            waits=(e_xl, e_p5_v, prev_hh_users), inc=True)
        cmb_evs = []
        for hc in range(8):
            par = hc % 2
            for vbc in range(2):
                t_op(lambda e, half=half, vbc=vbc, hc=hc, par=par, jj=jj: e.matmul(
                        p_ao4[par][:, :],
                        a2_all[:, half * 1024 + vbc * 512 + jj * 128:
                               half * 1024 + vbc * 512 + jj * 128 + 128],
                        WvuT[:, vbc * D + hc * 512: vbc * D + hc * 512 + 512],
                        start=(vbc == 0), stop=(vbc == 1)),
                     waits=(e_a2_done, e_wvu, ev_ao4_free[par], e_gs),
                     inc=(vbc == 1))
            e_mm = now(st_)
            e_cmb = v_op(lambda e, hc=hc, par=par, j=j, stt_t=stt_t: e.scalar_tensor_tensor(
                            ffn_sb[:, hc * 512:(hc + 1) * 512], p_ao4[par][:, :],
                            gs_all[:, j:j + 1], hh_sb[:, hc * 512:(hc + 1) * 512],
                            ALU.mult, ALU.add,
                            accum_out=stt_t[:, 9 + par:10 + par]),
                         waits=(e_mm, e_hh, e_gs, prev_sq3, last_stats4[sl]),
                         inc=True)
            ev_ao4_free[par] = e_cmb
            cmb_evs.append(e_cmb)
            if hc == 0:
                v_op(lambda e, stt_t=stt_t: e.tensor_copy(stt_t[:, 1:2],
                                                          stt_t[:, 9:10]))
            else:
                v_op(lambda e, par=par, stt_t=stt_t: e.tensor_add(
                    stt_t[:, 1:2], stt_t[:, 1:2], stt_t[:, 9 + par:10 + par]))
        e_ffull = now(sv_)
        e_sq3 = s_op(lambda e, stt_t=stt_t: e.activation(
            sink2[:, :], ffn_sb[:, :], AF.Square, accum_out=stt_t[:, 2:3]),
            waits=(e_ffull,), inc=True)
        prev_sq3 = e_sq3
        v_op(lambda e, stt_t=stt_t: e.tensor_scalar(stt_t[:, 3:4], stt_t[:, 1:2],
                                                    1.0 / D, None, ALU.mult))
        v_op(lambda e, stt_t=stt_t: e.tensor_scalar(stt_t[:, 4:5], stt_t[:, 2:3],
                                                    1.0 / D, None, ALU.mult),
             waits=(e_sq3,))
        v_op(lambda e, stt_t=stt_t: e.tensor_tensor(stt_t[:, 5:6], stt_t[:, 3:4],
                                                    stt_t[:, 3:4], ALU.mult))
        e_var3 = v_op(lambda e, stt_t=stt_t: e.tensor_sub(stt_t[:, 6:7], stt_t[:, 4:5],
                                                          stt_t[:, 5:6]), inc=True)
        s_op(lambda e, stt_t=stt_t: e.activation(stt_t[:, 7:8], stt_t[:, 6:7], AF.Ln,
                                                 bias=EPS), waits=(e_var3,))
        e_rstd3 = s_op(lambda e, stt_t=stt_t: e.activation(stt_t[:, 7:8], stt_t[:, 7:8],
                                                           AF.Exp, scale=-0.5), inc=True)
        e_nmr3 = v_op(lambda e, stt_t=stt_t: e.tensor_scalar(
            stt_t[:, 8:9], stt_t[:, 3:4], stt_t[:, 7:8], -1.0, ALU.mult, ALU.mult),
            waits=(e_rstd3,), inc=True)
        e_adj3 = s_op(lambda e, sl=sl, stt_t=stt_t: e.activation(
            adj2[:, :], xp2[sl][:, :], AF.Identity, bias=stt_t[:, 8:9]),
            waits=(e_nmr3, prev_fin), inc=True)
        ev_xp2_free[sl] = e_adj3
        e_fin = v_op(lambda e, sl=sl, stt_t=stt_t: e.scalar_tensor_tensor(
            outsb[sl][:, :], ffn_sb[:, :], stt_t[:, 7:8], adj2[:, :],
            ALU.mult, ALU.add),
            waits=(e_adj3, ev_out_free[sl]), inc=True)
        prev_fin = e_fin
        prev_hh_users = e_fin
        last_stats4[sl] = e_fin
        e_ow = dy(soo[sl], out_d[j * 128:(j + 1) * 128, :], outsb[sl][:, :],
                  waits=(e_fin,))
        ev_out_free[sl] = e_ow
    if debug:
        dq = nc.dram_tensor("dbg_qspill", [D, OWN], BF16, kind="ExternalOutput").ap()
        dk = nc.dram_tensor("dbg_kvin", [KVROWS, OWN], BF16, kind="ExternalOutput").ap()
        da = nc.dram_tensor("dbg_ao", [OWN, D], F32, kind="ExternalOutput").ap()
        dx = nc.dram_tensor("dbg_xp", [OWN, D], F32, kind="ExternalOutput").ap()
        dr = nc.dram_tensor("dbg_ar", [2, 16], F32, kind="ExternalOutput").ap()
        dy(sout, dq, qspill_d, waits=(now(soo[0]), now(soo[1]),))
        dy(sout, dk, kvin_d)
        dy(sout, da, ao_d)
        dy(sout, dx, xp_d)
        dy(sout, dr[0:1, :], arin_d)
        dy(sout, dr[1:2, :], arout_d)
        ds = nc.dram_tensor("dbg_stats", [128, 64], F32, kind="ExternalOutput").ap()
        dy(sout, ds[:, 0:8], sums_ao[:, :])
        dy(sout, ds[:, 8:16], sumsq_ao[:, :])
        dy(sout, ds[:, 16:24], lv_all[:, :])
        dy(sout, ds[:, 24:32], rstd2_k[:, :])
        dy(sout, ds[:, 32:40], nmr2_k[:, :])
        dy(sout, ds[:, 40:48], lffn_all[:, :])
        dy(sout, ds[:, 48:56], gs_all[:, :])
        dy(sout, ds[:, 56:64], stats[0][:, 0:8])
        dp5 = nc.dram_tensor("dbg_p5", [128, 64], F32, kind="ExternalOutput").ap()
        dy(sout, dp5, p5dbg[:, :])
    P.op("y", lambda e: e.nop(), waits=(now(soo[0]), now(soo[1]), now(sout)))
    a4p.close()
    a4.close()
    sKeep.close()

    with nc.Block() as block:
        P.emit(nc, block)
    ctx.close()
    return nc


_NC_CACHE = [None]


def prepare_in_maps(L0_attn_x, L0_attn_wq, L0_attn_wk, L0_attn_wv, L0_attn_wo,
                    L0_ffn_Wq, L0_ffn_Wr, L0_ffn_aux_keys, L0_ffn_aux_values,
                    L0_ffn_Wvd, L0_ffn_Wvu, L0_ffn_Wu1, L0_ffn_bu1, L0_ffn_Wu2,
                    L0_ffn_bu2, L0_ffn_gate_w1, L0_ffn_gate_bias, L0_ffn_log_rel):
    f = np.float32
    x = np.asarray(L0_attn_x, f)
    shared = dict(
        i_wq=np.ascontiguousarray(np.asarray(L0_attn_wq, f)),
        i_wk=np.ascontiguousarray(np.asarray(L0_attn_wk, f)),
        i_wv=np.ascontiguousarray(np.asarray(L0_attn_wv, f)),
        i_wo=np.ascontiguousarray(np.asarray(L0_attn_wo, f)),
        i_fWqT=np.ascontiguousarray(np.asarray(L0_ffn_Wq, f).T),
        i_fWrT=np.ascontiguousarray(np.asarray(L0_ffn_Wr, f).T),
        i_akT=np.ascontiguousarray(np.asarray(L0_ffn_aux_keys, f).T),
        i_av=np.ascontiguousarray(np.asarray(L0_ffn_aux_values, f)),
        i_WvdT=np.ascontiguousarray(np.asarray(L0_ffn_Wvd, f).T),
        i_WvuT=np.ascontiguousarray(np.asarray(L0_ffn_Wvu, f).T),
        i_Wu1T=np.ascontiguousarray(np.asarray(L0_ffn_Wu1, f).T),
        i_bu1c=np.ascontiguousarray(np.asarray(L0_ffn_bu1, f).reshape(8, 128).T),
        i_wu2c=np.ascontiguousarray(np.asarray(L0_ffn_Wu2, f).reshape(8, 128).T),
        i_bu2=np.asarray(L0_ffn_bu2, f).reshape(1, 1),
        i_gw1=np.asarray(L0_ffn_gate_w1, f).reshape(1, 1),
        i_gb=np.asarray(L0_ffn_gate_bias, f).reshape(1, 1),
        i_lrel=np.ascontiguousarray(np.asarray(L0_ffn_log_rel, f).reshape(1, 1024)),
        c_scal=np.ascontiguousarray(np.tile(np.array(
            [EPS, 1.0, -0.5, 0.125, 1.0 / math.sqrt(48.0), ISQHD, GC1, 0.0],
            dtype=np.float32), (128, 1))),
        c_idbf=np.eye(128, dtype=ml_dtypes.bfloat16),
        c_idf32=np.eye(128, dtype=np.float32),
        c_ones=np.ones((128, 128), dtype=ml_dtypes.bfloat16),
        c_rot=_rot_matrix(),
    )
    in_maps = []
    idx_by_r = {}
    for c in range(8):
        b, r = c // 4, c % 4
        if r not in idx_by_r:
            j = np.arange(NT)[:, None]
            s = np.arange(TS)[None, :]
            idx_by_r[r] = (512 * j + 4 * s + r).reshape(-1)
        idx = idx_by_r[r]
        cos, sin, dm = _host_tables(r)
        m = dict(shared)
        m["x_own"] = np.ascontiguousarray(x[b][idx])
        m["c_cos"] = cos
        m["c_sin"] = sin
        m["c_dmask"] = dm
        in_maps.append(m)
    return in_maps, idx_by_r


def kernel(**inputs):
    in_maps, idx_by_r = prepare_in_maps(**inputs)
    if _NC_CACHE[0] is None:
        _NC_CACHE[0] = build_program()
    nc = _NC_CACHE[0]
    res = run_bass_kernel_spmd(nc, in_maps, core_ids=list(range(8)))
    out = np.empty((B, T, D), dtype=np.float32)
    for c in range(8):
        b, r = c // 4, c % 4
        out[b][idx_by_r[r]] = np.asarray(res.results[c]["o_out"], np.float32)
    return out


# version 06
# speedup vs baseline: 1.4570x; 1.4570x over previous
import math
import numpy as np
import ml_dtypes
from contextlib import ExitStack

import concourse.bass as bass
import concourse.mybir as mybir
from concourse.bass_utils import run_bass_kernel_spmd

F32 = mybir.dt.float32
BF16 = mybir.dt.bfloat16
AF = mybir.ActivationFunctionType
ALU = mybir.AluOpType
AX = mybir.AxisListType

B, T, D = 2, 4096, 4096
H, KVH, HD = 32, 8, 128
NT, TS = 8, 128
OWN = NT * TS
NFC = D // 128
EPS = 1e-5
ISQHD = 1.0 / math.sqrt(HD)
NBOUND = -1.0 / HD
GC1 = 0.7978845608028654
GC2 = 0.044715
KVROWS = 2052
VST = 132


def _inv_freq():
    THETA, FACTOR, HI_FF, LO_FF, ORIG_MAX = 500000.0, 8.0, 4.0, 1.0, 8192
    inv = 1.0 / THETA ** (np.arange(0, HD, 2, dtype=np.float64) / HD)
    wavelen = 2.0 * np.pi / inv
    low_wl = ORIG_MAX / LO_FF
    high_wl = ORIG_MAX / HI_FF
    smooth = (ORIG_MAX / wavelen - LO_FF) / (HI_FF - LO_FF)
    scaled = np.where(wavelen > low_wl, inv / FACTOR, inv)
    mid = (wavelen <= low_wl) & (wavelen >= high_wl)
    scaled = np.where(mid, (1 - smooth) * inv / FACTOR + smooth * inv, scaled)
    return scaled.astype(np.float32)


def _host_tables(r):
    inv = _inv_freq()
    j = np.arange(NT)[:, None]
    s = np.arange(TS)[None, :]
    pos = (512 * j + 4 * s + r).reshape(-1).astype(np.float32)
    fr = pos[None, :] * inv[:, None]
    cos = np.concatenate([np.cos(fr), np.cos(fr)], axis=0).astype(np.float32)
    sin = np.concatenate([np.sin(fr), np.sin(fr)], axis=0).astype(np.float32)
    sp = np.arange(TS)[:, None]
    dm = np.zeros((TS, 4 * TS), dtype=np.float32)
    for rr in range(4):
        dm[:, rr * TS:(rr + 1) * TS] = (sp < s) | ((sp == s) & (rr <= r))
    return cos, sin, dm.astype(ml_dtypes.bfloat16)


def _rot_matrix():
    R = np.zeros((HD, HD), dtype=np.float32)
    for dp in range(64):
        R[dp + 64, dp] = -1.0
    for dp in range(64, 128):
        R[dp - 64, dp] = 1.0
    return R.astype(ml_dtypes.bfloat16)


class Stream:
    __slots__ = ("h", "step", "n", "name")

    def __init__(self, h, step, name):
        self.h, self.step, self.n, self.name = h, step, 0, name


class Prog:
    def __init__(self):
        self.q = {e: [] for e in "tvsyg"}
        self.hw = {e: {} for e in "tvsyg"}
        self.all_sems = []
        self.boot = None
        self.chain = {}

    def op(self, e, fn, waits=(), inc=None, step=None):
        ch = self.chain.get(e)
        if ch is not None:
            waits = tuple(waits) + ((ch, ch.n),)
            if inc is None:
                inc = ch
        ws = []
        for w in waits:
            if w is None:
                continue
            sem, val = w
            if self.hw[e].get(sem, 0) < val:
                self.hw[e][sem] = val
                ws.append((sem, val))
        st = step if step is not None else (inc.step if inc is not None else 0)
        ev = None
        if inc is not None:
            inc.n += st
            ev = (inc, inc.n)
        self.q[e].append((ws, fn, inc, st))
        return ev

    def emit(self, nc, block):
        def mk(e):
            def body(eng):
                for (ws, fn, inc, st) in self.q[e]:
                    for (sem, val) in ws:
                        eng.wait_ge(sem.h, val)
                    ins = fn(eng)
                    if inc is not None:
                        ins.then_inc(inc.h, st)
            return body
        block.tensor(mk("t"))
        block.vector(mk("v"))
        block.scalar(mk("s"))
        block.sync(mk("y"))
        block.gpsimd(mk("g"))


def build_program(debug=False, stop_after=None):
    nc = bass.Bass("TRN2", target_bir_lowering=False)

    def din(name, shape, dt=F32):
        return nc.dram_tensor(name, shape, dt, kind="ExternalInput").ap()

    x_d = din("x_own", [OWN, D])
    wq_d = din("i_wq", [D, D])
    wk_d = din("i_wk", [D, KVH * HD])
    wv_d = din("i_wv", [D, KVH * HD])
    wo_d = din("i_wo", [D, D])
    fWqT_d = din("i_fWqT", [D, 64])
    fWrT_d = din("i_fWrT", [64, 48])
    akT_d = din("i_akT", [64, 1024])
    av_d = din("i_av", [1024, D])
    WvdT_d = din("i_WvdT", [D, 256])
    WvuT_d = din("i_WvuT", [256, D])
    Wu1T_d = din("i_Wu1T", [D, 1024])
    bu1_d = din("i_bu1c", [128, 8])
    wu2c_d = din("i_wu2c", [128, 8])
    bu2_d = din("i_bu2", [1, 1])
    gw1_d = din("i_gw1", [1, 1])
    gb_d = din("i_gb", [1, 1])
    lrel_d = din("i_lrel", [1, 1024])
    cos_d = din("c_cos", [HD, OWN])
    sin_d = din("c_sin", [HD, OWN])
    id_bf_d = din("c_idbf", [128, 128], BF16)
    id_f32_d = din("c_idf32", [128, 128], F32)
    ones_bf_d = din("c_ones", [128, 128], BF16)
    rot_d = din("c_rot", [128, 128], BF16)
    dmask_d = din("c_dmask", [128, 512], BF16)
    cscal_d = din("c_scal", [128, 8], F32)
    out_d = nc.dram_tensor("o_out", [OWN, D], F32, kind="ExternalOutput").ap()

    qspill_d = nc.dram_tensor("z_qspill", [D, OWN], BF16).ap()
    kvin_d = nc.dram_tensor("z_kvin", [KVROWS, OWN], BF16).ap()
    kvg_d = nc.dram_tensor("z_kvg", [4 * KVROWS, OWN], BF16).ap()
    ao_d = nc.dram_tensor("z_ao", [OWN, D], F32).ap()
    xp_d = nc.dram_tensor("z_xp", [OWN, D], F32).ap()
    arin_d = nc.dram_tensor("z_arin", [1, 16], F32).ap()
    arout_d = nc.dram_tensor("z_arout", [1, 16], F32).ap()

    P = Prog()
    ctx = ExitStack()

    def sem(name, step):
        s = Stream(ctx.enter_context(nc.semaphore(name)), step, name)
        P.all_sems.append(s)
        return s

    def sem2(name):
        return [sem(name + "0", 16), sem(name + "1", 16)]

    sx2 = sem2("sx"); swk2 = sem2("swk"); swv = sem("swv", 16)
    swq4 = [sem(f"swq{i}", 16) for i in range(4)]; skv = sem("skv", 16)
    skh2 = sem2("skh"); svh2 = sem2("svh"); sqr2 = sem2("sqr")
    sao = sem("sao", 16); sxp = sem("sxp", 16); sxr2 = sem2("sxr")
    swo2 = sem2("swo"); sav = sem("sav", 16); swu2 = sem2("swu")
    scst = sem("scst", 16); skm = sem("skm", 16)
    swvd = sem("swvd", 16); swvu = sem("swvu", 16)
    sout = sem("sout", 16); sar = sem("sar", 16)
    ssp = sem2("ssp"); svb = sem2("svb"); sat = sem2("sat"); soo = sem2("soo")
    st_ = sem("st", 1); sv_ = sem("sv", 1); ss_ = sem("ss", 1); cc_ = sem("cc", 1)
    P.chain = {"v": sv_, "s": ss_}

    def sb(stack, name, shape, dt, side=None):
        kw = {"side": side} if side else {}
        return stack.enter_context(nc.sbuf_tensor(name, shape, dt, **kw))

    def ps(stack, name, shape, dt=F32):
        return stack.enter_context(nc.psum_tensor(name, shape, dt))

    def dy(stream, out, in_, waits=(), **kw):
        return P.op("y", lambda e: e.dma_start(out, in_, **kw), waits, inc=stream)

    def dg(stream, out, in_, waits=(), **kw):
        return P.op("g", lambda e: e.dma_start(out, in_, **kw), waits, inc=stream)

    def dg_split(stream, out3, in3, waits=(), parts=8):
        C = out3.shape[1]
        step = C // parts
        ev = None
        for i in range(parts):
            ev = dg(stream, out3[:, i * step:(i + 1) * step, :],
                    in3[:, i * step:(i + 1) * step, :], waits=waits)
        return ev

    def t_op(fn, waits=(), inc=False):
        return P.op("t", fn, waits, inc=st_ if inc else None)

    def v_op(fn, waits=(), inc=False):
        return P.op("v", fn, waits, inc=sv_ if inc else None)

    def s_op(fn, waits=(), inc=False):
        return P.op("s", fn, waits, inc=ss_ if inc else None)

    def now(s):
        return (s, s.n)

    id_bf = sb(ctx, "id_bf", [128, 128], BF16)
    id_f32 = sb(ctx, "id_f32", [128, 128], F32)
    ones_bf = sb(ctx, "ones_bf", [128, 128], BF16)
    rot_bf = sb(ctx, "rot_bf", [128, 128], BF16)
    dmask = sb(ctx, "dmask", [128, 512], BF16)
    cos_t = sb(ctx, "cos_t", [HD, OWN], F32)
    sin_t = sb(ctx, "sin_t", [HD, OWN], F32)
    WqT = sb(ctx, "WqT", [128, NFC * 64], BF16)
    WrT = sb(ctx, "WrT", [64, 48], BF16)
    akT_raw = sb(ctx, "akT_raw", [64, 1024], BF16)
    akT_aug = sb(ctx, "akT_aug", [128, 1024], BF16)
    rkT_aug = sb(ctx, "rkT_aug", [128, 1024], BF16)
    wu2c = sb(ctx, "wu2c", [128, 8], BF16)
    bu1c = sb(ctx, "bu1c", [128, 8], F32)
    bu2_sb = sb(ctx, "bu2_sb", [1, 1], F32)
    gw1_sb = sb(ctx, "gw1_sb", [1, 1], F32)
    gb_sb = sb(ctx, "gb_sb", [1, 1], F32)
    MHC = sb(ctx, "MHC", [128, 32], F32)
    qm2 = sb(ctx, "qm2", [1, 64], F32)
    qmx = sb(ctx, "qmx", [1, 32], F32)
    kn2 = sb(ctx, "kn2", [1, 16], F32)
    knx = sb(ctx, "knx", [1, 8], F32)
    km4 = sb(ctx, "km4", [1, 32], F32)
    mrow = sb(ctx, "mrow", [1, 32], F32)
    mrow_bf = sb(ctx, "mrow_bf", [1, 32], BF16)
    stats = [sb(ctx, f"stats{i}", [128, 16], F32) for i in range(2)]
    rstd2_k = sb(ctx, "rstd2_k", [128, 8], F32)
    nmr2_k = sb(ctx, "nmr2_k", [128, 8], F32)
    sums_ao = sb(ctx, "sums_ao", [128, 8], F32)
    sumsq_ao = sb(ctx, "sumsq_ao", [128, 8], F32)
    lv_all = sb(ctx, "lv_all", [128, 8], F32)
    lffn_all = sb(ctx, "lffn_all", [128, 8], F32)
    gs_all = sb(ctx, "gs_all", [128, 8], F32)
    gcol = sb(ctx, "gcol", [128, 1], F32)
    rcol = sb(ctx, "rcol", [128, 1], F32)
    lvrow = sb(ctx, "lvrow", [1, 1024], F32)
    sigrow = sb(ctx, "sigrow", [1, 1024], F32)
    growA = sb(ctx, "growA", [1, 1024], F32)
    growB = sb(ctx, "growB", [1, 1024], F32)
    arsb = sb(ctx, "arsb", [1, 16], F32)
    negbig = sb(ctx, "negbig", [128, 1], F32)
    q_aug = sb(ctx, "q_aug", [128, 1024], BF16)
    rq_aug = sb(ctx, "rq_aug", [128, 1024], BF16)
    p5dbg = sb(ctx, "p5dbg", [128, 64], F32)

    cscal = sb(ctx, "cscal", [128, 8], F32)
    CVALS = [EPS, 1.0, -0.5, 0.125, 1.0 / math.sqrt(48.0), ISQHD, GC1, 0.0]
    for i, vv in enumerate(CVALS):
        nc.const_aps.register(F32, vv, cscal[:, i:i + 1])

    e_id = dy(scst, id_bf[:, :], id_bf_d)
    dy(scst, cscal[:, :], cscal_d)
    dy(scst, id_f32[:, :], id_f32_d)
    dy(scst, ones_bf[:, :], ones_bf_d)
    dy(scst, rot_bf[:, :], rot_d)
    dy(scst, dmask[:, :], dmask_d)
    dy(scst, cos_t[:, :], cos_d)
    dy(scst, sin_t[:, :], sin_d)
    dy(scst, bu1c[:, :], bu1_d)
    dy(scst, bu2_sb[:, :], bu2_d)
    dy(scst, gw1_sb[:, :], gw1_d)
    dy(scst, gb_sb[:, :], gb_d)
    dg_split(scst, WqT[:, :].rearrange("p (c q) -> p c q", c=NFC),
       fWqT_d.rearrange("(c p) q -> p c q", p=128))
    dg(scst, WrT[:, :], fWrT_d)
    dg(scst, akT_raw[:, :], akT_d)
    dg(scst, akT_aug[64:65, :], lrel_d)
    dg(scst, rkT_aug[48:49, :], lrel_d)
    e_wu2 = dg(scst, wu2c[:, :], wu2c_d)
    e_cst = now(scst)

    e_ms1 = v_op(lambda e: e.memset(q_aug[64:96, :], 1.0), waits=(e_cst,), inc=True)
    v_op(lambda e: e.memset(rq_aug[32:64, :], 1.0), inc=True)
    e_ms2 = v_op(lambda e: e.memset(negbig[:, :], -1e30), inc=True)

    e_ak = s_op(lambda e: e.activation(akT_aug[0:64, :], akT_raw[:, :], AF.Copy,
                                       scale=0.125), waits=(e_cst,), inc=True)

    with nc.psum_tensor("p_rk", [48, 1024], F32) as p_rk:
        for half in range(2):
            t_op(lambda e, h=half: e.matmul(p_rk[:, h * 512:(h + 1) * 512],
                                            WrT[:, :], akT_raw[:, h * 512:(h + 1) * 512],
                                            start=True, stop=True),
                 waits=(e_cst,), inc=(half == 1))
        e_rk_mm = now(st_)
        e_rk = s_op(lambda e: e.activation(rkT_aug[0:48, :], p_rk[:, :], AF.Copy,
                                           scale=1.0 / math.sqrt(48.0)),
                    waits=(e_rk_mm,), inc=True)

    if stop_after == "p0":
        do = nc.dram_tensor("dbg_p0", [128, 1024], BF16, kind="ExternalOutput").ap()
        dy(sout, do, rkT_aug[:, :], waits=(e_rk, e_ak, e_ms1, e_ms2, e_cst))
        P.op("y", lambda e: e.nop(), waits=(now(sout),))
        with nc.Block() as block:
            P.emit(nc, block)
        ctx.close()
        return nc

    p1 = ExitStack()
    x_sb = [sb(p1, f"x_sb{i}", [128, D], F32) for i in range(2)]
    ln_tm = sb(p1, "ln_tm", [128, D], BF16)
    ln_fm = sb(p1, "ln_fm", [128, NFC * OWN], BF16)
    scrA = sb(p1, "scrA", [128, 512], F32)
    scrB = sb(p1, "scrB", [128, 512], F32)
    q0_bf = [sb(p1, f"q0_bf{i}", [128, 512], BF16) for i in range(2)]
    qf_bf = [sb(p1, f"qf_bf{i}", [128, 512], BF16) for i in range(2)]
    qsq_bf = [sb(p1, f"qsq_bf{i}", [128, 512], BF16) for i in range(2)]

    p1a = ExitStack()
    p_tr = [ps(p1a, f"p_tr{i}", [128, 512], BF16) for i in range(2)]

    ev_x_free = [None, None]
    ev_tr_evac = {0: None, 1: None}
    ev_lnfm = None
    last_stats = [None, None]
    last_tr_of_tile = None
    for j in range(NT):
        sl = j % 2
        stt_t = stats[sl]
        e_x = dy(sx2[sl], x_sb[sl][:, :], x_d[j * 128:(j + 1) * 128, :],
                 waits=(ev_x_free[sl],))
        e_sum = v_op(lambda e, sl=sl, stt_t=stt_t: e.reduce_sum(
            stt_t[:, 0:1], x_sb[sl][:, :], axis=AX.X),
            waits=(e_x, last_stats[sl]), inc=True)
        e_sq = s_op(lambda e, sl=sl, stt_t=stt_t: e.activation(
            ln_tm[:, :], x_sb[sl][:, :], AF.Square, accum_out=stt_t[:, 1:2]),
            waits=(e_x, last_stats[sl], last_tr_of_tile), inc=True)
        v_op(lambda e, stt_t=stt_t: e.tensor_scalar(stt_t[:, 2:3], stt_t[:, 0:1],
                                                    1.0 / D, None, ALU.mult),
             waits=(e_sum,))
        v_op(lambda e, stt_t=stt_t: e.tensor_scalar(stt_t[:, 3:4], stt_t[:, 1:2],
                                                    1.0 / D, None, ALU.mult),
             waits=(e_sq,))
        v_op(lambda e, stt_t=stt_t: e.tensor_tensor(stt_t[:, 4:5], stt_t[:, 2:3],
                                                    stt_t[:, 2:3], ALU.mult))
        e_var = v_op(lambda e, stt_t=stt_t: e.tensor_sub(stt_t[:, 5:6], stt_t[:, 3:4],
                                                         stt_t[:, 4:5]), inc=True)
        s_op(lambda e, stt_t=stt_t: e.activation(stt_t[:, 6:7], stt_t[:, 5:6], AF.Ln,
                                                 bias=EPS), waits=(e_var,))
        e_rstd = s_op(lambda e, stt_t=stt_t: e.activation(stt_t[:, 7:8], stt_t[:, 6:7],
                                                          AF.Exp, scale=-0.5), inc=True)
        e_nmr = v_op(lambda e, stt_t=stt_t: e.tensor_scalar(stt_t[:, 8:9], stt_t[:, 2:3],
                                                            stt_t[:, 7:8], -1.0,
                                                            ALU.mult, ALU.mult),
                     waits=(e_rstd,), inc=True)
        e_ln = s_op(lambda e, sl=sl, stt_t=stt_t: e.activation(
            ln_tm[:, :], x_sb[sl][:, :], AF.Identity, bias=stt_t[:, 8:9],
            scale=stt_t[:, 7:8]), waits=(e_nmr,), inc=True)
        ev_x_free[sl] = e_ln
        last_stats[sl] = e_ln
        lnfm3 = ln_fm[:, :].rearrange("p (fc t) -> p fc t", fc=NFC)
        for g in range(8):
            par = g % 2
            for k in range(4):
                fc = g * 4 + k
                t_op(lambda e, fc=fc, par=par, k=k: e.transpose(
                        p_tr[par][:, k * 128:(k + 1) * 128],
                        ln_tm[:, fc * 128:(fc + 1) * 128], id_bf[:, :]),
                     waits=(e_ln, e_id, ev_tr_evac[par]), inc=(k == 3))
            e_trg = now(st_)
            e_ev = v_op(lambda e, g=g, j=j, par=par, lnfm3=lnfm3: e.tensor_copy(
                            lnfm3[:, g * 4:(g + 1) * 4, j * 128:(j + 1) * 128],
                            p_tr[par][:, :].rearrange("p (k t) -> p k t", k=4)),
                        waits=(e_trg,), inc=True)
            ev_tr_evac[par] = e_ev
            ev_lnfm = e_ev
        last_tr_of_tile = now(st_)
    p1a.close()

    q0_used = [None, None]
    qf_used = [None, None]
    qsq_used = [None, None]

    def rope_block(p_src, e_mm, head, half, is_k, p_rot, p_qn, war):
        par = half
        c0 = half * 512
        e_q0 = s_op(lambda e, par=par: e.activation(q0_bf[par][:, :], p_src[:, :],
                                                    AF.Copy),
                    waits=(e_mm, q0_used[par], war.get("rot")), inc=True)
        e_rot = t_op(lambda e, par=par: e.matmul(p_rot[:, :], rot_bf[:, :],
                                                 q0_bf[par][:, :],
                                                 start=True, stop=True),
                     waits=(e_q0, war.get("rotv")), inc=True)
        e_t1 = v_op(lambda e, par=par, c0=c0: e.tensor_mul(
            scrA[:, :], q0_bf[par][:, :], cos_t[:, c0:c0 + 512]),
            waits=(e_q0,), inc=True)
        q0_used[par] = e_t1
        e_t2 = v_op(lambda e, c0=c0: e.tensor_mul(scrB[:, :], p_rot[:, :],
                                                  sin_t[:, c0:c0 + 512]),
                    waits=(e_rot,), inc=True)
        war["rotv"] = e_t2
        e_qf = v_op(lambda e, par=par: e.tensor_add(qf_bf[par][:, :], scrA[:, :],
                                                    scrB[:, :]),
                    waits=(e_t1, e_t2) + tuple(qf_used[par] or ()), inc=True)
        e_sq = s_op(lambda e, par=par: e.activation(qsq_bf[par][:, :], qf_bf[par][:, :],
                                                    AF.Square),
                    waits=(e_qf, qsq_used[par]), inc=True)
        e_qn = t_op(lambda e, par=par: e.matmul(p_qn[:, :], ones_bf[:, 0:1],
                                                qsq_bf[par][:, :], start=True, stop=True),
                    waits=(e_sq, war.get("qn")), inc=True)
        qsq_used[par] = e_qn
        col = head * 2 + half
        dstn = kn2 if is_k else qm2
        e_red = v_op(lambda e, col=col, dstn=dstn: e.reduce_max(
            dstn[0:1, col:col + 1], p_qn[:, :], axis=AX.X),
            waits=(e_qn,), inc=True)
        war["qn"] = e_red
        if is_k:
            e_w = dy(ssp[par], kvin_d[head * 128:(head + 1) * 128, c0:c0 + 512],
                     qf_bf[par][:, :], waits=(e_qf,))
        else:
            e_w = dy(ssp[par], qspill_d[head * 128:(head + 1) * 128, c0:c0 + 512],
                     qf_bf[par][:, :], waits=(e_qf,))
        qf_used[par] = [e_sq, e_w]
        war["rot"] = e_q0
        return e_w, e_sq

    p1b = ExitStack()
    wkcol = [sb(p1b, f"wkcol{i}", [128, NFC * 128], BF16) for i in range(2)]
    p_pj = [ps(p1b, f"p_pj{i}", [128, 512]) for i in range(2)]
    p_rot1 = ps(p1b, "p_rot1", [128, 512])
    p_qn1 = ps(p1b, "p_qn1", [1, 512])

    war1 = {}
    ev_wk_free = [None, None]
    ev_pj_free = [None, None]
    wk_r = wk_d.rearrange("(fc p) n -> p fc n", p=128)
    for kvh in range(KVH):
        sl = kvh % 2
        e_wk = dg_split(swk2[sl], wkcol[sl][:, :].rearrange("p (c m) -> p c m", c=NFC),
                  wk_r[:, :, kvh * 128:(kvh + 1) * 128], waits=(ev_wk_free[sl],))
        last_mm = None
        for half in range(2):
            par = half
            for fc in range(NFC):
                t_op(lambda e, fc=fc, par=par, sl=sl, half=half: e.matmul(
                        p_pj[par][:, :], wkcol[sl][:, fc * 128:(fc + 1) * 128],
                        ln_fm[:, fc * OWN + half * 512: fc * OWN + half * 512 + 512],
                        start=(fc == 0), stop=(fc == NFC - 1)),
                     waits=(e_wk, ev_lnfm, ev_pj_free[par]), inc=(fc == NFC - 1))
            e_mm = now(st_)
            last_mm = e_mm
            e_w, e_q0c = rope_block(p_pj[par], e_mm, kvh, half, True,
                                    p_rot1, p_qn1, war1)
            ev_pj_free[par] = war1["rot"]
        ev_wk_free[sl] = last_mm
    e_knx = v_op(lambda e: e.reduce_max(knx[0:1, :], kn2[0:1, :].rearrange(
        "p (h a) -> p h a", a=2), axis=AX.X), inc=True)
    dg(skv, kvin_d[2048:2049, 0:8], knx[0:1, :].rearrange("p (a c) -> p a c", a=1),
       waits=(e_knx,))
    e_p1b_s = now(ss_)
    e_p1b_v = now(sv_)
    p1b.close()

    p1c = ExitStack()
    wvh = sb(p1c, "wvh", [128, NFC * 512], BF16)
    v_bf = [sb(p1c, f"v_bf{i}", [128, 512], BF16) for i in range(2)]
    p_pv = [ps(p1c, f"p_pv{i}", [128, 512]) for i in range(2)]
    wv_r = wv_d.rearrange("(fc p) n -> p fc n", p=128)
    ev_wv_free = None
    ev_pv_free = [e_p1b_s, e_p1b_v]
    ev_vbf_free = [None, None]
    for vhalf in range(2):
        e_wv = dg_split(swv, wvh[:, :].rearrange("p (c m) -> p c m", c=NFC),
                  wv_r[:, :, vhalf * 512:(vhalf + 1) * 512],
                  waits=(ev_wv_free, e_p1b_s, e_p1b_v))
        last = None
        for j in range(NT):
            par = j % 2
            for fc in range(NFC):
                t_op(lambda e, fc=fc, par=par, j=j: e.matmul(
                        p_pv[par][:, :],
                        ln_fm[:, fc * OWN + j * 128: fc * OWN + j * 128 + 128],
                        wvh[:, fc * 512:(fc + 1) * 512],
                        start=(fc == 0), stop=(fc == NFC - 1)),
                     waits=(e_wv, ev_lnfm, ev_pv_free[par]), inc=(fc == NFC - 1))
            e_mm = now(st_)
            last = e_mm
            e_ev = s_op(lambda e, par=par: e.activation(v_bf[par][:, :], p_pv[par][:, :],
                                                        AF.Copy),
                        waits=(e_mm, ev_vbf_free[par]), inc=True)
            ev_pv_free[par] = e_ev
            e_vw = dy(svb[par], kvin_d[1024 + j * 128:1024 + (j + 1) * 128,
                                  vhalf * 512:(vhalf + 1) * 512],
                      v_bf[par][:, :], waits=(e_ev,))
            ev_vbf_free[par] = e_vw
        ev_wv_free = last
    e_p1c_s = now(ss_)
    p1c.close()

    kv_done_waits = (now(skv), now(ssp[0]), now(ssp[1]), now(svb[0]), now(svb[1]))

    p1d = ExitStack()
    wqcol = [sb(p1d, f"wqcol{i}", [128, NFC * 128], BF16) for i in range(4)]
    p_pj2 = [ps(p1d, f"p_pj2{i}", [128, 512]) for i in range(2)]
    p_rot2 = ps(p1d, "p_rot2", [128, 512])
    p_qn2 = ps(p1d, "p_qn2", [1, 512])
    wq_r = wq_d.rearrange("(fc p) n -> p fc n", p=128)
    war2 = {}
    ev_wq_free = [None, None, None, None]
    ev_pj2_free = [e_p1c_s, e_p1c_s]
    for h in range(H):
        sl = h % 4
        e_wq = dg_split(swq4[sl], wqcol[sl][:, :].rearrange("p (c m) -> p c m", c=NFC),
                  wq_r[:, :, h * 128:(h + 1) * 128], waits=(ev_wq_free[sl], e_p1c_s))
        last_mm = None
        for half in range(2):
            par = half
            for fc in range(NFC):
                t_op(lambda e, fc=fc, par=par, sl=sl, half=half: e.matmul(
                        p_pj2[par][:, :], wqcol[sl][:, fc * 128:(fc + 1) * 128],
                        ln_fm[:, fc * OWN + half * 512: fc * OWN + half * 512 + 512],
                        start=(fc == 0), stop=(fc == NFC - 1)),
                     waits=(e_wq, ev_lnfm, ev_pj2_free[par]), inc=(fc == NFC - 1))
            e_mm = now(st_)
            last_mm = e_mm
            e_w, e_q0c = rope_block(p_pj2[par], e_mm, h, half, False,
                                    p_rot2, p_qn2, war2)
            ev_pj2_free[par] = war2["rot"]
        ev_wq_free[sl] = last_mm
    e_qmx = v_op(lambda e: e.reduce_max(qmx[0:1, :], qm2[0:1, :].rearrange(
        "p (h a) -> p h a", a=2), axis=AX.X), inc=True)
    e_p1_t = now(st_)
    e_p1_s = now(ss_)
    e_p1_v = now(sv_)
    p1d.close()
    p1.close()

    if stop_after == "p1":
        dq = nc.dram_tensor("dbg_qspill", [D, OWN], BF16, kind="ExternalOutput").ap()
        dk = nc.dram_tensor("dbg_kvin", [KVROWS, OWN], BF16, kind="ExternalOutput").ap()
        dy(sout, dq, qspill_d,
           waits=(now(ssp[0]), now(ssp[1]), now(svb[0]), now(svb[1]), now(skv),
                  e_p1_t, e_p1_s, e_p1_v))
        dy(sout, dk, kvin_d)
        P.op("y", lambda e: e.nop(), waits=(now(sout),))
        with nc.Block() as block:
            P.emit(nc, block)
        ctx.close()
        return nc

    P.op("g", lambda e: e.collective_compute(
        "AllGather", ALU.bypass,
        replica_groups=[[0, 1, 2, 3], [4, 5, 6, 7]],
        ins=[kvin_d.opt()], outs=[kvg_d.opt()]),
        waits=kv_done_waits, inc=cc_)
    e_ag = now(cc_)

    kvg_rr = kvg_d.rearrange("(rr rows) c -> rows rr c", rows=KVROWS)
    e_km = dg(skm, km4[0:1, :].rearrange("p (rr c) -> p rr c", rr=4),
              kvg_rr[2048:2049, :, 0:8], waits=(e_ag,))
    e_k1 = v_op(lambda e: e.tensor_max(km4[0:1, 0:8], km4[0:1, 8:16], km4[0:1, 0:8]),
                waits=(e_km,), inc=True)
    e_k2 = v_op(lambda e: e.tensor_max(km4[0:1, 16:24], km4[0:1, 24:32],
                                       km4[0:1, 16:24]), waits=(e_km,), inc=True)
    e_k3 = v_op(lambda e: e.tensor_max(km4[0:1, 0:8], km4[0:1, 0:8], km4[0:1, 16:24]),
                waits=(e_k1, e_k2), inc=True)
    e_ks = s_op(lambda e: e.activation(km4[0:1, 0:8], km4[0:1, 0:8], AF.Sqrt),
                waits=(e_k3,), inc=True)
    e_qs = s_op(lambda e: e.activation(qmx[0:1, :], qmx[0:1, :], AF.Sqrt),
                waits=(e_qmx,), inc=True)
    mrow3 = mrow[0:1, :].rearrange("p (h a) -> p h a", a=4)
    km3 = km4[0:1, 0:8].rearrange("p (h a) -> p h a", a=1)
    for sub in range(4):
        v_op(lambda e, sub=sub: e.tensor_copy(mrow3[:, :, sub:sub + 1], km3),
             waits=(e_ks,))
    e_mr = v_op(lambda e: e.scalar_tensor_tensor(mrow[0:1, :], qmx[0:1, :], NBOUND,
                                                 mrow[0:1, :], ALU.mult, ALU.mult),
                waits=(e_qs,), inc=True)
    e_mrb = s_op(lambda e: e.activation(mrow_bf[0:1, :], mrow[0:1, :], AF.Copy),
                 waits=(e_mr,), inc=True)
    with nc.psum_tensor("p_mhc", [128, 32], F32) as p_mhc:
        t_op(lambda e: e.matmul(p_mhc[:, :], ones_bf[0:1, :], mrow_bf[0:1, :],
                                start=True, stop=True),
             waits=(e_mrb, e_p1_s, e_p1_v), inc=True)
        e_mh1 = now(st_)
        e_mhc = s_op(lambda e: e.activation(MHC[:, :], p_mhc[:, :], AF.Copy),
                     waits=(e_mh1,), inc=True)

    sAO = ExitStack()
    AO_fm = sb(sAO, "AO_fm", [128, H * OWN], BF16)
    p3 = ExitStack()
    K_h = [sb(p3, f"K_h{i}", [128, 4 * OWN], BF16) for i in range(2)]
    V_h = [sb(p3, f"V_h{i}", [128, 32 * VST], BF16) for i in range(2)]
    Q_h = [sb(p3, f"Q_h{i}", [128, OWN], BF16) for i in range(2)]
    PT = sb(p3, "PT", [128, NT * 512], BF16)
    AOq = [sb(p3, f"AOq{i}", [128, 128], BF16) for i in range(2)]
    rl_sb = [sb(p3, f"rl_sb{i}", [128, 1], F32) for i in range(2)]
    p_S = [ps(p3, f"p_S{i}", [128, 512]) for i in range(3)]
    p_AV = [ps(p3, f"p_AV{i}", [128, 129]) for i in range(2)]
    p_AOT = [ps(p3, f"p_AOT{i}", [128, 128], BF16) for i in range(2)]

    ev_vones = []
    for i in range(2):
        vh3 = V_h[i][:, :].rearrange("p (m c) -> p m c", c=VST)
        ev_vones.append(v_op(lambda e, vh3=vh3: e.memset(vh3[:, :, 128:129], 1.0),
                             waits=(e_p1_t, e_p1_s, now(swq4[0]), now(swq4[1]), now(swq4[2]), now(swq4[3]),
                                    now(sx2[0]), now(sx2[1])), inc=True))

    ev_kh_free = [None, None]
    ev_vh_free = [None, None]
    ev_qh_free = [None, None]
    exp_ev = {0: None, 1: None, 2: None}
    av_war = {0: None, 1: None}
    aot_war = {0: None, 1: None}
    aoq_war = {0: None, 1: None}
    rl_war = {0: None, 1: None}
    for kvh in range(KVH):
        ks = kvh % 2
        e_kh = e_vh = None
        for rr in range(4):
            e_kh = dy(skh2[ks], K_h[ks][:, rr * OWN:(rr + 1) * OWN],
                      kvg_d[rr * KVROWS + kvh * 128: rr * KVROWS + (kvh + 1) * 128, :],
                      waits=(e_ag, ev_kh_free[ks], e_p1_t, e_p1_s, e_p1_v))
            vdst = V_h[ks][:, :].rearrange("p (m c) -> p m c", c=VST)
            vsrc = kvg_d[rr * KVROWS + 1024: rr * KVROWS + 2048,
                         kvh * 128:(kvh + 1) * 128].rearrange(
                             "(jj s) e -> s jj e", s=128)
            e_vh = dy(svh2[ks], vdst[:, rr * 8:(rr + 1) * 8, 0:128], vsrc,
                      waits=(e_ag, ev_vh_free[ks], ev_vones[ks]))
        for qh4 in range(4):
            h = kvh * 4 + qh4
            qs = h % 2
            e_qh = dy(sqr2[qs], Q_h[qs][:, :], qspill_d[h * 128:(h + 1) * 128, :],
                      waits=(now(ssp[0]), now(ssp[1]), ev_qh_free[qs]))
            last_t_of_head = None
            for j in range(NT):
                avp = j % 2
                ready = {}

                def emit_S(i, j=j, h=h, ks=ks, qs=qs):
                    spar = i % 3
                    for rr in range(4):
                        t_op(lambda e, i=i, rr=rr, spar=spar: e.matmul(
                                p_S[spar][:, rr * 128:(rr + 1) * 128],
                                K_h[ks][:, rr * OWN + i * 128: rr * OWN + i * 128 + 128],
                                Q_h[qs][:, j * 128:(j + 1) * 128],
                                start=True, stop=True),
                             waits=(e_kh, e_qh, exp_ev[spar], e_mhc),
                             inc=(rr == 3))
                    e_sc = now(st_)
                    e_exp = s_op(lambda e, i=i, spar=spar: e.activation(
                                    PT[:, i * 512:(i + 1) * 512], p_S[spar][:, :],
                                    AF.Exp, bias=MHC[:, h:h + 1], scale=ISQHD),
                                 waits=(e_sc, e_mhc), inc=True)
                    exp_ev[spar] = e_exp
                    if i == j:
                        ready[i] = v_op(lambda e, i=i: e.tensor_mul(
                                        PT[:, i * 512:(i + 1) * 512],
                                        PT[:, i * 512:(i + 1) * 512], dmask[:, :]),
                                        waits=(e_exp,), inc=True)
                    else:
                        ready[i] = e_exp

                def emit_AV(i, j=j, ks=ks, avp=avp):
                    for rr in range(4):
                        t_op(lambda e, i=i, rr=rr: e.matmul(
                                p_AV[avp][:, :],
                                PT[:, i * 512 + rr * 128: i * 512 + rr * 128 + 128],
                                V_h[ks][:, (rr * 8 + i) * VST: (rr * 8 + i) * VST + 129],
                                start=(i == 0 and rr == 0), stop=(i == j and rr == 3)),
                             waits=(ready[i], e_vh, av_war[avp]),
                             inc=(i == j and rr == 3))

                LOOKAHEAD = 2
                for i in range(min(LOOKAHEAD, j + 1)):
                    emit_S(i)
                for i in range(j + 1):
                    if i + LOOKAHEAD <= j:
                        emit_S(i + LOOKAHEAD)
                    emit_AV(i)
                e_avd = now(st_)
                last_t_of_head = e_avd
                e_rl = v_op(lambda e, avp=avp: e.reciprocal(rl_sb[avp][:, :],
                                                            p_AV[avp][:, 128:129]),
                            waits=(e_avd, rl_war[avp]), inc=True)
                e_ao = s_op(lambda e, avp=avp: e.activation(
                                AOq[avp][:, :], p_AV[avp][:, 0:128], AF.Copy,
                                scale=rl_sb[avp][:, :]),
                            waits=(e_rl, aoq_war[avp]), inc=True)
                av_war[avp] = e_ao
                rl_war[avp] = e_ao
                e_tr = t_op(lambda e, avp=avp: e.transpose(p_AOT[avp][:, :],
                                                           AOq[avp][:, :], id_bf[:, :]),
                            waits=(e_ao, aot_war[avp]), inc=True)
                aoq_war[avp] = now(st_)
                e_fm = v_op(lambda e, avp=avp, h=h, j=j: e.tensor_copy(
                                AO_fm[:, h * OWN + j * 128: h * OWN + j * 128 + 128],
                                p_AOT[avp][:, :]),
                            waits=(now(st_),), inc=True)
                aot_war[avp] = e_fm
            ev_qh_free[qs] = last_t_of_head
        ev_kh_free[ks] = last_t_of_head
        ev_vh_free[ks] = last_t_of_head
    e_attn_done = now(sv_)
    e_p3_t = now(st_)
    e_p3_s = now(ss_)
    p3.close()

    p4 = ExitStack()
    wosl = [sb(p4, "wosl0", [128, NFC * 512], BF16, side="right"),
            sb(p4, "wosl1", [128, NFC * 512], BF16)]
    aost = [sb(p4, f"aost{i}", [128, 512], F32) for i in range(2)]
    sink4 = sb(p4, "sink4", [128, 512], F32)
    p_o2 = [ps(p4, f"p_o2{i}", [128, 512]) for i in range(2)]
    wo_r = wo_d.rearrange("(fc p) n -> p fc n", p=128)
    ev_wo_free = [None, None]
    ev_o2_free = [None, None]
    ev_aost_free = [None, None]
    acc_war = [None, None]
    for oc in range(8):
        ws = oc % 2
        w_extra = (e_p1_t, e_p1_s, e_p1_v) if ws == 0 else (e_p3_t, e_p3_s,
                                                            e_attn_done)
        e_wo = dg_split(swo2[ws], wosl[ws][:, :].rearrange("p (c m) -> p c m", c=NFC),
                  wo_r[:, :, oc * 512:(oc + 1) * 512],
                  waits=(ev_wo_free[ws],) + w_extra)
        last = None
        for j in range(NT):
            par = j % 2
            stt_t = stats[par]
            for fc in range(NFC):
                t_op(lambda e, fc=fc, par=par, j=j, ws=ws: e.matmul(
                        p_o2[par][:, :],
                        AO_fm[:, fc * OWN + j * 128: fc * OWN + j * 128 + 128],
                        wosl[ws][:, fc * 512:(fc + 1) * 512],
                        start=(fc == 0), stop=(fc == NFC - 1)),
                     waits=(e_wo, e_attn_done, ev_o2_free[par]), inc=(fc == NFC - 1))
            e_mm = now(st_)
            last = e_mm
            e_ev = s_op(lambda e, par=par, stt_t=stt_t: e.activation(
                            aost[par][:, :], p_o2[par][:, :], AF.Copy,
                            accum_out=stt_t[:, 9:10]),
                        waits=(e_mm, ev_aost_free[par], acc_war[par]), inc=True)
            ev_o2_free[par] = e_ev
            if oc == 0:
                v_op(lambda e, j=j, stt_t=stt_t: e.tensor_copy(
                    sums_ao[:, j:j + 1], stt_t[:, 9:10]), waits=(e_ev,))
            else:
                v_op(lambda e, j=j, stt_t=stt_t: e.tensor_add(
                    sums_ao[:, j:j + 1], sums_ao[:, j:j + 1], stt_t[:, 9:10]),
                    waits=(e_ev,))
            e_sq = s_op(lambda e, par=par, stt_t=stt_t: e.activation(
                            sink4[:, :], aost[par][:, :], AF.Square,
                            accum_out=stt_t[:, 10:11]), inc=True)
            if oc == 0:
                e_ac = v_op(lambda e, j=j, stt_t=stt_t: e.tensor_copy(
                    sumsq_ao[:, j:j + 1], stt_t[:, 10:11]), waits=(e_sq,), inc=True)
            else:
                e_ac = v_op(lambda e, j=j, stt_t=stt_t: e.tensor_add(
                    sumsq_ao[:, j:j + 1], sumsq_ao[:, j:j + 1], stt_t[:, 10:11]),
                    waits=(e_sq,), inc=True)
            acc_war[par] = e_ac
            e_w = dy(sat[par], ao_d[j * 128:(j + 1) * 128, oc * 512:(oc + 1) * 512],
                     aost[par][:, :], waits=(e_sq,))
            ev_aost_free[par] = e_w
        ev_wo_free[ws] = last
    e_p4_v = now(sv_)
    e_p4_t = now(st_)
    e_p4_s = now(ss_)
    p4.close()
    sAO.close()

    sHF = ExitStack()
    h_fm = sb(sHF, "h_fm", [128, NFC * OWN], BF16, side="right")
    p5 = ExitStack()
    xp_sb = sb(p5, "xp_sb", [128, D], F32)
    x2_sb = [sb(p5, f"x2_sb{i}", [128, D], F32) for i in range(2)]
    ao_rd = sb(p5, "ao_rd", [128, D], F32)
    adj_sb = sb(p5, "adj_sb", [128, D], F32)
    ln2_tm = sb(p5, "ln2_tm", [128, D], BF16)
    p_t5 = [ps(p5, f"p_t5{i}", [128, 512], BF16) for i in range(2)]

    ev_x2_free = [None, None]
    ev_ao_free = None
    ev_tr5 = {0: None, 1: None}
    ev_hfm = None
    ev_xp_w = None
    last_stats5 = [None, None]
    last_tr5 = None
    prev_xp = None
    prev_ln2 = None
    for j in range(NT):
        sl = j % 2
        stt_t = stats[sl]
        e_x = dy(sx2[sl], x2_sb[sl][:, :], x_d[j * 128:(j + 1) * 128, :],
                 waits=(ev_x2_free[sl], e_p4_t, e_p4_s, e_p4_v))
        e_a = dy(sao, ao_rd[:, :], ao_d[j * 128:(j + 1) * 128, :],
                 waits=(now(sat[0]), now(sat[1]), ev_ao_free, e_p4_t, e_p4_s,
                        e_p4_v))
        v_op(lambda e, j=j, stt_t=stt_t: e.tensor_scalar(
            stt_t[:, 2:3], sums_ao[:, j:j + 1], 1.0 / D, None, ALU.mult),
            waits=(e_p4_v, last_stats5[sl]))
        v_op(lambda e, j=j, stt_t=stt_t: e.tensor_scalar(
            stt_t[:, 3:4], sumsq_ao[:, j:j + 1], 1.0 / D, None, ALU.mult))
        v_op(lambda e, stt_t=stt_t: e.tensor_tensor(stt_t[:, 4:5], stt_t[:, 2:3],
                                                    stt_t[:, 2:3], ALU.mult))
        e_var = v_op(lambda e, stt_t=stt_t: e.tensor_sub(stt_t[:, 5:6], stt_t[:, 3:4],
                                                         stt_t[:, 4:5]), inc=True)
        s_op(lambda e, stt_t=stt_t: e.activation(stt_t[:, 6:7], stt_t[:, 5:6], AF.Ln,
                                                 bias=EPS),
             waits=(e_var, last_stats5[sl]))
        e_rstd = s_op(lambda e, stt_t=stt_t: e.activation(stt_t[:, 7:8], stt_t[:, 6:7],
                                                          AF.Exp, scale=-0.5), inc=True)
        e_nmr = v_op(lambda e, stt_t=stt_t: e.tensor_scalar(
            stt_t[:, 8:9], stt_t[:, 2:3], stt_t[:, 7:8], -1.0, ALU.mult, ALU.mult),
            waits=(e_rstd,), inc=True)
        v_op(lambda e, j=j, stt_t=stt_t: e.tensor_copy(
            p5dbg[:, j * 8:j * 8 + 7], stt_t[:, 2:9]), waits=(e_nmr,))
        e_adj = s_op(lambda e, sl=sl, stt_t=stt_t: e.activation(
            adj_sb[:, :], x2_sb[sl][:, :], AF.Identity, bias=stt_t[:, 8:9]),
            waits=(e_x, e_nmr, prev_xp), inc=True)
        e_xp = v_op(lambda e, stt_t=stt_t: e.scalar_tensor_tensor(
            xp_sb[:, :], ao_rd[:, :], stt_t[:, 7:8], adj_sb[:, :],
            ALU.mult, ALU.add, accum_out=stt_t[:, 11:12]),
            waits=(e_a, e_adj, prev_ln2, ev_xp_w), inc=True)
        prev_xp = e_xp
        ev_x2_free[sl] = e_xp
        ev_ao_free = e_xp
        ev_xp_w = dy(sxp, xp_d[j * 128:(j + 1) * 128, :], xp_sb[:, :], waits=(e_xp,))
        e_sq2 = s_op(lambda e, stt_t=stt_t: e.activation(
            ln2_tm[:, :], xp_sb[:, :], AF.Square, accum_out=stt_t[:, 12:13]),
            waits=(e_xp, last_tr5), inc=True)
        v_op(lambda e, stt_t=stt_t: e.tensor_scalar(stt_t[:, 2:3], stt_t[:, 11:12],
                                                    1.0 / D, None, ALU.mult),
             waits=(e_xp,))
        v_op(lambda e, stt_t=stt_t: e.tensor_scalar(stt_t[:, 3:4], stt_t[:, 12:13],
                                                    1.0 / D, None, ALU.mult),
             waits=(e_sq2,))
        v_op(lambda e, stt_t=stt_t: e.tensor_tensor(stt_t[:, 4:5], stt_t[:, 2:3],
                                                    stt_t[:, 2:3], ALU.mult))
        e_var2 = v_op(lambda e, stt_t=stt_t: e.tensor_sub(stt_t[:, 5:6], stt_t[:, 3:4],
                                                          stt_t[:, 4:5]), inc=True)
        s_op(lambda e, stt_t=stt_t: e.activation(stt_t[:, 6:7], stt_t[:, 5:6], AF.Ln,
                                                 bias=EPS), waits=(e_var2,))
        e_rstd2 = s_op(lambda e, j=j, stt_t=stt_t: e.activation(
            rstd2_k[:, j:j + 1], stt_t[:, 6:7], AF.Exp, scale=-0.5), inc=True)
        e_nmr2 = v_op(lambda e, j=j, stt_t=stt_t: e.tensor_scalar(
            nmr2_k[:, j:j + 1], stt_t[:, 2:3], rstd2_k[:, j:j + 1], -1.0,
            ALU.mult, ALU.mult), waits=(e_rstd2,), inc=True)
        e_ln2 = s_op(lambda e, j=j: e.activation(ln2_tm[:, :], xp_sb[:, :], AF.Identity,
                                                 bias=nmr2_k[:, j:j + 1],
                                                 scale=rstd2_k[:, j:j + 1]),
                     waits=(e_nmr2,), inc=True)
        prev_ln2 = e_ln2
        v_op(lambda e, stt_t=stt_t: e.tensor_scalar(stt_t[:, 13:14], stt_t[:, 5:6],
                                                    EPS, None, ALU.add),
             waits=(e_var2,))
        e_rec = v_op(lambda e, stt_t=stt_t: e.reciprocal(stt_t[:, 14:15],
                                                         stt_t[:, 13:14]), inc=True)
        e_vh2 = v_op(lambda e, stt_t=stt_t: e.tensor_tensor(
            stt_t[:, 15:16], stt_t[:, 5:6], stt_t[:, 14:15], ALU.mult),
            waits=(e_rec,), inc=True)
        e_lv = s_op(lambda e, j=j, stt_t=stt_t: e.activation(
            lv_all[:, j:j + 1], stt_t[:, 15:16], AF.Ln, bias=1.0),
            waits=(e_vh2,), inc=True)
        last_stats5[sl] = e_lv
        hfm3 = h_fm[:, :].rearrange("p (fc t) -> p fc t", fc=NFC)
        for g in range(8):
            par = g % 2
            for k in range(4):
                fc = g * 4 + k
                t_op(lambda e, fc=fc, par=par, k=k: e.transpose(
                        p_t5[par][:, k * 128:(k + 1) * 128],
                        ln2_tm[:, fc * 128:(fc + 1) * 128], id_bf[:, :]),
                     waits=(e_ln2, ev_tr5[par]), inc=(k == 3))
            e_trg = now(st_)
            e_ev = v_op(lambda e, g=g, j=j, par=par, hfm3=hfm3: e.tensor_copy(
                            hfm3[:, g * 4:(g + 1) * 4, j * 128:(j + 1) * 128],
                            p_t5[par][:, :].rearrange("p (k t) -> p k t", k=4)),
                        waits=(e_trg,), inc=True)
            ev_tr5[par] = e_ev
            ev_hfm = e_ev
        last_tr5 = now(st_)
    e_lv_done = now(ss_)

    with nc.psum_tensor("p_lvT", [1, 128], F32) as p_lvT:
        e_lvc = None
        for j in range(NT):
            e_t = t_op(lambda e, j=j: e.transpose(p_lvT[:, :], lv_all[:, j:j + 1],
                                                  id_f32[:, :]),
                       waits=(e_lv_done, e_lvc), inc=True)
            e_lvc = v_op(lambda e, j=j: e.tensor_copy(
                lvrow[0:1, j * 128:(j + 1) * 128], p_lvT[0:1, :]),
                waits=(now(st_),), inc=True)
    v_op(lambda e: e.memset(arsb[0:1, :], 0.0), waits=(e_lvc,))
    e_sumlv = v_op(lambda e: e.reduce_sum(arsb[0:1, 0:1], lvrow[0:1, :], axis=AX.X),
                   inc=True)
    e_arw = dy(sar, arin_d, arsb[0:1, :], waits=(e_sumlv,))
    e_p5_t = now(st_)
    e_p5_s = now(ss_)
    e_p5_v = now(sv_)
    p5.close()

    a1 = ExitStack()
    wu1c = [sb(a1, f"wu1c{i}", [128, NFC * 128], BF16) for i in range(2)]
    gg = sb(a1, "gg", [128, 8 * OWN], BF16)
    gb_s = sb(a1, "gb_s", [128, 512], F32)
    sqg = sb(a1, "sqg", [128, 512], F32)
    x3g = sb(a1, "x3g", [128, 512], F32)
    thg = sb(a1, "thg", [128, 512], F32)
    p_g = [ps(a1, f"p_g{i}", [128, 512]) for i in range(2)]
    p_le = [ps(a1, f"p_le{i}", [1, 512]) for i in range(2)]
    wu1_r = Wu1T_d.rearrange("(fc p) n -> p fc n", p=128)
    ev_wu_free = [None, None]
    ev_g_free = [None, None]
    prev_ggv = None
    prev_x3 = None
    for uc in range(8):
        ws = uc % 2
        e_wu = dg_split(swu2[ws], wu1c[ws][:, :].rearrange("p (c m) -> p c m", c=NFC),
                  wu1_r[:, :, uc * 128:(uc + 1) * 128],
                  waits=(ev_wu_free[ws], e_p5_t, e_p5_s, e_p5_v, now(sxp)))
        last = None
        for half in range(2):
            par = half
            for fc in range(NFC):
                t_op(lambda e, fc=fc, par=par, ws=ws, half=half: e.matmul(
                        p_g[par][:, :], wu1c[ws][:, fc * 128:(fc + 1) * 128],
                        h_fm[:, fc * OWN + half * 512: fc * OWN + half * 512 + 512],
                        start=(fc == 0), stop=(fc == NFC - 1)),
                     waits=(e_wu, ev_hfm, ev_g_free[par], e_p5_v),
                     inc=(fc == NFC - 1))
            e_mm = now(st_)
            last = e_mm
            e_gb = s_op(lambda e, par=par, uc=uc: e.activation(
                gb_s[:, :], p_g[par][:, :], AF.Identity, bias=bu1c[:, uc:uc + 1]),
                waits=(e_mm, e_cst, prev_ggv), inc=True)
            ev_g_free[par] = e_gb
            e_sqg = s_op(lambda e: e.activation(sqg[:, :], gb_s[:, :], AF.Square),
                         waits=(prev_x3,), inc=True)
            e_x3 = v_op(lambda e: e.tensor_mul(x3g[:, :], sqg[:, :], gb_s[:, :]),
                        waits=(e_sqg,), inc=True)
            prev_x3 = e_x3
            e_ti = v_op(lambda e: e.scalar_tensor_tensor(thg[:, :], x3g[:, :], GC2,
                                                         gb_s[:, :], ALU.mult, ALU.add),
                        waits=(e_x3,), inc=True)
            e_th = s_op(lambda e: e.activation(thg[:, :], thg[:, :], AF.Tanh, scale=GC1),
                        waits=(e_ti,), inc=True)
            e_t5v = v_op(lambda e: e.tensor_scalar(thg[:, :], thg[:, :], 0.5, 0.5,
                                                   ALU.mult, ALU.add),
                         waits=(e_th,), inc=True)
            e_ggv = v_op(lambda e, uc=uc, half=half: e.tensor_mul(
                            gg[:, uc * OWN + half * 512: uc * OWN + half * 512 + 512],
                            thg[:, :], gb_s[:, :]),
                         waits=(e_t5v,), inc=True)
            prev_ggv = e_ggv
            t_op(lambda e, uc=uc, half=half: e.matmul(
                    p_le[half][:, :], wu2c[:, uc:uc + 1],
                    gg[:, uc * OWN + half * 512: uc * OWN + half * 512 + 512],
                    start=(uc == 0), stop=(uc == 7)),
                 waits=(e_ggv, e_wu2), inc=(uc == 7))
        ev_wu_free[ws] = last
    e_led = now(st_)
    for half in range(2):
        s_op(lambda e, half=half: e.activation(
            sigrow[0:1, half * 512:(half + 1) * 512], p_le[half][0:1, :],
            AF.Sigmoid, bias=bu2_sb[0:1, :]),
            waits=(e_led, e_cst), inc=(half == 1))
    e_sig = now(ss_)
    e_a1_t = now(st_)
    e_a1_v = now(sv_)
    a1.close()

    sA2 = ExitStack()
    wT = sb(sA2, "wT", [128, 8 * OWN], BF16)
    w_bf = sb(sA2, "w_bf", [128, 1024], BF16)
    srf = sb(sA2, "srf", [128, 1024], F32)
    mask_u8 = sb(sA2, "mask_u8", [128, 1024], mybir.dt.uint8)
    alm = sb(sA2, "alm", [128, 1024], F32)
    top8 = sb(sA2, "top8", [128, 8], F32)
    a2p = ExitStack()
    p_qy = ps(a2p, "p_qy", [64, 512])
    p_rq = ps(a2p, "p_rq", [48, 512])
    p_sr = ps(a2p, "p_sr", [128, 1024])
    p_al = ps(a2p, "p_al", [128, 1024])
    p_wt = [ps(a2p, f"p_wt{i}", [128, 512], BF16) for i in range(2)]

    e_qy_ev = None
    e_rq_ev = None
    ev_sr_free = None
    ev_al_free = None
    ev_wt_war = {0: None, 1: None}
    prev_wx = None
    prev_sel = None
    last_wtr = None
    for half in range(2):
        for fc in range(NFC):
            t_op(lambda e, fc=fc, half=half: e.matmul(
                    p_qy[:, :], WqT[:, fc * 64:(fc + 1) * 64],
                    h_fm[:, fc * OWN + half * 512: fc * OWN + half * 512 + 512],
                    start=(fc == 0), stop=(fc == NFC - 1)),
                 waits=(e_cst, ev_hfm, e_qy_ev, e_sig, e_a1_v), inc=(fc == NFC - 1))
        e_qmm = now(st_)
        e_qy_ev = s_op(lambda e, half=half: e.activation(
            q_aug[0:64, half * 512:(half + 1) * 512], p_qy[:, :], AF.Copy),
            waits=(e_qmm, e_ms1), inc=True)
        t_op(lambda e, half=half: e.matmul(p_rq[:, :], WrT[:, :],
                                           q_aug[0:64, half * 512:(half + 1) * 512],
                                           start=True, stop=True),
             waits=(e_qy_ev, e_rq_ev), inc=True)
        e_rqm = now(st_)
        e_rq_ev = s_op(lambda e, half=half: e.activation(
            rq_aug[0:48, half * 512:(half + 1) * 512], p_rq[:, :], AF.Copy),
            waits=(e_rqm,), inc=True)
        for jj in range(4):
            j = half * 4 + jj
            for hh in range(2):
                t_op(lambda e, jj=jj, half=half, hh=hh: e.matmul(
                        p_sr[:, hh * 512:(hh + 1) * 512],
                        rq_aug[0:49, half * 512 + jj * 128: half * 512 + jj * 128 + 128],
                        rkT_aug[0:49, hh * 512:(hh + 1) * 512],
                        start=True, stop=True),
                     waits=(e_rq_ev, e_rk, ev_sr_free), inc=(hh == 1))
            e_srm = now(st_)
            e_srf = s_op(lambda e: e.activation(srf[:, :], p_sr[:, :], AF.Copy),
                         waits=(e_srm, prev_sel), inc=True)
            ev_sr_free = e_srf
            e_t8 = v_op(lambda e: e.max(out=top8[:, :], in_=srf[:, :]),
                        waits=(e_srf,), inc=True)
            e_msk = v_op(lambda e: e.tensor_scalar(mask_u8[:, :], srf[:, :],
                                                   top8[:, 7:8], None, ALU.is_ge),
                         waits=(e_t8,), inc=True)
            for hh in range(2):
                t_op(lambda e, jj=jj, half=half, hh=hh: e.matmul(
                        p_al[:, hh * 512:(hh + 1) * 512],
                        q_aug[0:65, half * 512 + jj * 128: half * 512 + jj * 128 + 128],
                        akT_aug[0:65, hh * 512:(hh + 1) * 512],
                        start=True, stop=True),
                     waits=(e_qy_ev, e_ak, ev_al_free), inc=(hh == 1))
            e_almm = now(st_)
            e_sel = v_op(lambda e: e.select(alm[:, :], mask_u8[:, :], p_al[:, :],
                                            negbig[:, :].to_broadcast([128, 1024])),
                         waits=(e_almm, e_msk, e_ms2, prev_wx), inc=True)
            prev_sel = e_sel
            ev_al_free = e_sel
            e_nm = v_op(lambda e: e.reduce_max(stats[0][:, 9:10], alm[:, :], axis=AX.X,
                                               negate=True),
                        waits=(e_sel,), inc=True)
            e_wx = s_op(lambda e, j=j: e.activation(w_bf[:, :], alm[:, :], AF.Exp,
                                                    bias=stats[0][:, 9:10],
                                                    accum_out=lffn_all[:, j:j + 1]),
                        waits=(e_nm, last_wtr), inc=True)
            prev_wx = e_wx
            wt3 = wT[:, :].rearrange("p (sc t) -> p sc t", sc=8)
            for g in range(2):
                par = g
                for k in range(4):
                    sc = g * 4 + k
                    t_op(lambda e, sc=sc, par=par, k=k: e.transpose(
                            p_wt[par][:, k * 128:(k + 1) * 128],
                            w_bf[:, sc * 128:(sc + 1) * 128], id_bf[:, :]),
                         waits=(e_wx, ev_wt_war[par]), inc=(k == 3))
                e_trg = now(st_)
                e_evw = v_op(lambda e, g=g, j=j, par=par, wt3=wt3: e.tensor_copy(
                                wt3[:, g * 4:(g + 1) * 4, j * 128:(j + 1) * 128],
                                p_wt[par][:, :].rearrange("p (k t) -> p k t", k=4)),
                             waits=(e_trg,), inc=True)
                ev_wt_war[par] = e_evw
            last_wtr = now(st_)
    e_wT_done = now(sv_)
    e_a2_s = now(ss_)
    e_a2_t = now(st_)
    a2p.close()
    sHF.close()

    sKeep = ExitStack()
    WvuT = sb(sKeep, "WvuT", [128, 2 * D], BF16, side="right")
    a2_all = sb(sKeep, "a2_all", [128, 2048], BF16, side="right")
    a3 = ExitStack()
    av_sb = sb(a3, "av_sb", [128, 8 * D], BF16)
    WvdT = sb(a3, "WvdT", [128, NFC * 256], BF16)
    aux1 = sb(a3, "aux1", [128, NFC * 512], BF16)
    a3p = ExitStack()
    p_x1 = [ps(a3p, f"p_x1{i}", [128, 512]) for i in range(2)]
    p_a2 = [ps(a3p, f"p_a2{i}", [128, 512]) for i in range(2)]

    av_r = av_d.rearrange("(sc p) n -> p sc n", p=128)
    for sc in range(8):
        dg(sav, av_sb[:, sc * D:(sc + 1) * D], av_r[:, sc, :],
           waits=(e_a2_t, e_a2_s, e_wT_done))
    e_wvd = dg_split(swvd, WvdT[:, :].rearrange("p (c m) -> p c m", c=NFC),
               WvdT_d.rearrange("(c p) n -> p c n", p=128))
    e_wvu = dg(swvu, WvuT[:, :].rearrange("p (c m) -> p c m", c=2),
               WvuT_d.rearrange("(c p) n -> p c n", p=128))

    ev_x1_free = [None, None]
    ev_a2_free = [None, None]
    for half in range(2):
        for hc in range(NFC):
            par = hc % 2
            for sc in range(8):
                t_op(lambda e, hc=hc, sc=sc, half=half, par=par: e.matmul(
                        p_x1[par][:, :],
                        av_sb[:, sc * D + hc * 128: sc * D + hc * 128 + 128],
                        wT[:, sc * OWN + half * 512: sc * OWN + half * 512 + 512],
                        start=(sc == 0), stop=(sc == 7)),
                     waits=(now(sav), e_wT_done, ev_x1_free[par]), inc=(sc == 7))
            e_mm = now(st_)
            e_ev = s_op(lambda e, hc=hc, par=par: e.activation(
                            aux1[:, hc * 512:(hc + 1) * 512], p_x1[par][:, :], AF.Copy),
                        waits=(e_mm,), inc=True)
            ev_x1_free[par] = e_ev
        e_aux1 = now(ss_)
        for vbc in range(2):
            par = vbc
            for hc in range(NFC):
                t_op(lambda e, vbc=vbc, hc=hc, par=par: e.matmul(
                        p_a2[par][:, :],
                        WvdT[:, hc * 256 + vbc * 128: hc * 256 + vbc * 128 + 128],
                        aux1[:, hc * 512:(hc + 1) * 512],
                        start=(hc == 0), stop=(hc == NFC - 1)),
                     waits=(e_wvd, e_aux1, ev_a2_free[par]), inc=(hc == NFC - 1))
            e_mm = now(st_)
            e_ev = s_op(lambda e, half=half, vbc=vbc, par=par: e.activation(
                            a2_all[:, half * 1024 + vbc * 512:
                                   half * 1024 + vbc * 512 + 512],
                            p_a2[par][:, :], AF.Copy),
                        waits=(e_mm,), inc=True)
            ev_a2_free[par] = e_ev
    e_a2_done = now(ss_)
    e_a3_t = now(st_)
    a3p.close()
    a3.close()
    sA2.close()

    P.op("g", lambda e: e.collective_compute(
        "AllReduce", ALU.add, replica_groups=[list(range(8))],
        ins=[arin_d.opt()], outs=[arout_d.opt()]),
        waits=(now(sar),), inc=cc_)
    e_arr = now(cc_)
    e_arl = dg(sar, arsb[0:1, :], arout_d, waits=(e_arr,))
    v_op(lambda e: e.tensor_scalar(growA[0:1, 0:1], arsb[0:1, 0:1], 0.008192, None,
                                   ALU.add), waits=(e_arl,))
    v_op(lambda e: e.reciprocal(growA[0:1, 1:2], growA[0:1, 0:1]))
    e_hi = v_op(lambda e: e.tensor_scalar(growA[0:1, 2:3], growA[0:1, 1:2], 4096.0,
                                          None, ALU.mult), inc=True)
    v_op(lambda e: e.tensor_scalar(growB[0:1, :], sigrow[0:1, :], 2.5, None, ALU.mult),
         waits=(e_sig,))
    v_op(lambda e: e.scalar_tensor_tensor(growA[0:1, :], lvrow[0:1, :], growA[0:1, 2:3],
                                          growB[0:1, :], ALU.mult, ALU.add),
         waits=(e_hi, e_sumlv))
    v_op(lambda e: e.tensor_scalar_max(growA[0:1, :], growA[0:1, :], 0.0))
    v_op(lambda e: e.tensor_scalar_min(growA[0:1, :], growA[0:1, :], 5.0))
    v_op(lambda e: e.tensor_scalar(growA[0:1, :], growA[0:1, :], 2.0 / 3.0, -1.0 / 3.0,
                                   ALU.mult, ALU.add))
    v_op(lambda e: e.tensor_scalar_max(growA[0:1, :], growA[0:1, :], 0.0))
    v_op(lambda e: e.tensor_scalar_min(growA[0:1, :], growA[0:1, :], 1.0))
    e_tg = v_op(lambda e: e.tensor_scalar(growA[0:1, :], growA[0:1, :], gw1_sb[0:1, :],
                                          gb_sb[0:1, :], ALU.mult, ALU.add),
                waits=(e_cst,), inc=True)
    e_gt = s_op(lambda e: e.activation(growB[0:1, :], growA[0:1, :], AF.Sigmoid),
                waits=(e_tg, e_a2_done), inc=True)
    v_op(lambda e: e.tensor_scalar(growA[0:1, :], growB[0:1, :], 0.05, None, ALU.is_ge),
         waits=(e_gt,))
    e_gate = v_op(lambda e: e.tensor_mul(growB[0:1, :], growB[0:1, :], growA[0:1, :]),
                  inc=True)
    with nc.psum_tensor("p_gT", [128, 1], F32) as p_gT:
        ev_gTc = None
        for j in range(NT):
            t_op(lambda e, j=j: e.transpose(p_gT[:, :],
                                            growB[0:1, j * 128:(j + 1) * 128],
                                            id_f32[0:1, 0:1]),
                 waits=(e_gate, ev_gTc, e_a3_t, e_a2_done), inc=True)
            e_gtt = now(st_)
            e_gc = v_op(lambda e: e.tensor_copy(gcol[:, :], p_gT[:, :]),
                        waits=(e_gtt,), inc=True)
            ev_gTc = e_gc
            v_op(lambda e, j=j: e.reciprocal(rcol[:, :], lffn_all[:, j:j + 1]),
                 waits=(e_a2_s,))
            v_op(lambda e, j=j: e.tensor_tensor(gs_all[:, j:j + 1], gcol[:, :],
                                                rcol[:, :], ALU.mult))
    e_gs = now(sv_)

    a4 = ExitStack()
    xp2 = [sb(a4, f"xp2{i}", [128, D], F32) for i in range(2)]
    hh_sb = sb(a4, "hh_sb", [128, D], F32)
    ffn_sb = sb(a4, "ffn_sb", [128, D], F32)
    adj2 = sb(a4, "adj2", [128, D], F32)
    outsb = [sb(a4, f"outsb{i}", [128, D], F32) for i in range(2)]
    sink2 = sb(a4, "sink2", [128, D], BF16)
    a4p = ExitStack()
    p_ao4 = [ps(a4p, f"p_ao4{i}", [128, 512]) for i in range(2)]

    ev_xp2_free = [None, None]
    ev_out_free = [None, None]
    ev_ao4_free = [None, None]
    prev_hh_users = None
    prev_sq3 = None
    prev_fin = None
    last_stats4 = [None, None]
    for j in range(NT):
        sl = j % 2
        half = j // 4
        jj = j % 4
        stt_t = stats[sl]
        e_xl = dy(sxr2[sl], xp2[sl][:, :], xp_d[j * 128:(j + 1) * 128, :],
                  waits=(now(sxp), ev_xp2_free[sl], e_a3_t, e_a2_done, e_gs))
        e_hh = s_op(lambda e, sl=sl, j=j: e.activation(
            hh_sb[:, :], xp2[sl][:, :], AF.Identity, bias=nmr2_k[:, j:j + 1],
            scale=rstd2_k[:, j:j + 1]),
            waits=(e_xl, e_p5_v, prev_hh_users), inc=True)
        cmb_evs = []
        for hc in range(8):
            par = hc % 2
            for vbc in range(2):
                t_op(lambda e, half=half, vbc=vbc, hc=hc, par=par, jj=jj: e.matmul(
                        p_ao4[par][:, :],
                        a2_all[:, half * 1024 + vbc * 512 + jj * 128:
                               half * 1024 + vbc * 512 + jj * 128 + 128],
                        WvuT[:, vbc * D + hc * 512: vbc * D + hc * 512 + 512],
                        start=(vbc == 0), stop=(vbc == 1)),
                     waits=(e_a2_done, e_wvu, ev_ao4_free[par], e_gs),
                     inc=(vbc == 1))
            e_mm = now(st_)
            e_cmb = v_op(lambda e, hc=hc, par=par, j=j, stt_t=stt_t: e.scalar_tensor_tensor(
                            ffn_sb[:, hc * 512:(hc + 1) * 512], p_ao4[par][:, :],
                            gs_all[:, j:j + 1], hh_sb[:, hc * 512:(hc + 1) * 512],
                            ALU.mult, ALU.add,
                            accum_out=stt_t[:, 9 + par:10 + par]),
                         waits=(e_mm, e_hh, e_gs, prev_sq3, last_stats4[sl]),
                         inc=True)
            ev_ao4_free[par] = e_cmb
            cmb_evs.append(e_cmb)
            if hc == 0:
                v_op(lambda e, stt_t=stt_t: e.tensor_copy(stt_t[:, 1:2],
                                                          stt_t[:, 9:10]))
            else:
                v_op(lambda e, par=par, stt_t=stt_t: e.tensor_add(
                    stt_t[:, 1:2], stt_t[:, 1:2], stt_t[:, 9 + par:10 + par]))
        e_ffull = now(sv_)
        e_sq3 = s_op(lambda e, stt_t=stt_t: e.activation(
            sink2[:, :], ffn_sb[:, :], AF.Square, accum_out=stt_t[:, 2:3]),
            waits=(e_ffull,), inc=True)
        prev_sq3 = e_sq3
        v_op(lambda e, stt_t=stt_t: e.tensor_scalar(stt_t[:, 3:4], stt_t[:, 1:2],
                                                    1.0 / D, None, ALU.mult))
        v_op(lambda e, stt_t=stt_t: e.tensor_scalar(stt_t[:, 4:5], stt_t[:, 2:3],
                                                    1.0 / D, None, ALU.mult),
             waits=(e_sq3,))
        v_op(lambda e, stt_t=stt_t: e.tensor_tensor(stt_t[:, 5:6], stt_t[:, 3:4],
                                                    stt_t[:, 3:4], ALU.mult))
        e_var3 = v_op(lambda e, stt_t=stt_t: e.tensor_sub(stt_t[:, 6:7], stt_t[:, 4:5],
                                                          stt_t[:, 5:6]), inc=True)
        s_op(lambda e, stt_t=stt_t: e.activation(stt_t[:, 7:8], stt_t[:, 6:7], AF.Ln,
                                                 bias=EPS), waits=(e_var3,))
        e_rstd3 = s_op(lambda e, stt_t=stt_t: e.activation(stt_t[:, 7:8], stt_t[:, 7:8],
                                                           AF.Exp, scale=-0.5), inc=True)
        e_nmr3 = v_op(lambda e, stt_t=stt_t: e.tensor_scalar(
            stt_t[:, 8:9], stt_t[:, 3:4], stt_t[:, 7:8], -1.0, ALU.mult, ALU.mult),
            waits=(e_rstd3,), inc=True)
        e_adj3 = s_op(lambda e, sl=sl, stt_t=stt_t: e.activation(
            adj2[:, :], xp2[sl][:, :], AF.Identity, bias=stt_t[:, 8:9]),
            waits=(e_nmr3, prev_fin), inc=True)
        ev_xp2_free[sl] = e_adj3
        e_fin = v_op(lambda e, sl=sl, stt_t=stt_t: e.scalar_tensor_tensor(
            outsb[sl][:, :], ffn_sb[:, :], stt_t[:, 7:8], adj2[:, :],
            ALU.mult, ALU.add),
            waits=(e_adj3, ev_out_free[sl]), inc=True)
        prev_fin = e_fin
        prev_hh_users = e_fin
        last_stats4[sl] = e_fin
        e_ow = dy(soo[sl], out_d[j * 128:(j + 1) * 128, :], outsb[sl][:, :],
                  waits=(e_fin,))
        ev_out_free[sl] = e_ow
    if debug:
        dq = nc.dram_tensor("dbg_qspill", [D, OWN], BF16, kind="ExternalOutput").ap()
        dk = nc.dram_tensor("dbg_kvin", [KVROWS, OWN], BF16, kind="ExternalOutput").ap()
        da = nc.dram_tensor("dbg_ao", [OWN, D], F32, kind="ExternalOutput").ap()
        dx = nc.dram_tensor("dbg_xp", [OWN, D], F32, kind="ExternalOutput").ap()
        dr = nc.dram_tensor("dbg_ar", [2, 16], F32, kind="ExternalOutput").ap()
        dy(sout, dq, qspill_d, waits=(now(soo[0]), now(soo[1]),))
        dy(sout, dk, kvin_d)
        dy(sout, da, ao_d)
        dy(sout, dx, xp_d)
        dy(sout, dr[0:1, :], arin_d)
        dy(sout, dr[1:2, :], arout_d)
        ds = nc.dram_tensor("dbg_stats", [128, 64], F32, kind="ExternalOutput").ap()
        dy(sout, ds[:, 0:8], sums_ao[:, :])
        dy(sout, ds[:, 8:16], sumsq_ao[:, :])
        dy(sout, ds[:, 16:24], lv_all[:, :])
        dy(sout, ds[:, 24:32], rstd2_k[:, :])
        dy(sout, ds[:, 32:40], nmr2_k[:, :])
        dy(sout, ds[:, 40:48], lffn_all[:, :])
        dy(sout, ds[:, 48:56], gs_all[:, :])
        dy(sout, ds[:, 56:64], stats[0][:, 0:8])
        dp5 = nc.dram_tensor("dbg_p5", [128, 64], F32, kind="ExternalOutput").ap()
        dy(sout, dp5, p5dbg[:, :])
    P.op("y", lambda e: e.nop(), waits=(now(soo[0]), now(soo[1]), now(sout)))
    a4p.close()
    a4.close()
    sKeep.close()

    with nc.Block() as block:
        P.emit(nc, block)
    ctx.close()
    return nc


_NC_CACHE = [None]


def prepare_in_maps(L0_attn_x, L0_attn_wq, L0_attn_wk, L0_attn_wv, L0_attn_wo,
                    L0_ffn_Wq, L0_ffn_Wr, L0_ffn_aux_keys, L0_ffn_aux_values,
                    L0_ffn_Wvd, L0_ffn_Wvu, L0_ffn_Wu1, L0_ffn_bu1, L0_ffn_Wu2,
                    L0_ffn_bu2, L0_ffn_gate_w1, L0_ffn_gate_bias, L0_ffn_log_rel):
    f = np.float32
    x = np.asarray(L0_attn_x, f)
    shared = dict(
        i_wq=np.ascontiguousarray(np.asarray(L0_attn_wq, f)),
        i_wk=np.ascontiguousarray(np.asarray(L0_attn_wk, f)),
        i_wv=np.ascontiguousarray(np.asarray(L0_attn_wv, f)),
        i_wo=np.ascontiguousarray(np.asarray(L0_attn_wo, f)),
        i_fWqT=np.ascontiguousarray(np.asarray(L0_ffn_Wq, f).T),
        i_fWrT=np.ascontiguousarray(np.asarray(L0_ffn_Wr, f).T),
        i_akT=np.ascontiguousarray(np.asarray(L0_ffn_aux_keys, f).T),
        i_av=np.ascontiguousarray(np.asarray(L0_ffn_aux_values, f)),
        i_WvdT=np.ascontiguousarray(np.asarray(L0_ffn_Wvd, f).T),
        i_WvuT=np.ascontiguousarray(np.asarray(L0_ffn_Wvu, f).T),
        i_Wu1T=np.ascontiguousarray(np.asarray(L0_ffn_Wu1, f).T),
        i_bu1c=np.ascontiguousarray(np.asarray(L0_ffn_bu1, f).reshape(8, 128).T),
        i_wu2c=np.ascontiguousarray(np.asarray(L0_ffn_Wu2, f).reshape(8, 128).T),
        i_bu2=np.asarray(L0_ffn_bu2, f).reshape(1, 1),
        i_gw1=np.asarray(L0_ffn_gate_w1, f).reshape(1, 1),
        i_gb=np.asarray(L0_ffn_gate_bias, f).reshape(1, 1),
        i_lrel=np.ascontiguousarray(np.asarray(L0_ffn_log_rel, f).reshape(1, 1024)),
        c_scal=np.ascontiguousarray(np.tile(np.array(
            [EPS, 1.0, -0.5, 0.125, 1.0 / math.sqrt(48.0), ISQHD, GC1, 0.0],
            dtype=np.float32), (128, 1))),
        c_idbf=np.eye(128, dtype=ml_dtypes.bfloat16),
        c_idf32=np.eye(128, dtype=np.float32),
        c_ones=np.ones((128, 128), dtype=ml_dtypes.bfloat16),
        c_rot=_rot_matrix(),
    )
    in_maps = []
    idx_by_r = {}
    for c in range(8):
        b, r = c // 4, c % 4
        if r not in idx_by_r:
            j = np.arange(NT)[:, None]
            s = np.arange(TS)[None, :]
            idx_by_r[r] = (512 * j + 4 * s + r).reshape(-1)
        idx = idx_by_r[r]
        cos, sin, dm = _host_tables(r)
        m = dict(shared)
        m["x_own"] = np.ascontiguousarray(x[b][idx])
        m["c_cos"] = cos
        m["c_sin"] = sin
        m["c_dmask"] = dm
        in_maps.append(m)
    return in_maps, idx_by_r


def kernel(**inputs):
    in_maps, idx_by_r = prepare_in_maps(**inputs)
    if _NC_CACHE[0] is None:
        _NC_CACHE[0] = build_program()
    nc = _NC_CACHE[0]
    res = run_bass_kernel_spmd(nc, in_maps, core_ids=list(range(8)))
    out = np.empty((B, T, D), dtype=np.float32)
    for c in range(8):
        b, r = c // 4, c % 4
        out[b][idx_by_r[r]] = np.asarray(res.results[c]["o_out"], np.float32)
    return out


# version 07
# speedup vs baseline: 1.5169x; 1.0411x over previous
import math
import numpy as np
import ml_dtypes
from contextlib import ExitStack

import concourse.bass as bass
import concourse.mybir as mybir
from concourse.bass_utils import run_bass_kernel_spmd

F32 = mybir.dt.float32
BF16 = mybir.dt.bfloat16
AF = mybir.ActivationFunctionType
ALU = mybir.AluOpType
AX = mybir.AxisListType

B, T, D = 2, 4096, 4096
H, KVH, HD = 32, 8, 128
NT, TS = 8, 128
OWN = NT * TS
NFC = D // 128
EPS = 1e-5
ISQHD = 1.0 / math.sqrt(HD)
NBOUND = -1.0 / HD
GC1 = 0.7978845608028654
GC2 = 0.044715
KVROWS = 2052
VST = 132


def _inv_freq():
    THETA, FACTOR, HI_FF, LO_FF, ORIG_MAX = 500000.0, 8.0, 4.0, 1.0, 8192
    inv = 1.0 / THETA ** (np.arange(0, HD, 2, dtype=np.float64) / HD)
    wavelen = 2.0 * np.pi / inv
    low_wl = ORIG_MAX / LO_FF
    high_wl = ORIG_MAX / HI_FF
    smooth = (ORIG_MAX / wavelen - LO_FF) / (HI_FF - LO_FF)
    scaled = np.where(wavelen > low_wl, inv / FACTOR, inv)
    mid = (wavelen <= low_wl) & (wavelen >= high_wl)
    scaled = np.where(mid, (1 - smooth) * inv / FACTOR + smooth * inv, scaled)
    return scaled.astype(np.float32)


def _host_tables(r):
    inv = _inv_freq()
    j = np.arange(NT)[:, None]
    s = np.arange(TS)[None, :]
    pos = (512 * j + 4 * s + r).reshape(-1).astype(np.float32)
    fr = pos[None, :] * inv[:, None]
    cos = np.concatenate([np.cos(fr), np.cos(fr)], axis=0).astype(np.float32)
    sin = np.concatenate([np.sin(fr), np.sin(fr)], axis=0).astype(np.float32)
    sp = np.arange(TS)[:, None]
    dm = np.zeros((TS, 4 * TS), dtype=np.float32)
    for rr in range(4):
        dm[:, rr * TS:(rr + 1) * TS] = (sp < s) | ((sp == s) & (rr <= r))
    return cos, sin, dm.astype(ml_dtypes.bfloat16)


def _rot_matrix():
    R = np.zeros((HD, HD), dtype=np.float32)
    for dp in range(64):
        R[dp + 64, dp] = -1.0
    for dp in range(64, 128):
        R[dp - 64, dp] = 1.0
    return R.astype(ml_dtypes.bfloat16)


class Stream:
    __slots__ = ("h", "step", "n", "name")

    def __init__(self, h, step, name):
        self.h, self.step, self.n, self.name = h, step, 0, name


class Prog:
    def __init__(self):
        self.q = {e: [] for e in "tvsyg"}
        self.hw = {e: {} for e in "tvsyg"}
        self.all_sems = []
        self.boot = None
        self.chain = {}

    def op(self, e, fn, waits=(), inc=None, step=None):
        ch = self.chain.get(e)
        if ch is not None:
            waits = tuple(waits) + ((ch, ch.n),)
            if inc is None:
                inc = ch
        ws = []
        for w in waits:
            if w is None:
                continue
            sem, val = w
            if self.hw[e].get(sem, 0) < val:
                self.hw[e][sem] = val
                ws.append((sem, val))
        st = step if step is not None else (inc.step if inc is not None else 0)
        ev = None
        if inc is not None:
            inc.n += st
            ev = (inc, inc.n)
        self.q[e].append((ws, fn, inc, st))
        return ev

    def emit(self, nc, block):
        def mk(e):
            def body(eng):
                for (ws, fn, inc, st) in self.q[e]:
                    for (sem, val) in ws:
                        eng.wait_ge(sem.h, val)
                    ins = fn(eng)
                    if inc is not None:
                        ins.then_inc(inc.h, st)
            return body
        block.tensor(mk("t"))
        block.vector(mk("v"))
        block.scalar(mk("s"))
        block.sync(mk("y"))
        block.gpsimd(mk("g"))


def build_program(debug=False, stop_after=None):
    nc = bass.Bass("TRN2", target_bir_lowering=False)

    def din(name, shape, dt=F32):
        return nc.dram_tensor(name, shape, dt, kind="ExternalInput").ap()

    x_d = din("x_own", [OWN, D])
    wq_d = din("i_wq", [D, D])
    wk_d = din("i_wk", [D, KVH * HD])
    wv_d = din("i_wv", [D, KVH * HD])
    wo_d = din("i_wo", [D, D])
    fWqT_d = din("i_fWqT", [D, 64])
    fWrT_d = din("i_fWrT", [64, 48])
    akT_d = din("i_akT", [64, 1024])
    av_d = din("i_av", [1024, D])
    WvdT_d = din("i_WvdT", [D, 256])
    WvuT_d = din("i_WvuT", [256, D])
    Wu1T_d = din("i_Wu1T", [D, 1024])
    bu1_d = din("i_bu1c", [128, 8])
    wu2c_d = din("i_wu2c", [128, 8])
    bu2_d = din("i_bu2", [1, 1])
    gw1_d = din("i_gw1", [1, 1])
    gb_d = din("i_gb", [1, 1])
    lrel_d = din("i_lrel", [1, 1024])
    cos_d = din("c_cos", [HD, OWN])
    sin_d = din("c_sin", [HD, OWN])
    id_bf_d = din("c_idbf", [128, 128], BF16)
    id_f32_d = din("c_idf32", [128, 128], F32)
    ones_bf_d = din("c_ones", [128, 128], BF16)
    rot_d = din("c_rot", [128, 128], BF16)
    dmask_d = din("c_dmask", [128, 512], BF16)
    cscal_d = din("c_scal", [128, 8], F32)
    out_d = nc.dram_tensor("o_out", [OWN, D], F32, kind="ExternalOutput").ap()

    qspill_d = nc.dram_tensor("z_qspill", [D, OWN], BF16).ap()
    kvin_d = nc.dram_tensor("z_kvin", [KVROWS, OWN], BF16).ap()
    kvg_d = nc.dram_tensor("z_kvg", [4 * KVROWS, OWN], BF16).ap()
    ao_d = nc.dram_tensor("z_ao", [OWN, D], F32).ap()
    xp_d = nc.dram_tensor("z_xp", [OWN, D], F32).ap()
    arin_d = nc.dram_tensor("z_arin", [1, 16], F32).ap()
    arout_d = nc.dram_tensor("z_arout", [1, 16], F32).ap()

    P = Prog()
    ctx = ExitStack()

    def sem(name, step):
        s = Stream(ctx.enter_context(nc.semaphore(name)), step, name)
        P.all_sems.append(s)
        return s

    def sem2(name):
        return [sem(name + "0", 16), sem(name + "1", 16)]

    sx2 = sem2("sx"); swk2 = sem2("swk"); swv = sem("swv", 16)
    swq4 = [sem(f"swq{i}", 16) for i in range(4)]; skv = sem("skv", 16)
    skh2 = sem2("skh"); svh2 = sem2("svh"); sqr2 = sem2("sqr")
    sao = sem("sao", 16); sxp = sem("sxp", 16); sxr2 = sem2("sxr")
    swo2 = sem2("swo"); sav = sem("sav", 16); swu2 = sem2("swu")
    scst = sem("scst", 16); skm = sem("skm", 16)
    swvd = sem("swvd", 16); swvu = sem("swvu", 16)
    sout = sem("sout", 16); sar = sem("sar", 16)
    ssp = sem2("ssp"); svb = sem2("svb"); sat = sem2("sat"); soo = sem2("soo")
    st_ = sem("st", 1); sv_ = sem("sv", 1); ss_ = sem("ss", 1); cc_ = sem("cc", 1)
    P.chain = {"v": sv_, "s": ss_}

    def sb(stack, name, shape, dt, side=None):
        kw = {"side": side} if side else {}
        return stack.enter_context(nc.sbuf_tensor(name, shape, dt, **kw))

    def ps(stack, name, shape, dt=F32):
        return stack.enter_context(nc.psum_tensor(name, shape, dt))

    def dy(stream, out, in_, waits=(), **kw):
        return P.op("y", lambda e: e.dma_start(out, in_, **kw), waits, inc=stream)

    def dg(stream, out, in_, waits=(), **kw):
        return P.op("g", lambda e: e.dma_start(out, in_, **kw), waits, inc=stream)

    def dg_split(stream, out3, in3, waits=(), parts=2):
        C = out3.shape[1]
        step = C // parts
        ev = None
        for i in range(parts):
            ev = dg(stream, out3[:, i * step:(i + 1) * step, :],
                    in3[:, i * step:(i + 1) * step, :], waits=waits)
        return ev

    def t_op(fn, waits=(), inc=False):
        return P.op("t", fn, waits, inc=st_ if inc else None)

    def v_op(fn, waits=(), inc=False):
        return P.op("v", fn, waits, inc=sv_ if inc else None)

    def s_op(fn, waits=(), inc=False):
        return P.op("s", fn, waits, inc=ss_ if inc else None)

    def now(s):
        return (s, s.n)

    id_bf = sb(ctx, "id_bf", [128, 128], BF16)
    id_f32 = sb(ctx, "id_f32", [128, 128], F32)
    ones_bf = sb(ctx, "ones_bf", [128, 128], BF16)
    rot_bf = sb(ctx, "rot_bf", [128, 128], BF16)
    dmask = sb(ctx, "dmask", [128, 512], BF16)
    cos_t = sb(ctx, "cos_t", [HD, OWN], F32)
    sin_t = sb(ctx, "sin_t", [HD, OWN], F32)
    WqT = sb(ctx, "WqT", [128, NFC * 64], BF16)
    WrT = sb(ctx, "WrT", [64, 48], BF16)
    akT_raw = sb(ctx, "akT_raw", [64, 1024], BF16)
    akT_aug = sb(ctx, "akT_aug", [128, 1024], BF16)
    rkT_aug = sb(ctx, "rkT_aug", [128, 1024], BF16)
    wu2c = sb(ctx, "wu2c", [128, 8], BF16)
    bu1c = sb(ctx, "bu1c", [128, 8], F32)
    bu2_sb = sb(ctx, "bu2_sb", [1, 1], F32)
    gw1_sb = sb(ctx, "gw1_sb", [1, 1], F32)
    gb_sb = sb(ctx, "gb_sb", [1, 1], F32)
    MHC = sb(ctx, "MHC", [128, 32], F32)
    qm2 = sb(ctx, "qm2", [1, 64], F32)
    qmx = sb(ctx, "qmx", [1, 32], F32)
    kn2 = sb(ctx, "kn2", [1, 16], F32)
    knx = sb(ctx, "knx", [1, 8], F32)
    km4 = sb(ctx, "km4", [1, 32], F32)
    mrow = sb(ctx, "mrow", [1, 32], F32)
    mrow_bf = sb(ctx, "mrow_bf", [1, 32], BF16)
    stats = [sb(ctx, f"stats{i}", [128, 16], F32) for i in range(2)]
    rstd2_k = sb(ctx, "rstd2_k", [128, 8], F32)
    nmr2_k = sb(ctx, "nmr2_k", [128, 8], F32)
    sums_ao = sb(ctx, "sums_ao", [128, 8], F32)
    sumsq_ao = sb(ctx, "sumsq_ao", [128, 8], F32)
    lv_all = sb(ctx, "lv_all", [128, 8], F32)
    lffn_all = sb(ctx, "lffn_all", [128, 8], F32)
    gs_all = sb(ctx, "gs_all", [128, 8], F32)
    gcol = sb(ctx, "gcol", [128, 1], F32)
    rcol = sb(ctx, "rcol", [128, 1], F32)
    lvrow = sb(ctx, "lvrow", [1, 1024], F32)
    sigrow = sb(ctx, "sigrow", [1, 1024], F32)
    growA = sb(ctx, "growA", [1, 1024], F32)
    growB = sb(ctx, "growB", [1, 1024], F32)
    arsb = sb(ctx, "arsb", [1, 16], F32)
    negbig = sb(ctx, "negbig", [128, 1], F32)
    q_aug = sb(ctx, "q_aug", [128, 1024], BF16)
    rq_aug = sb(ctx, "rq_aug", [128, 1024], BF16)
    p5dbg = sb(ctx, "p5dbg", [128, 64], F32)

    cscal = sb(ctx, "cscal", [128, 8], F32)
    CVALS = [EPS, 1.0, -0.5, 0.125, 1.0 / math.sqrt(48.0), ISQHD, GC1, 0.0]
    for i, vv in enumerate(CVALS):
        nc.const_aps.register(F32, vv, cscal[:, i:i + 1])

    e_id = dy(scst, id_bf[:, :], id_bf_d)
    dy(scst, cscal[:, :], cscal_d)
    dy(scst, id_f32[:, :], id_f32_d)
    dy(scst, ones_bf[:, :], ones_bf_d)
    dy(scst, rot_bf[:, :], rot_d)
    dy(scst, dmask[:, :], dmask_d)
    dy(scst, cos_t[:, :], cos_d)
    dy(scst, sin_t[:, :], sin_d)
    dy(scst, bu1c[:, :], bu1_d)
    dy(scst, bu2_sb[:, :], bu2_d)
    dy(scst, gw1_sb[:, :], gw1_d)
    dy(scst, gb_sb[:, :], gb_d)
    dg_split(scst, WqT[:, :].rearrange("p (c q) -> p c q", c=NFC),
       fWqT_d.rearrange("(c p) q -> p c q", p=128))
    dg(scst, WrT[:, :], fWrT_d)
    dg(scst, akT_raw[:, :], akT_d)
    dg(scst, akT_aug[64:65, :], lrel_d)
    dg(scst, rkT_aug[48:49, :], lrel_d)
    e_wu2 = dg(scst, wu2c[:, :], wu2c_d)
    e_cst = now(scst)

    e_ms1 = v_op(lambda e: e.memset(q_aug[64:96, :], 1.0), waits=(e_cst,), inc=True)
    v_op(lambda e: e.memset(rq_aug[32:64, :], 1.0), inc=True)
    e_ms2 = v_op(lambda e: e.memset(negbig[:, :], -1e30), inc=True)

    e_ak = s_op(lambda e: e.activation(akT_aug[0:64, :], akT_raw[:, :], AF.Copy,
                                       scale=0.125), waits=(e_cst,), inc=True)

    with nc.psum_tensor("p_rk", [48, 1024], F32) as p_rk:
        for half in range(2):
            t_op(lambda e, h=half: e.matmul(p_rk[:, h * 512:(h + 1) * 512],
                                            WrT[:, :], akT_raw[:, h * 512:(h + 1) * 512],
                                            start=True, stop=True),
                 waits=(e_cst,), inc=(half == 1))
        e_rk_mm = now(st_)
        e_rk = s_op(lambda e: e.activation(rkT_aug[0:48, :], p_rk[:, :], AF.Copy,
                                           scale=1.0 / math.sqrt(48.0)),
                    waits=(e_rk_mm,), inc=True)

    if stop_after == "p0":
        do = nc.dram_tensor("dbg_p0", [128, 1024], BF16, kind="ExternalOutput").ap()
        dy(sout, do, rkT_aug[:, :], waits=(e_rk, e_ak, e_ms1, e_ms2, e_cst))
        P.op("y", lambda e: e.nop(), waits=(now(sout),))
        with nc.Block() as block:
            P.emit(nc, block)
        ctx.close()
        return nc

    p1 = ExitStack()
    x_sb = [sb(p1, f"x_sb{i}", [128, D], F32) for i in range(2)]
    ln_tm = sb(p1, "ln_tm", [128, D], BF16)
    ln_fm = sb(p1, "ln_fm", [128, NFC * OWN], BF16)
    scrA = sb(p1, "scrA", [128, 512], F32)
    scrB = sb(p1, "scrB", [128, 512], F32)
    q0_bf = [sb(p1, f"q0_bf{i}", [128, 512], BF16) for i in range(2)]
    qf_bf = [sb(p1, f"qf_bf{i}", [128, 512], BF16) for i in range(2)]
    qsq_bf = [sb(p1, f"qsq_bf{i}", [128, 512], BF16) for i in range(2)]

    p1a = ExitStack()
    p_tr = [ps(p1a, f"p_tr{i}", [128, 512], BF16) for i in range(2)]

    ev_x_free = [None, None]
    ev_tr_evac = {0: None, 1: None}
    ev_lnfm = None
    last_stats = [None, None]
    last_tr_of_tile = None
    for j in range(NT):
        sl = j % 2
        stt_t = stats[sl]
        e_x = dy(sx2[sl], x_sb[sl][:, :], x_d[j * 128:(j + 1) * 128, :],
                 waits=(ev_x_free[sl],))
        e_sum = v_op(lambda e, sl=sl, stt_t=stt_t: e.reduce_sum(
            stt_t[:, 0:1], x_sb[sl][:, :], axis=AX.X),
            waits=(e_x, last_stats[sl]), inc=True)
        e_sq = s_op(lambda e, sl=sl, stt_t=stt_t: e.activation(
            ln_tm[:, :], x_sb[sl][:, :], AF.Square, accum_out=stt_t[:, 1:2]),
            waits=(e_x, last_stats[sl], last_tr_of_tile), inc=True)
        v_op(lambda e, stt_t=stt_t: e.tensor_scalar(stt_t[:, 2:3], stt_t[:, 0:1],
                                                    1.0 / D, None, ALU.mult),
             waits=(e_sum,))
        v_op(lambda e, stt_t=stt_t: e.tensor_scalar(stt_t[:, 3:4], stt_t[:, 1:2],
                                                    1.0 / D, None, ALU.mult),
             waits=(e_sq,))
        v_op(lambda e, stt_t=stt_t: e.tensor_tensor(stt_t[:, 4:5], stt_t[:, 2:3],
                                                    stt_t[:, 2:3], ALU.mult))
        e_var = v_op(lambda e, stt_t=stt_t: e.tensor_sub(stt_t[:, 5:6], stt_t[:, 3:4],
                                                         stt_t[:, 4:5]), inc=True)
        s_op(lambda e, stt_t=stt_t: e.activation(stt_t[:, 6:7], stt_t[:, 5:6], AF.Ln,
                                                 bias=EPS), waits=(e_var,))
        e_rstd = s_op(lambda e, stt_t=stt_t: e.activation(stt_t[:, 7:8], stt_t[:, 6:7],
                                                          AF.Exp, scale=-0.5), inc=True)
        e_nmr = v_op(lambda e, stt_t=stt_t: e.tensor_scalar(stt_t[:, 8:9], stt_t[:, 2:3],
                                                            stt_t[:, 7:8], -1.0,
                                                            ALU.mult, ALU.mult),
                     waits=(e_rstd,), inc=True)
        e_ln = s_op(lambda e, sl=sl, stt_t=stt_t: e.activation(
            ln_tm[:, :], x_sb[sl][:, :], AF.Identity, bias=stt_t[:, 8:9],
            scale=stt_t[:, 7:8]), waits=(e_nmr,), inc=True)
        ev_x_free[sl] = e_ln
        last_stats[sl] = e_ln
        lnfm3 = ln_fm[:, :].rearrange("p (fc t) -> p fc t", fc=NFC)
        for g in range(8):
            par = g % 2
            for k in range(4):
                fc = g * 4 + k
                t_op(lambda e, fc=fc, par=par, k=k: e.transpose(
                        p_tr[par][:, k * 128:(k + 1) * 128],
                        ln_tm[:, fc * 128:(fc + 1) * 128], id_bf[:, :]),
                     waits=(e_ln, e_id, ev_tr_evac[par]), inc=(k == 3))
            e_trg = now(st_)
            e_ev = v_op(lambda e, g=g, j=j, par=par, lnfm3=lnfm3: e.tensor_copy(
                            lnfm3[:, g * 4:(g + 1) * 4, j * 128:(j + 1) * 128],
                            p_tr[par][:, :].rearrange("p (k t) -> p k t", k=4)),
                        waits=(e_trg,), inc=True)
            ev_tr_evac[par] = e_ev
            ev_lnfm = e_ev
        last_tr_of_tile = now(st_)
    p1a.close()

    q0_used = [None, None]
    qf_used = [None, None]
    qsq_used = [None, None]

    def rope_block(p_src, e_mm, head, half, is_k, p_rot, p_qn, war):
        par = half
        c0 = half * 512
        e_q0 = s_op(lambda e, par=par: e.activation(q0_bf[par][:, :], p_src[:, :],
                                                    AF.Copy),
                    waits=(e_mm, q0_used[par], war.get("rot")), inc=True)
        e_rot = t_op(lambda e, par=par: e.matmul(p_rot[:, :], rot_bf[:, :],
                                                 q0_bf[par][:, :],
                                                 start=True, stop=True),
                     waits=(e_q0, war.get("rotv")), inc=True)
        e_t1 = v_op(lambda e, par=par, c0=c0: e.tensor_mul(
            scrA[:, :], q0_bf[par][:, :], cos_t[:, c0:c0 + 512]),
            waits=(e_q0,), inc=True)
        q0_used[par] = e_t1
        e_t2 = v_op(lambda e, c0=c0: e.tensor_mul(scrB[:, :], p_rot[:, :],
                                                  sin_t[:, c0:c0 + 512]),
                    waits=(e_rot,), inc=True)
        war["rotv"] = e_t2
        e_qf = v_op(lambda e, par=par: e.tensor_add(qf_bf[par][:, :], scrA[:, :],
                                                    scrB[:, :]),
                    waits=(e_t1, e_t2) + tuple(qf_used[par] or ()), inc=True)
        e_sq = s_op(lambda e, par=par: e.activation(qsq_bf[par][:, :], qf_bf[par][:, :],
                                                    AF.Square),
                    waits=(e_qf, qsq_used[par]), inc=True)
        e_qn = t_op(lambda e, par=par: e.matmul(p_qn[:, :], ones_bf[:, 0:1],
                                                qsq_bf[par][:, :], start=True, stop=True),
                    waits=(e_sq, war.get("qn")), inc=True)
        qsq_used[par] = e_qn
        col = head * 2 + half
        dstn = kn2 if is_k else qm2
        e_red = v_op(lambda e, col=col, dstn=dstn: e.reduce_max(
            dstn[0:1, col:col + 1], p_qn[:, :], axis=AX.X),
            waits=(e_qn,), inc=True)
        war["qn"] = e_red
        if is_k:
            e_w = dy(ssp[par], kvin_d[head * 128:(head + 1) * 128, c0:c0 + 512],
                     qf_bf[par][:, :], waits=(e_qf,))
        else:
            e_w = dy(ssp[par], qspill_d[head * 128:(head + 1) * 128, c0:c0 + 512],
                     qf_bf[par][:, :], waits=(e_qf,))
        qf_used[par] = [e_sq, e_w]
        war["rot"] = e_q0
        return e_w, e_sq

    p1b = ExitStack()
    wkcol = [sb(p1b, f"wkcol{i}", [128, NFC * 128], BF16) for i in range(2)]
    p_pj = [ps(p1b, f"p_pj{i}", [128, 512]) for i in range(2)]
    p_rot1 = ps(p1b, "p_rot1", [128, 512])
    p_qn1 = ps(p1b, "p_qn1", [1, 512])

    war1 = {}
    ev_wk_free = [None, None]
    ev_pj_free = [None, None]
    wk_r = wk_d.rearrange("(fc p) n -> p fc n", p=128)
    for kvh in range(KVH):
        sl = kvh % 2
        e_wk = dg_split(swk2[sl], wkcol[sl][:, :].rearrange("p (c m) -> p c m", c=NFC),
                  wk_r[:, :, kvh * 128:(kvh + 1) * 128], waits=(ev_wk_free[sl],))
        last_mm = None
        for half in range(2):
            par = half
            for fc in range(NFC):
                t_op(lambda e, fc=fc, par=par, sl=sl, half=half: e.matmul(
                        p_pj[par][:, :], wkcol[sl][:, fc * 128:(fc + 1) * 128],
                        ln_fm[:, fc * OWN + half * 512: fc * OWN + half * 512 + 512],
                        start=(fc == 0), stop=(fc == NFC - 1)),
                     waits=(e_wk, ev_lnfm, ev_pj_free[par]), inc=(fc == NFC - 1))
            e_mm = now(st_)
            last_mm = e_mm
            e_w, e_q0c = rope_block(p_pj[par], e_mm, kvh, half, True,
                                    p_rot1, p_qn1, war1)
            ev_pj_free[par] = war1["rot"]
        ev_wk_free[sl] = last_mm
    e_knx = v_op(lambda e: e.reduce_max(knx[0:1, :], kn2[0:1, :].rearrange(
        "p (h a) -> p h a", a=2), axis=AX.X), inc=True)
    dg(skv, kvin_d[2048:2049, 0:8], knx[0:1, :].rearrange("p (a c) -> p a c", a=1),
       waits=(e_knx,))
    e_p1b_s = now(ss_)
    e_p1b_v = now(sv_)
    p1b.close()

    p1c = ExitStack()
    wvh = sb(p1c, "wvh", [128, NFC * 512], BF16)
    v_bf = [sb(p1c, f"v_bf{i}", [128, 512], BF16) for i in range(2)]
    p_pv = [ps(p1c, f"p_pv{i}", [128, 512]) for i in range(2)]
    wv_r = wv_d.rearrange("(fc p) n -> p fc n", p=128)
    ev_wv_free = None
    ev_pv_free = [e_p1b_s, e_p1b_v]
    ev_vbf_free = [None, None]
    for vhalf in range(2):
        e_wv = dg_split(swv, wvh[:, :].rearrange("p (c m) -> p c m", c=NFC),
                  wv_r[:, :, vhalf * 512:(vhalf + 1) * 512],
                  waits=(ev_wv_free, e_p1b_s, e_p1b_v))
        last = None
        for j in range(NT):
            par = j % 2
            for fc in range(NFC):
                t_op(lambda e, fc=fc, par=par, j=j: e.matmul(
                        p_pv[par][:, :],
                        ln_fm[:, fc * OWN + j * 128: fc * OWN + j * 128 + 128],
                        wvh[:, fc * 512:(fc + 1) * 512],
                        start=(fc == 0), stop=(fc == NFC - 1)),
                     waits=(e_wv, ev_lnfm, ev_pv_free[par]), inc=(fc == NFC - 1))
            e_mm = now(st_)
            last = e_mm
            e_ev = s_op(lambda e, par=par: e.activation(v_bf[par][:, :], p_pv[par][:, :],
                                                        AF.Copy),
                        waits=(e_mm, ev_vbf_free[par]), inc=True)
            ev_pv_free[par] = e_ev
            e_vw = dy(svb[par], kvin_d[1024 + j * 128:1024 + (j + 1) * 128,
                                  vhalf * 512:(vhalf + 1) * 512],
                      v_bf[par][:, :], waits=(e_ev,))
            ev_vbf_free[par] = e_vw
        ev_wv_free = last
    e_p1c_s = now(ss_)
    p1c.close()

    kv_done_waits = (now(skv), now(ssp[0]), now(ssp[1]), now(svb[0]), now(svb[1]))

    p1d = ExitStack()
    wqcol = [sb(p1d, f"wqcol{i}", [128, NFC * 128], BF16) for i in range(4)]
    p_pj2 = [ps(p1d, f"p_pj2{i}", [128, 512]) for i in range(2)]
    p_rot2 = ps(p1d, "p_rot2", [128, 512])
    p_qn2 = ps(p1d, "p_qn2", [1, 512])
    wq_r = wq_d.rearrange("(fc p) n -> p fc n", p=128)
    war2 = {}
    ev_wq_free = [None, None, None, None]
    ev_pj2_free = [e_p1c_s, e_p1c_s]
    for h in range(H):
        sl = h % 4
        e_wq = dg_split(swq4[sl], wqcol[sl][:, :].rearrange("p (c m) -> p c m", c=NFC),
                  wq_r[:, :, h * 128:(h + 1) * 128], waits=(ev_wq_free[sl], e_p1c_s))
        last_mm = None
        for half in range(2):
            par = half
            for fc in range(NFC):
                t_op(lambda e, fc=fc, par=par, sl=sl, half=half: e.matmul(
                        p_pj2[par][:, :], wqcol[sl][:, fc * 128:(fc + 1) * 128],
                        ln_fm[:, fc * OWN + half * 512: fc * OWN + half * 512 + 512],
                        start=(fc == 0), stop=(fc == NFC - 1)),
                     waits=(e_wq, ev_lnfm, ev_pj2_free[par]), inc=(fc == NFC - 1))
            e_mm = now(st_)
            last_mm = e_mm
            e_w, e_q0c = rope_block(p_pj2[par], e_mm, h, half, False,
                                    p_rot2, p_qn2, war2)
            ev_pj2_free[par] = war2["rot"]
        ev_wq_free[sl] = last_mm
    e_qmx = v_op(lambda e: e.reduce_max(qmx[0:1, :], qm2[0:1, :].rearrange(
        "p (h a) -> p h a", a=2), axis=AX.X), inc=True)
    e_p1_t = now(st_)
    e_p1_s = now(ss_)
    e_p1_v = now(sv_)
    p1d.close()
    p1.close()

    if stop_after == "p1":
        dq = nc.dram_tensor("dbg_qspill", [D, OWN], BF16, kind="ExternalOutput").ap()
        dk = nc.dram_tensor("dbg_kvin", [KVROWS, OWN], BF16, kind="ExternalOutput").ap()
        dy(sout, dq, qspill_d,
           waits=(now(ssp[0]), now(ssp[1]), now(svb[0]), now(svb[1]), now(skv),
                  e_p1_t, e_p1_s, e_p1_v))
        dy(sout, dk, kvin_d)
        P.op("y", lambda e: e.nop(), waits=(now(sout),))
        with nc.Block() as block:
            P.emit(nc, block)
        ctx.close()
        return nc

    P.op("g", lambda e: e.collective_compute(
        "AllGather", ALU.bypass,
        replica_groups=[[0, 1, 2, 3], [4, 5, 6, 7]],
        ins=[kvin_d.opt()], outs=[kvg_d.opt()]),
        waits=kv_done_waits, inc=cc_)
    e_ag = now(cc_)

    kvg_rr = kvg_d.rearrange("(rr rows) c -> rows rr c", rows=KVROWS)
    e_km = dg(skm, km4[0:1, :].rearrange("p (rr c) -> p rr c", rr=4),
              kvg_rr[2048:2049, :, 0:8], waits=(e_ag,))
    e_k1 = v_op(lambda e: e.tensor_max(km4[0:1, 0:8], km4[0:1, 8:16], km4[0:1, 0:8]),
                waits=(e_km,), inc=True)
    e_k2 = v_op(lambda e: e.tensor_max(km4[0:1, 16:24], km4[0:1, 24:32],
                                       km4[0:1, 16:24]), waits=(e_km,), inc=True)
    e_k3 = v_op(lambda e: e.tensor_max(km4[0:1, 0:8], km4[0:1, 0:8], km4[0:1, 16:24]),
                waits=(e_k1, e_k2), inc=True)
    e_ks = s_op(lambda e: e.activation(km4[0:1, 0:8], km4[0:1, 0:8], AF.Sqrt),
                waits=(e_k3,), inc=True)
    e_qs = s_op(lambda e: e.activation(qmx[0:1, :], qmx[0:1, :], AF.Sqrt),
                waits=(e_qmx,), inc=True)
    mrow3 = mrow[0:1, :].rearrange("p (h a) -> p h a", a=4)
    km3 = km4[0:1, 0:8].rearrange("p (h a) -> p h a", a=1)
    for sub in range(4):
        v_op(lambda e, sub=sub: e.tensor_copy(mrow3[:, :, sub:sub + 1], km3),
             waits=(e_ks,))
    e_mr = v_op(lambda e: e.scalar_tensor_tensor(mrow[0:1, :], qmx[0:1, :], NBOUND,
                                                 mrow[0:1, :], ALU.mult, ALU.mult),
                waits=(e_qs,), inc=True)
    e_mrb = s_op(lambda e: e.activation(mrow_bf[0:1, :], mrow[0:1, :], AF.Copy),
                 waits=(e_mr,), inc=True)
    with nc.psum_tensor("p_mhc", [128, 32], F32) as p_mhc:
        t_op(lambda e: e.matmul(p_mhc[:, :], ones_bf[0:1, :], mrow_bf[0:1, :],
                                start=True, stop=True),
             waits=(e_mrb, e_p1_s, e_p1_v), inc=True)
        e_mh1 = now(st_)
        e_mhc = s_op(lambda e: e.activation(MHC[:, :], p_mhc[:, :], AF.Copy),
                     waits=(e_mh1,), inc=True)

    sAO = ExitStack()
    AO_fm = sb(sAO, "AO_fm", [128, H * OWN], BF16)
    p3 = ExitStack()
    K_h = [sb(p3, f"K_h{i}", [128, 4 * OWN], BF16) for i in range(2)]
    V_h = [sb(p3, f"V_h{i}", [128, 32 * VST], BF16) for i in range(2)]
    Q_h = [sb(p3, f"Q_h{i}", [128, OWN], BF16) for i in range(2)]
    PT = sb(p3, "PT", [128, NT * 512], BF16)
    AOq = [sb(p3, f"AOq{i}", [128, 128], BF16) for i in range(2)]
    rl_sb = [sb(p3, f"rl_sb{i}", [128, 1], F32) for i in range(2)]
    p_S = [ps(p3, f"p_S{i}", [128, 512]) for i in range(4)]
    p_AV = [ps(p3, f"p_AV{i}", [128, 129]) for i in range(2)]
    p_AOT = [ps(p3, f"p_AOT{i}", [128, 128], BF16) for i in range(2)]

    ev_vones = []
    for i in range(2):
        vh3 = V_h[i][:, :].rearrange("p (m c) -> p m c", c=VST)
        ev_vones.append(v_op(lambda e, vh3=vh3: e.memset(vh3[:, :, 128:129], 1.0),
                             waits=(e_p1_t, e_p1_s, now(swq4[0]), now(swq4[1]), now(swq4[2]), now(swq4[3]),
                                    now(sx2[0]), now(sx2[1])), inc=True))

    ev_kh_free = [None, None]
    ev_vh_free = [None, None]
    ev_qh_free = [None, None]
    pend_tail = [None]
    exp_ev = {0: None, 1: None, 2: None, 3: None}
    av_war = {0: None, 1: None}
    aot_war = {0: None, 1: None}
    aoq_war = {0: None, 1: None}
    rl_war = {0: None, 1: None}
    for kvh in range(KVH):
        ks = kvh % 2
        e_kh = e_vh = None
        for rr in range(4):
            e_kh = dy(skh2[ks], K_h[ks][:, rr * OWN:(rr + 1) * OWN],
                      kvg_d[rr * KVROWS + kvh * 128: rr * KVROWS + (kvh + 1) * 128, :],
                      waits=(e_ag, ev_kh_free[ks], e_p1_t, e_p1_s, e_p1_v))
            vdst = V_h[ks][:, :].rearrange("p (m c) -> p m c", c=VST)
            vsrc = kvg_d[rr * KVROWS + 1024: rr * KVROWS + 2048,
                         kvh * 128:(kvh + 1) * 128].rearrange(
                             "(jj s) e -> s jj e", s=128)
            e_vh = dy(svh2[ks], vdst[:, rr * 8:(rr + 1) * 8, 0:128], vsrc,
                      waits=(e_ag, ev_vh_free[ks], ev_vones[ks]))
        for qh4 in range(4):
            h = kvh * 4 + qh4
            qs = h % 2
            e_qh = dy(sqr2[qs], Q_h[qs][:, :], qspill_d[h * 128:(h + 1) * 128, :],
                      waits=(now(ssp[0]), now(ssp[1]), ev_qh_free[qs]))
            last_t_of_head = None
            for j in range(NT):
                avp = j % 2
                ready = {}

                def emit_S(i, j=j, h=h, ks=ks, qs=qs):
                    spar = i % 4
                    for rr in range(4):
                        t_op(lambda e, i=i, rr=rr, spar=spar: e.matmul(
                                p_S[spar][:, rr * 128:(rr + 1) * 128],
                                K_h[ks][:, rr * OWN + i * 128: rr * OWN + i * 128 + 128],
                                Q_h[qs][:, j * 128:(j + 1) * 128],
                                start=True, stop=True),
                             waits=(e_kh, e_qh, exp_ev[spar], e_mhc),
                             inc=(rr == 3))
                    e_sc = now(st_)
                    e_exp = s_op(lambda e, i=i, spar=spar: e.activation(
                                    PT[:, i * 512:(i + 1) * 512], p_S[spar][:, :],
                                    AF.Exp, bias=MHC[:, h:h + 1], scale=ISQHD),
                                 waits=(e_sc, e_mhc), inc=True)
                    exp_ev[spar] = e_exp
                    if i == j:
                        ready[i] = v_op(lambda e, i=i: e.tensor_mul(
                                        PT[:, i * 512:(i + 1) * 512],
                                        PT[:, i * 512:(i + 1) * 512], dmask[:, :]),
                                        waits=(e_exp,), inc=True)
                    else:
                        ready[i] = e_exp

                def emit_AV(i, j=j, ks=ks, avp=avp):
                    for rr in range(4):
                        t_op(lambda e, i=i, rr=rr: e.matmul(
                                p_AV[avp][:, :],
                                PT[:, i * 512 + rr * 128: i * 512 + rr * 128 + 128],
                                V_h[ks][:, (rr * 8 + i) * VST: (rr * 8 + i) * VST + 129],
                                start=(i == 0 and rr == 0), stop=(i == j and rr == 3)),
                             waits=(ready[i], e_vh, av_war[avp]),
                             inc=(i == j and rr == 3))

                LOOKAHEAD = 3
                for i in range(min(LOOKAHEAD, j + 1)):
                    emit_S(i)
                emitted_tail = False
                for i in range(j + 1):
                    if i + LOOKAHEAD <= j:
                        emit_S(i + LOOKAHEAD)
                    if not emitted_tail and i >= 1 and pend_tail[0] is not None:
                        pend_tail[0]()
                        pend_tail[0] = None
                        emitted_tail = True
                    emit_AV(i)
                e_avd = now(st_)
                last_t_of_head = e_avd
                e_rl = v_op(lambda e, avp=avp: e.reciprocal(rl_sb[avp][:, :],
                                                            p_AV[avp][:, 128:129]),
                            waits=(e_avd, rl_war[avp]), inc=True)
                e_ao = s_op(lambda e, avp=avp: e.activation(
                                AOq[avp][:, :], p_AV[avp][:, 0:128], AF.Copy,
                                scale=rl_sb[avp][:, :]),
                            waits=(e_rl, aoq_war[avp]), inc=True)
                av_war[avp] = e_ao
                rl_war[avp] = e_ao

                def mk_tail(avp=avp, h=h, j=j, e_ao=e_ao):
                    def tail():
                        t_op(lambda e: e.transpose(p_AOT[avp][:, :], AOq[avp][:, :],
                                                   id_bf[:, :]),
                             waits=(e_ao, aot_war[avp]), inc=True)
                        aoq_war[avp] = now(st_)
                        e_fm = v_op(lambda e: e.tensor_copy(
                                        AO_fm[:, h * OWN + j * 128:
                                              h * OWN + j * 128 + 128],
                                        p_AOT[avp][:, :]),
                                    waits=(now(st_),), inc=True)
                        aot_war[avp] = e_fm
                    return tail
                if pend_tail[0] is not None:
                    pend_tail[0]()
                pend_tail[0] = mk_tail()
            ev_qh_free[qs] = last_t_of_head
        ev_kh_free[ks] = last_t_of_head
        ev_vh_free[ks] = last_t_of_head
    if pend_tail[0] is not None:
        pend_tail[0]()
        pend_tail[0] = None
    e_attn_done = now(sv_)
    e_p3_t = now(st_)
    e_p3_s = now(ss_)
    p3.close()

    p4 = ExitStack()
    wosl = [sb(p4, "wosl0", [128, NFC * 512], BF16, side="right"),
            sb(p4, "wosl1", [128, NFC * 512], BF16)]
    aost = [sb(p4, f"aost{i}", [128, 512], F32) for i in range(2)]
    sink4 = sb(p4, "sink4", [128, 512], F32)
    p_o2 = [ps(p4, f"p_o2{i}", [128, 512]) for i in range(2)]
    wo_r = wo_d.rearrange("(fc p) n -> p fc n", p=128)
    ev_wo_free = [None, None]
    ev_o2_free = [None, None]
    ev_aost_free = [None, None]
    acc_war = [None, None]
    for oc in range(8):
        ws = oc % 2
        w_extra = (e_p1_t, e_p1_s, e_p1_v) if ws == 0 else (e_p3_t, e_p3_s,
                                                            e_attn_done)
        e_wo = dg_split(swo2[ws], wosl[ws][:, :].rearrange("p (c m) -> p c m", c=NFC),
                  wo_r[:, :, oc * 512:(oc + 1) * 512],
                  waits=(ev_wo_free[ws],) + w_extra)
        last = None
        for j in range(NT):
            par = j % 2
            stt_t = stats[par]
            for fc in range(NFC):
                t_op(lambda e, fc=fc, par=par, j=j, ws=ws: e.matmul(
                        p_o2[par][:, :],
                        AO_fm[:, fc * OWN + j * 128: fc * OWN + j * 128 + 128],
                        wosl[ws][:, fc * 512:(fc + 1) * 512],
                        start=(fc == 0), stop=(fc == NFC - 1)),
                     waits=(e_wo, e_attn_done, ev_o2_free[par]), inc=(fc == NFC - 1))
            e_mm = now(st_)
            last = e_mm
            e_ev = s_op(lambda e, par=par, stt_t=stt_t: e.activation(
                            aost[par][:, :], p_o2[par][:, :], AF.Copy,
                            accum_out=stt_t[:, 9:10]),
                        waits=(e_mm, ev_aost_free[par], acc_war[par]), inc=True)
            ev_o2_free[par] = e_ev
            if oc == 0:
                v_op(lambda e, j=j, stt_t=stt_t: e.tensor_copy(
                    sums_ao[:, j:j + 1], stt_t[:, 9:10]), waits=(e_ev,))
            else:
                v_op(lambda e, j=j, stt_t=stt_t: e.tensor_add(
                    sums_ao[:, j:j + 1], sums_ao[:, j:j + 1], stt_t[:, 9:10]),
                    waits=(e_ev,))
            e_sq = s_op(lambda e, par=par, stt_t=stt_t: e.activation(
                            sink4[:, :], aost[par][:, :], AF.Square,
                            accum_out=stt_t[:, 10:11]), inc=True)
            if oc == 0:
                e_ac = v_op(lambda e, j=j, stt_t=stt_t: e.tensor_copy(
                    sumsq_ao[:, j:j + 1], stt_t[:, 10:11]), waits=(e_sq,), inc=True)
            else:
                e_ac = v_op(lambda e, j=j, stt_t=stt_t: e.tensor_add(
                    sumsq_ao[:, j:j + 1], sumsq_ao[:, j:j + 1], stt_t[:, 10:11]),
                    waits=(e_sq,), inc=True)
            acc_war[par] = e_ac
            e_w = dy(sat[par], ao_d[j * 128:(j + 1) * 128, oc * 512:(oc + 1) * 512],
                     aost[par][:, :], waits=(e_sq,))
            ev_aost_free[par] = e_w
        ev_wo_free[ws] = last
    e_p4_v = now(sv_)
    e_p4_t = now(st_)
    e_p4_s = now(ss_)
    p4.close()
    sAO.close()

    sHF = ExitStack()
    h_fm = sb(sHF, "h_fm", [128, NFC * OWN], BF16, side="right")
    p5 = ExitStack()
    xp_sb = sb(p5, "xp_sb", [128, D], F32)
    x2_sb = [sb(p5, f"x2_sb{i}", [128, D], F32) for i in range(2)]
    ao_rd = sb(p5, "ao_rd", [128, D], F32)
    adj_sb = sb(p5, "adj_sb", [128, D], F32)
    ln2_tm = sb(p5, "ln2_tm", [128, D], BF16)
    p_t5 = [ps(p5, f"p_t5{i}", [128, 512], BF16) for i in range(2)]

    ev_x2_free = [None, None]
    ev_ao_free = None
    ev_tr5 = {0: None, 1: None}
    ev_hfm = None
    ev_xp_w = None
    last_stats5 = [None, None]
    last_tr5 = None
    prev_xp = None
    prev_ln2 = None
    for j in range(NT):
        sl = j % 2
        stt_t = stats[sl]
        e_x = dy(sx2[sl], x2_sb[sl][:, :], x_d[j * 128:(j + 1) * 128, :],
                 waits=(ev_x2_free[sl], e_p4_t, e_p4_s, e_p4_v))
        e_a = dy(sao, ao_rd[:, :], ao_d[j * 128:(j + 1) * 128, :],
                 waits=(now(sat[0]), now(sat[1]), ev_ao_free, e_p4_t, e_p4_s,
                        e_p4_v))
        v_op(lambda e, j=j, stt_t=stt_t: e.tensor_scalar(
            stt_t[:, 2:3], sums_ao[:, j:j + 1], 1.0 / D, None, ALU.mult),
            waits=(e_p4_v, last_stats5[sl]))
        v_op(lambda e, j=j, stt_t=stt_t: e.tensor_scalar(
            stt_t[:, 3:4], sumsq_ao[:, j:j + 1], 1.0 / D, None, ALU.mult))
        v_op(lambda e, stt_t=stt_t: e.tensor_tensor(stt_t[:, 4:5], stt_t[:, 2:3],
                                                    stt_t[:, 2:3], ALU.mult))
        e_var = v_op(lambda e, stt_t=stt_t: e.tensor_sub(stt_t[:, 5:6], stt_t[:, 3:4],
                                                         stt_t[:, 4:5]), inc=True)
        s_op(lambda e, stt_t=stt_t: e.activation(stt_t[:, 6:7], stt_t[:, 5:6], AF.Ln,
                                                 bias=EPS),
             waits=(e_var, last_stats5[sl]))
        e_rstd = s_op(lambda e, stt_t=stt_t: e.activation(stt_t[:, 7:8], stt_t[:, 6:7],
                                                          AF.Exp, scale=-0.5), inc=True)
        e_nmr = v_op(lambda e, stt_t=stt_t: e.tensor_scalar(
            stt_t[:, 8:9], stt_t[:, 2:3], stt_t[:, 7:8], -1.0, ALU.mult, ALU.mult),
            waits=(e_rstd,), inc=True)
        v_op(lambda e, j=j, stt_t=stt_t: e.tensor_copy(
            p5dbg[:, j * 8:j * 8 + 7], stt_t[:, 2:9]), waits=(e_nmr,))
        e_adj = s_op(lambda e, sl=sl, stt_t=stt_t: e.activation(
            adj_sb[:, :], x2_sb[sl][:, :], AF.Identity, bias=stt_t[:, 8:9]),
            waits=(e_x, e_nmr, prev_xp), inc=True)
        e_xp = v_op(lambda e, stt_t=stt_t: e.scalar_tensor_tensor(
            xp_sb[:, :], ao_rd[:, :], stt_t[:, 7:8], adj_sb[:, :],
            ALU.mult, ALU.add, accum_out=stt_t[:, 11:12]),
            waits=(e_a, e_adj, prev_ln2, ev_xp_w), inc=True)
        prev_xp = e_xp
        ev_x2_free[sl] = e_xp
        ev_ao_free = e_xp
        ev_xp_w = dy(sxp, xp_d[j * 128:(j + 1) * 128, :], xp_sb[:, :], waits=(e_xp,))
        e_sq2 = s_op(lambda e, stt_t=stt_t: e.activation(
            ln2_tm[:, :], xp_sb[:, :], AF.Square, accum_out=stt_t[:, 12:13]),
            waits=(e_xp, last_tr5), inc=True)
        v_op(lambda e, stt_t=stt_t: e.tensor_scalar(stt_t[:, 2:3], stt_t[:, 11:12],
                                                    1.0 / D, None, ALU.mult),
             waits=(e_xp,))
        v_op(lambda e, stt_t=stt_t: e.tensor_scalar(stt_t[:, 3:4], stt_t[:, 12:13],
                                                    1.0 / D, None, ALU.mult),
             waits=(e_sq2,))
        v_op(lambda e, stt_t=stt_t: e.tensor_tensor(stt_t[:, 4:5], stt_t[:, 2:3],
                                                    stt_t[:, 2:3], ALU.mult))
        e_var2 = v_op(lambda e, stt_t=stt_t: e.tensor_sub(stt_t[:, 5:6], stt_t[:, 3:4],
                                                          stt_t[:, 4:5]), inc=True)
        s_op(lambda e, stt_t=stt_t: e.activation(stt_t[:, 6:7], stt_t[:, 5:6], AF.Ln,
                                                 bias=EPS), waits=(e_var2,))
        e_rstd2 = s_op(lambda e, j=j, stt_t=stt_t: e.activation(
            rstd2_k[:, j:j + 1], stt_t[:, 6:7], AF.Exp, scale=-0.5), inc=True)
        e_nmr2 = v_op(lambda e, j=j, stt_t=stt_t: e.tensor_scalar(
            nmr2_k[:, j:j + 1], stt_t[:, 2:3], rstd2_k[:, j:j + 1], -1.0,
            ALU.mult, ALU.mult), waits=(e_rstd2,), inc=True)
        e_ln2 = s_op(lambda e, j=j: e.activation(ln2_tm[:, :], xp_sb[:, :], AF.Identity,
                                                 bias=nmr2_k[:, j:j + 1],
                                                 scale=rstd2_k[:, j:j + 1]),
                     waits=(e_nmr2,), inc=True)
        prev_ln2 = e_ln2
        v_op(lambda e, stt_t=stt_t: e.tensor_scalar(stt_t[:, 13:14], stt_t[:, 5:6],
                                                    EPS, None, ALU.add),
             waits=(e_var2,))
        e_rec = v_op(lambda e, stt_t=stt_t: e.reciprocal(stt_t[:, 14:15],
                                                         stt_t[:, 13:14]), inc=True)
        e_vh2 = v_op(lambda e, stt_t=stt_t: e.tensor_tensor(
            stt_t[:, 15:16], stt_t[:, 5:6], stt_t[:, 14:15], ALU.mult),
            waits=(e_rec,), inc=True)
        e_lv = s_op(lambda e, j=j, stt_t=stt_t: e.activation(
            lv_all[:, j:j + 1], stt_t[:, 15:16], AF.Ln, bias=1.0),
            waits=(e_vh2,), inc=True)
        last_stats5[sl] = e_lv
        hfm3 = h_fm[:, :].rearrange("p (fc t) -> p fc t", fc=NFC)
        for g in range(8):
            par = g % 2
            for k in range(4):
                fc = g * 4 + k
                t_op(lambda e, fc=fc, par=par, k=k: e.transpose(
                        p_t5[par][:, k * 128:(k + 1) * 128],
                        ln2_tm[:, fc * 128:(fc + 1) * 128], id_bf[:, :]),
                     waits=(e_ln2, ev_tr5[par]), inc=(k == 3))
            e_trg = now(st_)
            e_ev = v_op(lambda e, g=g, j=j, par=par, hfm3=hfm3: e.tensor_copy(
                            hfm3[:, g * 4:(g + 1) * 4, j * 128:(j + 1) * 128],
                            p_t5[par][:, :].rearrange("p (k t) -> p k t", k=4)),
                        waits=(e_trg,), inc=True)
            ev_tr5[par] = e_ev
            ev_hfm = e_ev
        last_tr5 = now(st_)
    e_lv_done = now(ss_)

    with nc.psum_tensor("p_lvT", [1, 128], F32) as p_lvT:
        e_lvc = None
        for j in range(NT):
            e_t = t_op(lambda e, j=j: e.transpose(p_lvT[:, :], lv_all[:, j:j + 1],
                                                  id_f32[:, :]),
                       waits=(e_lv_done, e_lvc), inc=True)
            e_lvc = v_op(lambda e, j=j: e.tensor_copy(
                lvrow[0:1, j * 128:(j + 1) * 128], p_lvT[0:1, :]),
                waits=(now(st_),), inc=True)
    v_op(lambda e: e.memset(arsb[0:1, :], 0.0), waits=(e_lvc,))
    e_sumlv = v_op(lambda e: e.reduce_sum(arsb[0:1, 0:1], lvrow[0:1, :], axis=AX.X),
                   inc=True)
    e_arw = dy(sar, arin_d, arsb[0:1, :], waits=(e_sumlv,))
    e_p5_t = now(st_)
    e_p5_s = now(ss_)
    e_p5_v = now(sv_)
    p5.close()

    a1 = ExitStack()
    wu1c = [sb(a1, f"wu1c{i}", [128, NFC * 128], BF16) for i in range(2)]
    gg = sb(a1, "gg", [128, 8 * OWN], BF16)
    gb_s = sb(a1, "gb_s", [128, 512], F32)
    sqg = sb(a1, "sqg", [128, 512], F32)
    x3g = sb(a1, "x3g", [128, 512], F32)
    thg = sb(a1, "thg", [128, 512], F32)
    p_g = [ps(a1, f"p_g{i}", [128, 512]) for i in range(2)]
    p_le = [ps(a1, f"p_le{i}", [1, 512]) for i in range(2)]
    wu1_r = Wu1T_d.rearrange("(fc p) n -> p fc n", p=128)
    ev_wu_free = [None, None]
    ev_g_free = [None, None]
    prev_ggv = None
    prev_x3 = None
    for uc in range(8):
        ws = uc % 2
        e_wu = dg_split(swu2[ws], wu1c[ws][:, :].rearrange("p (c m) -> p c m", c=NFC),
                  wu1_r[:, :, uc * 128:(uc + 1) * 128],
                  waits=(ev_wu_free[ws], e_p5_t, e_p5_s, e_p5_v, now(sxp)))
        last = None
        for half in range(2):
            par = half
            for fc in range(NFC):
                t_op(lambda e, fc=fc, par=par, ws=ws, half=half: e.matmul(
                        p_g[par][:, :], wu1c[ws][:, fc * 128:(fc + 1) * 128],
                        h_fm[:, fc * OWN + half * 512: fc * OWN + half * 512 + 512],
                        start=(fc == 0), stop=(fc == NFC - 1)),
                     waits=(e_wu, ev_hfm, ev_g_free[par], e_p5_v),
                     inc=(fc == NFC - 1))
            e_mm = now(st_)
            last = e_mm
            e_gb = s_op(lambda e, par=par, uc=uc: e.activation(
                gb_s[:, :], p_g[par][:, :], AF.Identity, bias=bu1c[:, uc:uc + 1]),
                waits=(e_mm, e_cst, prev_ggv), inc=True)
            ev_g_free[par] = e_gb
            e_sqg = s_op(lambda e: e.activation(sqg[:, :], gb_s[:, :], AF.Square),
                         waits=(prev_x3,), inc=True)
            e_x3 = v_op(lambda e: e.tensor_mul(x3g[:, :], sqg[:, :], gb_s[:, :]),
                        waits=(e_sqg,), inc=True)
            prev_x3 = e_x3
            e_ti = v_op(lambda e: e.scalar_tensor_tensor(thg[:, :], x3g[:, :], GC2,
                                                         gb_s[:, :], ALU.mult, ALU.add),
                        waits=(e_x3,), inc=True)
            e_th = s_op(lambda e: e.activation(thg[:, :], thg[:, :], AF.Tanh, scale=GC1),
                        waits=(e_ti,), inc=True)
            e_t5v = v_op(lambda e: e.tensor_scalar(thg[:, :], thg[:, :], 0.5, 0.5,
                                                   ALU.mult, ALU.add),
                         waits=(e_th,), inc=True)
            e_ggv = v_op(lambda e, uc=uc, half=half: e.tensor_mul(
                            gg[:, uc * OWN + half * 512: uc * OWN + half * 512 + 512],
                            thg[:, :], gb_s[:, :]),
                         waits=(e_t5v,), inc=True)
            prev_ggv = e_ggv
            t_op(lambda e, uc=uc, half=half: e.matmul(
                    p_le[half][:, :], wu2c[:, uc:uc + 1],
                    gg[:, uc * OWN + half * 512: uc * OWN + half * 512 + 512],
                    start=(uc == 0), stop=(uc == 7)),
                 waits=(e_ggv, e_wu2), inc=(uc == 7))
        ev_wu_free[ws] = last
    e_led = now(st_)
    for half in range(2):
        s_op(lambda e, half=half: e.activation(
            sigrow[0:1, half * 512:(half + 1) * 512], p_le[half][0:1, :],
            AF.Sigmoid, bias=bu2_sb[0:1, :]),
            waits=(e_led, e_cst), inc=(half == 1))
    e_sig = now(ss_)
    e_a1_t = now(st_)
    e_a1_v = now(sv_)
    a1.close()

    sA2 = ExitStack()
    wT = sb(sA2, "wT", [128, 8 * OWN], BF16)
    w_bf = sb(sA2, "w_bf", [128, 1024], BF16)
    srf = sb(sA2, "srf", [128, 1024], F32)
    mask_u8 = sb(sA2, "mask_u8", [128, 1024], mybir.dt.uint8)
    alm = sb(sA2, "alm", [128, 1024], F32)
    top8 = sb(sA2, "top8", [128, 8], F32)
    a2p = ExitStack()
    p_qy = ps(a2p, "p_qy", [64, 512])
    p_rq = ps(a2p, "p_rq", [48, 512])
    p_sr = ps(a2p, "p_sr", [128, 1024])
    p_al = ps(a2p, "p_al", [128, 1024])
    p_wt = [ps(a2p, f"p_wt{i}", [128, 512], BF16) for i in range(2)]

    e_qy_ev = None
    e_rq_ev = None
    ev_sr_free = None
    ev_al_free = None
    ev_wt_war = {0: None, 1: None}
    prev_wx = None
    prev_sel = None
    last_wtr = None
    for half in range(2):
        for fc in range(NFC):
            t_op(lambda e, fc=fc, half=half: e.matmul(
                    p_qy[:, :], WqT[:, fc * 64:(fc + 1) * 64],
                    h_fm[:, fc * OWN + half * 512: fc * OWN + half * 512 + 512],
                    start=(fc == 0), stop=(fc == NFC - 1)),
                 waits=(e_cst, ev_hfm, e_qy_ev, e_sig, e_a1_v), inc=(fc == NFC - 1))
        e_qmm = now(st_)
        e_qy_ev = s_op(lambda e, half=half: e.activation(
            q_aug[0:64, half * 512:(half + 1) * 512], p_qy[:, :], AF.Copy),
            waits=(e_qmm, e_ms1), inc=True)
        t_op(lambda e, half=half: e.matmul(p_rq[:, :], WrT[:, :],
                                           q_aug[0:64, half * 512:(half + 1) * 512],
                                           start=True, stop=True),
             waits=(e_qy_ev, e_rq_ev), inc=True)
        e_rqm = now(st_)
        e_rq_ev = s_op(lambda e, half=half: e.activation(
            rq_aug[0:48, half * 512:(half + 1) * 512], p_rq[:, :], AF.Copy),
            waits=(e_rqm,), inc=True)
        for jj in range(4):
            j = half * 4 + jj
            for hh in range(2):
                t_op(lambda e, jj=jj, half=half, hh=hh: e.matmul(
                        p_sr[:, hh * 512:(hh + 1) * 512],
                        rq_aug[0:49, half * 512 + jj * 128: half * 512 + jj * 128 + 128],
                        rkT_aug[0:49, hh * 512:(hh + 1) * 512],
                        start=True, stop=True),
                     waits=(e_rq_ev, e_rk, ev_sr_free), inc=(hh == 1))
            e_srm = now(st_)
            e_srf = s_op(lambda e: e.activation(srf[:, :], p_sr[:, :], AF.Copy),
                         waits=(e_srm, prev_sel), inc=True)
            ev_sr_free = e_srf
            e_t8 = v_op(lambda e: e.max(out=top8[:, :], in_=srf[:, :]),
                        waits=(e_srf,), inc=True)
            e_msk = v_op(lambda e: e.tensor_scalar(mask_u8[:, :], srf[:, :],
                                                   top8[:, 7:8], None, ALU.is_ge),
                         waits=(e_t8,), inc=True)
            for hh in range(2):
                t_op(lambda e, jj=jj, half=half, hh=hh: e.matmul(
                        p_al[:, hh * 512:(hh + 1) * 512],
                        q_aug[0:65, half * 512 + jj * 128: half * 512 + jj * 128 + 128],
                        akT_aug[0:65, hh * 512:(hh + 1) * 512],
                        start=True, stop=True),
                     waits=(e_qy_ev, e_ak, ev_al_free), inc=(hh == 1))
            e_almm = now(st_)
            e_sel = v_op(lambda e: e.select(alm[:, :], mask_u8[:, :], p_al[:, :],
                                            negbig[:, :].to_broadcast([128, 1024])),
                         waits=(e_almm, e_msk, e_ms2, prev_wx), inc=True)
            prev_sel = e_sel
            ev_al_free = e_sel
            e_nm = v_op(lambda e: e.reduce_max(stats[0][:, 9:10], alm[:, :], axis=AX.X,
                                               negate=True),
                        waits=(e_sel,), inc=True)
            e_wx = s_op(lambda e, j=j: e.activation(w_bf[:, :], alm[:, :], AF.Exp,
                                                    bias=stats[0][:, 9:10],
                                                    accum_out=lffn_all[:, j:j + 1]),
                        waits=(e_nm, last_wtr), inc=True)
            prev_wx = e_wx
            wt3 = wT[:, :].rearrange("p (sc t) -> p sc t", sc=8)
            for g in range(2):
                par = g
                for k in range(4):
                    sc = g * 4 + k
                    t_op(lambda e, sc=sc, par=par, k=k: e.transpose(
                            p_wt[par][:, k * 128:(k + 1) * 128],
                            w_bf[:, sc * 128:(sc + 1) * 128], id_bf[:, :]),
                         waits=(e_wx, ev_wt_war[par]), inc=(k == 3))
                e_trg = now(st_)
                e_evw = v_op(lambda e, g=g, j=j, par=par, wt3=wt3: e.tensor_copy(
                                wt3[:, g * 4:(g + 1) * 4, j * 128:(j + 1) * 128],
                                p_wt[par][:, :].rearrange("p (k t) -> p k t", k=4)),
                             waits=(e_trg,), inc=True)
                ev_wt_war[par] = e_evw
            last_wtr = now(st_)
    e_wT_done = now(sv_)
    e_a2_s = now(ss_)
    e_a2_t = now(st_)
    a2p.close()
    sHF.close()

    sKeep = ExitStack()
    WvuT = sb(sKeep, "WvuT", [128, 2 * D], BF16, side="right")
    a2_all = sb(sKeep, "a2_all", [128, 2048], BF16, side="right")
    a3 = ExitStack()
    av_sb = sb(a3, "av_sb", [128, 8 * D], BF16)
    WvdT = sb(a3, "WvdT", [128, NFC * 256], BF16)
    aux1 = sb(a3, "aux1", [128, NFC * 512], BF16)
    a3p = ExitStack()
    p_x1 = [ps(a3p, f"p_x1{i}", [128, 512]) for i in range(2)]
    p_a2 = [ps(a3p, f"p_a2{i}", [128, 512]) for i in range(2)]

    av_r = av_d.rearrange("(sc p) n -> p sc n", p=128)
    for sc in range(8):
        dg(sav, av_sb[:, sc * D:(sc + 1) * D], av_r[:, sc, :],
           waits=(e_a2_t, e_a2_s, e_wT_done))
    e_wvd = dg_split(swvd, WvdT[:, :].rearrange("p (c m) -> p c m", c=NFC),
               WvdT_d.rearrange("(c p) n -> p c n", p=128))
    e_wvu = dg(swvu, WvuT[:, :].rearrange("p (c m) -> p c m", c=2),
               WvuT_d.rearrange("(c p) n -> p c n", p=128))

    ev_x1_free = [None, None]
    ev_a2_free = [None, None]
    for half in range(2):
        for hc in range(NFC):
            par = hc % 2
            for sc in range(8):
                t_op(lambda e, hc=hc, sc=sc, half=half, par=par: e.matmul(
                        p_x1[par][:, :],
                        av_sb[:, sc * D + hc * 128: sc * D + hc * 128 + 128],
                        wT[:, sc * OWN + half * 512: sc * OWN + half * 512 + 512],
                        start=(sc == 0), stop=(sc == 7)),
                     waits=(now(sav), e_wT_done, ev_x1_free[par]), inc=(sc == 7))
            e_mm = now(st_)
            e_ev = s_op(lambda e, hc=hc, par=par: e.activation(
                            aux1[:, hc * 512:(hc + 1) * 512], p_x1[par][:, :], AF.Copy),
                        waits=(e_mm,), inc=True)
            ev_x1_free[par] = e_ev
        e_aux1 = now(ss_)
        for vbc in range(2):
            par = vbc
            for hc in range(NFC):
                t_op(lambda e, vbc=vbc, hc=hc, par=par: e.matmul(
                        p_a2[par][:, :],
                        WvdT[:, hc * 256 + vbc * 128: hc * 256 + vbc * 128 + 128],
                        aux1[:, hc * 512:(hc + 1) * 512],
                        start=(hc == 0), stop=(hc == NFC - 1)),
                     waits=(e_wvd, e_aux1, ev_a2_free[par]), inc=(hc == NFC - 1))
            e_mm = now(st_)
            e_ev = s_op(lambda e, half=half, vbc=vbc, par=par: e.activation(
                            a2_all[:, half * 1024 + vbc * 512:
                                   half * 1024 + vbc * 512 + 512],
                            p_a2[par][:, :], AF.Copy),
                        waits=(e_mm,), inc=True)
            ev_a2_free[par] = e_ev
    e_a2_done = now(ss_)
    e_a3_t = now(st_)
    a3p.close()
    a3.close()
    sA2.close()

    P.op("g", lambda e: e.collective_compute(
        "AllReduce", ALU.add, replica_groups=[list(range(8))],
        ins=[arin_d.opt()], outs=[arout_d.opt()]),
        waits=(now(sar),), inc=cc_)
    e_arr = now(cc_)
    e_arl = dg(sar, arsb[0:1, :], arout_d, waits=(e_arr,))
    v_op(lambda e: e.tensor_scalar(growA[0:1, 0:1], arsb[0:1, 0:1], 0.008192, None,
                                   ALU.add), waits=(e_arl,))
    v_op(lambda e: e.reciprocal(growA[0:1, 1:2], growA[0:1, 0:1]))
    e_hi = v_op(lambda e: e.tensor_scalar(growA[0:1, 2:3], growA[0:1, 1:2], 4096.0,
                                          None, ALU.mult), inc=True)
    v_op(lambda e: e.tensor_scalar(growB[0:1, :], sigrow[0:1, :], 2.5, None, ALU.mult),
         waits=(e_sig,))
    v_op(lambda e: e.scalar_tensor_tensor(growA[0:1, :], lvrow[0:1, :], growA[0:1, 2:3],
                                          growB[0:1, :], ALU.mult, ALU.add),
         waits=(e_hi, e_sumlv))
    v_op(lambda e: e.tensor_scalar_max(growA[0:1, :], growA[0:1, :], 0.0))
    v_op(lambda e: e.tensor_scalar_min(growA[0:1, :], growA[0:1, :], 5.0))
    v_op(lambda e: e.tensor_scalar(growA[0:1, :], growA[0:1, :], 2.0 / 3.0, -1.0 / 3.0,
                                   ALU.mult, ALU.add))
    v_op(lambda e: e.tensor_scalar_max(growA[0:1, :], growA[0:1, :], 0.0))
    v_op(lambda e: e.tensor_scalar_min(growA[0:1, :], growA[0:1, :], 1.0))
    e_tg = v_op(lambda e: e.tensor_scalar(growA[0:1, :], growA[0:1, :], gw1_sb[0:1, :],
                                          gb_sb[0:1, :], ALU.mult, ALU.add),
                waits=(e_cst,), inc=True)
    e_gt = s_op(lambda e: e.activation(growB[0:1, :], growA[0:1, :], AF.Sigmoid),
                waits=(e_tg, e_a2_done), inc=True)
    v_op(lambda e: e.tensor_scalar(growA[0:1, :], growB[0:1, :], 0.05, None, ALU.is_ge),
         waits=(e_gt,))
    e_gate = v_op(lambda e: e.tensor_mul(growB[0:1, :], growB[0:1, :], growA[0:1, :]),
                  inc=True)
    with nc.psum_tensor("p_gT", [128, 1], F32) as p_gT:
        ev_gTc = None
        for j in range(NT):
            t_op(lambda e, j=j: e.transpose(p_gT[:, :],
                                            growB[0:1, j * 128:(j + 1) * 128],
                                            id_f32[0:1, 0:1]),
                 waits=(e_gate, ev_gTc, e_a3_t, e_a2_done), inc=True)
            e_gtt = now(st_)
            e_gc = v_op(lambda e: e.tensor_copy(gcol[:, :], p_gT[:, :]),
                        waits=(e_gtt,), inc=True)
            ev_gTc = e_gc
            v_op(lambda e, j=j: e.reciprocal(rcol[:, :], lffn_all[:, j:j + 1]),
                 waits=(e_a2_s,))
            v_op(lambda e, j=j: e.tensor_tensor(gs_all[:, j:j + 1], gcol[:, :],
                                                rcol[:, :], ALU.mult))
    e_gs = now(sv_)

    a4 = ExitStack()
    xp2 = [sb(a4, f"xp2{i}", [128, D], F32) for i in range(2)]
    hh_sb = sb(a4, "hh_sb", [128, D], F32)
    ffn_sb = sb(a4, "ffn_sb", [128, D], F32)
    adj2 = sb(a4, "adj2", [128, D], F32)
    outsb = [sb(a4, f"outsb{i}", [128, D], F32) for i in range(2)]
    sink2 = sb(a4, "sink2", [128, D], BF16)
    a4p = ExitStack()
    p_ao4 = [ps(a4p, f"p_ao4{i}", [128, 512]) for i in range(2)]

    ev_xp2_free = [None, None]
    ev_out_free = [None, None]
    ev_ao4_free = [None, None]
    prev_hh_users = None
    prev_sq3 = None
    prev_fin = None
    last_stats4 = [None, None]
    for j in range(NT):
        sl = j % 2
        half = j // 4
        jj = j % 4
        stt_t = stats[sl]
        e_xl = dy(sxr2[sl], xp2[sl][:, :], xp_d[j * 128:(j + 1) * 128, :],
                  waits=(now(sxp), ev_xp2_free[sl], e_a3_t, e_a2_done, e_gs))
        e_hh = s_op(lambda e, sl=sl, j=j: e.activation(
            hh_sb[:, :], xp2[sl][:, :], AF.Identity, bias=nmr2_k[:, j:j + 1],
            scale=rstd2_k[:, j:j + 1]),
            waits=(e_xl, e_p5_v, prev_hh_users), inc=True)
        cmb_evs = []
        for hc in range(8):
            par = hc % 2
            for vbc in range(2):
                t_op(lambda e, half=half, vbc=vbc, hc=hc, par=par, jj=jj: e.matmul(
                        p_ao4[par][:, :],
                        a2_all[:, half * 1024 + vbc * 512 + jj * 128:
                               half * 1024 + vbc * 512 + jj * 128 + 128],
                        WvuT[:, vbc * D + hc * 512: vbc * D + hc * 512 + 512],
                        start=(vbc == 0), stop=(vbc == 1)),
                     waits=(e_a2_done, e_wvu, ev_ao4_free[par], e_gs),
                     inc=(vbc == 1))
            e_mm = now(st_)
            e_cmb = v_op(lambda e, hc=hc, par=par, j=j, stt_t=stt_t: e.scalar_tensor_tensor(
                            ffn_sb[:, hc * 512:(hc + 1) * 512], p_ao4[par][:, :],
                            gs_all[:, j:j + 1], hh_sb[:, hc * 512:(hc + 1) * 512],
                            ALU.mult, ALU.add,
                            accum_out=stt_t[:, 9 + par:10 + par]),
                         waits=(e_mm, e_hh, e_gs, prev_sq3, last_stats4[sl]),
                         inc=True)
            ev_ao4_free[par] = e_cmb
            cmb_evs.append(e_cmb)
            if hc == 0:
                v_op(lambda e, stt_t=stt_t: e.tensor_copy(stt_t[:, 1:2],
                                                          stt_t[:, 9:10]))
            else:
                v_op(lambda e, par=par, stt_t=stt_t: e.tensor_add(
                    stt_t[:, 1:2], stt_t[:, 1:2], stt_t[:, 9 + par:10 + par]))
        e_ffull = now(sv_)
        e_sq3 = s_op(lambda e, stt_t=stt_t: e.activation(
            sink2[:, :], ffn_sb[:, :], AF.Square, accum_out=stt_t[:, 2:3]),
            waits=(e_ffull,), inc=True)
        prev_sq3 = e_sq3
        v_op(lambda e, stt_t=stt_t: e.tensor_scalar(stt_t[:, 3:4], stt_t[:, 1:2],
                                                    1.0 / D, None, ALU.mult))
        v_op(lambda e, stt_t=stt_t: e.tensor_scalar(stt_t[:, 4:5], stt_t[:, 2:3],
                                                    1.0 / D, None, ALU.mult),
             waits=(e_sq3,))
        v_op(lambda e, stt_t=stt_t: e.tensor_tensor(stt_t[:, 5:6], stt_t[:, 3:4],
                                                    stt_t[:, 3:4], ALU.mult))
        e_var3 = v_op(lambda e, stt_t=stt_t: e.tensor_sub(stt_t[:, 6:7], stt_t[:, 4:5],
                                                          stt_t[:, 5:6]), inc=True)
        s_op(lambda e, stt_t=stt_t: e.activation(stt_t[:, 7:8], stt_t[:, 6:7], AF.Ln,
                                                 bias=EPS), waits=(e_var3,))
        e_rstd3 = s_op(lambda e, stt_t=stt_t: e.activation(stt_t[:, 7:8], stt_t[:, 7:8],
                                                           AF.Exp, scale=-0.5), inc=True)
        e_nmr3 = v_op(lambda e, stt_t=stt_t: e.tensor_scalar(
            stt_t[:, 8:9], stt_t[:, 3:4], stt_t[:, 7:8], -1.0, ALU.mult, ALU.mult),
            waits=(e_rstd3,), inc=True)
        e_adj3 = s_op(lambda e, sl=sl, stt_t=stt_t: e.activation(
            adj2[:, :], xp2[sl][:, :], AF.Identity, bias=stt_t[:, 8:9]),
            waits=(e_nmr3, prev_fin), inc=True)
        ev_xp2_free[sl] = e_adj3
        e_fin = v_op(lambda e, sl=sl, stt_t=stt_t: e.scalar_tensor_tensor(
            outsb[sl][:, :], ffn_sb[:, :], stt_t[:, 7:8], adj2[:, :],
            ALU.mult, ALU.add),
            waits=(e_adj3, ev_out_free[sl]), inc=True)
        prev_fin = e_fin
        prev_hh_users = e_fin
        last_stats4[sl] = e_fin
        e_ow = dy(soo[sl], out_d[j * 128:(j + 1) * 128, :], outsb[sl][:, :],
                  waits=(e_fin,))
        ev_out_free[sl] = e_ow
    if debug:
        dq = nc.dram_tensor("dbg_qspill", [D, OWN], BF16, kind="ExternalOutput").ap()
        dk = nc.dram_tensor("dbg_kvin", [KVROWS, OWN], BF16, kind="ExternalOutput").ap()
        da = nc.dram_tensor("dbg_ao", [OWN, D], F32, kind="ExternalOutput").ap()
        dx = nc.dram_tensor("dbg_xp", [OWN, D], F32, kind="ExternalOutput").ap()
        dr = nc.dram_tensor("dbg_ar", [2, 16], F32, kind="ExternalOutput").ap()
        dy(sout, dq, qspill_d, waits=(now(soo[0]), now(soo[1]),))
        dy(sout, dk, kvin_d)
        dy(sout, da, ao_d)
        dy(sout, dx, xp_d)
        dy(sout, dr[0:1, :], arin_d)
        dy(sout, dr[1:2, :], arout_d)
        ds = nc.dram_tensor("dbg_stats", [128, 64], F32, kind="ExternalOutput").ap()
        dy(sout, ds[:, 0:8], sums_ao[:, :])
        dy(sout, ds[:, 8:16], sumsq_ao[:, :])
        dy(sout, ds[:, 16:24], lv_all[:, :])
        dy(sout, ds[:, 24:32], rstd2_k[:, :])
        dy(sout, ds[:, 32:40], nmr2_k[:, :])
        dy(sout, ds[:, 40:48], lffn_all[:, :])
        dy(sout, ds[:, 48:56], gs_all[:, :])
        dy(sout, ds[:, 56:64], stats[0][:, 0:8])
        dp5 = nc.dram_tensor("dbg_p5", [128, 64], F32, kind="ExternalOutput").ap()
        dy(sout, dp5, p5dbg[:, :])
    P.op("y", lambda e: e.nop(), waits=(now(soo[0]), now(soo[1]), now(sout)))
    a4p.close()
    a4.close()
    sKeep.close()

    with nc.Block() as block:
        P.emit(nc, block)
    ctx.close()
    return nc


_NC_CACHE = [None]


def prepare_in_maps(L0_attn_x, L0_attn_wq, L0_attn_wk, L0_attn_wv, L0_attn_wo,
                    L0_ffn_Wq, L0_ffn_Wr, L0_ffn_aux_keys, L0_ffn_aux_values,
                    L0_ffn_Wvd, L0_ffn_Wvu, L0_ffn_Wu1, L0_ffn_bu1, L0_ffn_Wu2,
                    L0_ffn_bu2, L0_ffn_gate_w1, L0_ffn_gate_bias, L0_ffn_log_rel):
    f = np.float32
    x = np.asarray(L0_attn_x, f)
    shared = dict(
        i_wq=np.ascontiguousarray(np.asarray(L0_attn_wq, f)),
        i_wk=np.ascontiguousarray(np.asarray(L0_attn_wk, f)),
        i_wv=np.ascontiguousarray(np.asarray(L0_attn_wv, f)),
        i_wo=np.ascontiguousarray(np.asarray(L0_attn_wo, f)),
        i_fWqT=np.ascontiguousarray(np.asarray(L0_ffn_Wq, f).T),
        i_fWrT=np.ascontiguousarray(np.asarray(L0_ffn_Wr, f).T),
        i_akT=np.ascontiguousarray(np.asarray(L0_ffn_aux_keys, f).T),
        i_av=np.ascontiguousarray(np.asarray(L0_ffn_aux_values, f)),
        i_WvdT=np.ascontiguousarray(np.asarray(L0_ffn_Wvd, f).T),
        i_WvuT=np.ascontiguousarray(np.asarray(L0_ffn_Wvu, f).T),
        i_Wu1T=np.ascontiguousarray(np.asarray(L0_ffn_Wu1, f).T),
        i_bu1c=np.ascontiguousarray(np.asarray(L0_ffn_bu1, f).reshape(8, 128).T),
        i_wu2c=np.ascontiguousarray(np.asarray(L0_ffn_Wu2, f).reshape(8, 128).T),
        i_bu2=np.asarray(L0_ffn_bu2, f).reshape(1, 1),
        i_gw1=np.asarray(L0_ffn_gate_w1, f).reshape(1, 1),
        i_gb=np.asarray(L0_ffn_gate_bias, f).reshape(1, 1),
        i_lrel=np.ascontiguousarray(np.asarray(L0_ffn_log_rel, f).reshape(1, 1024)),
        c_scal=np.ascontiguousarray(np.tile(np.array(
            [EPS, 1.0, -0.5, 0.125, 1.0 / math.sqrt(48.0), ISQHD, GC1, 0.0],
            dtype=np.float32), (128, 1))),
        c_idbf=np.eye(128, dtype=ml_dtypes.bfloat16),
        c_idf32=np.eye(128, dtype=np.float32),
        c_ones=np.ones((128, 128), dtype=ml_dtypes.bfloat16),
        c_rot=_rot_matrix(),
    )
    in_maps = []
    idx_by_r = {}
    for c in range(8):
        b, r = c // 4, c % 4
        if r not in idx_by_r:
            j = np.arange(NT)[:, None]
            s = np.arange(TS)[None, :]
            idx_by_r[r] = (512 * j + 4 * s + r).reshape(-1)
        idx = idx_by_r[r]
        cos, sin, dm = _host_tables(r)
        m = dict(shared)
        m["x_own"] = np.ascontiguousarray(x[b][idx])
        m["c_cos"] = cos
        m["c_sin"] = sin
        m["c_dmask"] = dm
        in_maps.append(m)
    return in_maps, idx_by_r


def kernel(**inputs):
    in_maps, idx_by_r = prepare_in_maps(**inputs)
    if _NC_CACHE[0] is None:
        _NC_CACHE[0] = build_program()
    nc = _NC_CACHE[0]
    res = run_bass_kernel_spmd(nc, in_maps, core_ids=list(range(8)))
    out = np.empty((B, T, D), dtype=np.float32)
    for c in range(8):
        b, r = c // 4, c % 4
        out[b][idx_by_r[r]] = np.asarray(res.results[c]["o_out"], np.float32)
    return out


# version 08
# speedup vs baseline: 1.5374x; 1.0135x over previous
import math
import numpy as np
import ml_dtypes
from contextlib import ExitStack

import concourse.bass as bass
import concourse.mybir as mybir
from concourse.bass_utils import run_bass_kernel_spmd

F32 = mybir.dt.float32
BF16 = mybir.dt.bfloat16
AF = mybir.ActivationFunctionType
ALU = mybir.AluOpType
AX = mybir.AxisListType

B, T, D = 2, 4096, 4096
H, KVH, HD = 32, 8, 128
NT, TS = 8, 128
OWN = NT * TS
NFC = D // 128
EPS = 1e-5
ISQHD = 1.0 / math.sqrt(HD)
NBOUND = -1.0 / HD
GC1 = 0.7978845608028654
GC2 = 0.044715
KVROWS = 2052
VST = 132


def _inv_freq():
    THETA, FACTOR, HI_FF, LO_FF, ORIG_MAX = 500000.0, 8.0, 4.0, 1.0, 8192
    inv = 1.0 / THETA ** (np.arange(0, HD, 2, dtype=np.float64) / HD)
    wavelen = 2.0 * np.pi / inv
    low_wl = ORIG_MAX / LO_FF
    high_wl = ORIG_MAX / HI_FF
    smooth = (ORIG_MAX / wavelen - LO_FF) / (HI_FF - LO_FF)
    scaled = np.where(wavelen > low_wl, inv / FACTOR, inv)
    mid = (wavelen <= low_wl) & (wavelen >= high_wl)
    scaled = np.where(mid, (1 - smooth) * inv / FACTOR + smooth * inv, scaled)
    return scaled.astype(np.float32)


def _host_tables(r):
    inv = _inv_freq()
    j = np.arange(NT)[:, None]
    s = np.arange(TS)[None, :]
    pos = (512 * j + 4 * s + r).reshape(-1).astype(np.float32)
    fr = pos[None, :] * inv[:, None]
    cos = np.concatenate([np.cos(fr), np.cos(fr)], axis=0).astype(np.float32)
    sin = np.concatenate([np.sin(fr), np.sin(fr)], axis=0).astype(np.float32)
    sp = np.arange(TS)[:, None]
    dm = np.zeros((TS, 4 * TS), dtype=np.float32)
    for rr in range(4):
        dm[:, rr * TS:(rr + 1) * TS] = (sp < s) | ((sp == s) & (rr <= r))
    return cos, sin, dm.astype(ml_dtypes.bfloat16)


def _rot_matrix():
    R = np.zeros((HD, HD), dtype=np.float32)
    for dp in range(64):
        R[dp + 64, dp] = -1.0
    for dp in range(64, 128):
        R[dp - 64, dp] = 1.0
    return R.astype(ml_dtypes.bfloat16)


class Stream:
    __slots__ = ("h", "step", "n", "name")

    def __init__(self, h, step, name):
        self.h, self.step, self.n, self.name = h, step, 0, name


class Prog:
    def __init__(self):
        self.q = {e: [] for e in "tvsyg"}
        self.hw = {e: {} for e in "tvsyg"}
        self.all_sems = []
        self.boot = None
        self.chain = {}

    def op(self, e, fn, waits=(), inc=None, step=None):
        ch = self.chain.get(e)
        if ch is not None:
            waits = tuple(waits) + ((ch, ch.n),)
            if inc is None:
                inc = ch
        ws = []
        for w in waits:
            if w is None:
                continue
            sem, val = w
            if self.hw[e].get(sem, 0) < val:
                self.hw[e][sem] = val
                ws.append((sem, val))
        st = step if step is not None else (inc.step if inc is not None else 0)
        ev = None
        if inc is not None:
            inc.n += st
            ev = (inc, inc.n)
        self.q[e].append((ws, fn, inc, st))
        return ev

    def emit(self, nc, block):
        def mk(e):
            def body(eng):
                for (ws, fn, inc, st) in self.q[e]:
                    for (sem, val) in ws:
                        eng.wait_ge(sem.h, val)
                    ins = fn(eng)
                    if inc is not None:
                        ins.then_inc(inc.h, st)
            return body
        block.tensor(mk("t"))
        block.vector(mk("v"))
        block.scalar(mk("s"))
        block.sync(mk("y"))
        block.gpsimd(mk("g"))


def build_program(debug=False, stop_after=None):
    nc = bass.Bass("TRN2", target_bir_lowering=False)

    def din(name, shape, dt=F32):
        return nc.dram_tensor(name, shape, dt, kind="ExternalInput").ap()

    x_d = din("x_own", [OWN, D])
    wq_d = din("i_wq", [D, D])
    wk_d = din("i_wk", [D, KVH * HD])
    wv_d = din("i_wv", [D, KVH * HD])
    wo_d = din("i_wo", [D, D])
    fWqT_d = din("i_fWqT", [D, 64])
    fWrT_d = din("i_fWrT", [64, 48])
    akT_d = din("i_akT", [64, 1024])
    av_d = din("i_av", [1024, D])
    WvdT_d = din("i_WvdT", [D, 256])
    WvuT_d = din("i_WvuT", [256, D])
    Wu1T_d = din("i_Wu1T", [D, 1024])
    bu1_d = din("i_bu1c", [128, 8])
    wu2c_d = din("i_wu2c", [128, 8])
    bu2_d = din("i_bu2", [1, 1])
    gw1_d = din("i_gw1", [1, 1])
    gb_d = din("i_gb", [1, 1])
    lrel_d = din("i_lrel", [1, 1024])
    cos_d = din("c_cos", [HD, OWN])
    sin_d = din("c_sin", [HD, OWN])
    id_bf_d = din("c_idbf", [128, 128], BF16)
    id_f32_d = din("c_idf32", [128, 128], F32)
    ones_bf_d = din("c_ones", [128, 128], BF16)
    rot_d = din("c_rot", [128, 128], BF16)
    dmask_d = din("c_dmask", [128, 512], BF16)
    cscal_d = din("c_scal", [128, 8], F32)
    out_d = nc.dram_tensor("o_out", [OWN, D], F32, kind="ExternalOutput").ap()

    qspill_d = nc.dram_tensor("z_qspill", [D, OWN], BF16).ap()
    kvin_d = nc.dram_tensor("z_kvin", [KVROWS, OWN], BF16).ap()
    kvg_d = nc.dram_tensor("z_kvg", [4 * KVROWS, OWN], BF16).ap()
    ao_d = nc.dram_tensor("z_ao", [OWN, D], F32).ap()
    xp_d = nc.dram_tensor("z_xp", [OWN, D], F32).ap()
    arin_d = nc.dram_tensor("z_arin", [1, 16], F32).ap()
    arout_d = nc.dram_tensor("z_arout", [1, 16], F32).ap()

    P = Prog()
    ctx = ExitStack()

    def sem(name, step):
        s = Stream(ctx.enter_context(nc.semaphore(name)), step, name)
        P.all_sems.append(s)
        return s

    def sem2(name):
        return [sem(name + "0", 16), sem(name + "1", 16)]

    sx2 = sem2("sx"); swk2 = sem2("swk"); swv = sem("swv", 16)
    swq4 = [sem(f"swq{i}", 16) for i in range(4)]; skv = sem("skv", 16)
    skh2 = sem2("skh"); svh2 = sem2("svh"); sqr2 = sem2("sqr")
    sao = sem("sao", 16); sxp = sem("sxp", 16); sxr2 = sem2("sxr")
    swo2 = sem2("swo"); sav = sem("sav", 16); swu2 = sem2("swu")
    scst = sem("scst", 16); skm = sem("skm", 16)
    swvd = sem("swvd", 16); swvu = sem("swvu", 16)
    sout = sem("sout", 16); sar = sem("sar", 16)
    ssp = sem2("ssp"); svb = sem2("svb"); sat = sem2("sat"); soo = sem2("soo")
    st_ = sem("st", 1); sv_ = sem("sv", 1); ss_ = sem("ss", 1); cc_ = sem("cc", 1)
    P.chain = {"v": sv_, "s": ss_}

    def sb(stack, name, shape, dt, side=None):
        kw = {"side": side} if side else {}
        return stack.enter_context(nc.sbuf_tensor(name, shape, dt, **kw))

    def ps(stack, name, shape, dt=F32):
        return stack.enter_context(nc.psum_tensor(name, shape, dt))

    def dy(stream, out, in_, waits=(), **kw):
        return P.op("y", lambda e: e.dma_start(out, in_, **kw), waits, inc=stream)

    def dg(stream, out, in_, waits=(), **kw):
        return P.op("g", lambda e: e.dma_start(out, in_, **kw), waits, inc=stream)

    def dg_split(stream, out3, in3, waits=(), parts=2):
        C = out3.shape[1]
        step = C // parts
        ev = None
        for i in range(parts):
            ev = dg(stream, out3[:, i * step:(i + 1) * step, :],
                    in3[:, i * step:(i + 1) * step, :], waits=waits)
        return ev

    def t_op(fn, waits=(), inc=False):
        return P.op("t", fn, waits, inc=st_ if inc else None)

    def v_op(fn, waits=(), inc=False):
        return P.op("v", fn, waits, inc=sv_ if inc else None)

    def s_op(fn, waits=(), inc=False):
        return P.op("s", fn, waits, inc=ss_ if inc else None)

    def now(s):
        return (s, s.n)

    id_bf = sb(ctx, "id_bf", [128, 128], BF16)
    id_f32 = sb(ctx, "id_f32", [128, 128], F32)
    ones_bf = sb(ctx, "ones_bf", [128, 128], BF16)
    rot_bf = sb(ctx, "rot_bf", [128, 128], BF16)
    dmask = sb(ctx, "dmask", [128, 512], BF16)

    WqT = sb(ctx, "WqT", [128, NFC * 64], BF16)
    WrT = sb(ctx, "WrT", [64, 48], BF16)
    akT_raw = sb(ctx, "akT_raw", [64, 1024], BF16)
    akT_aug = sb(ctx, "akT_aug", [128, 1024], BF16)
    rkT_aug = sb(ctx, "rkT_aug", [128, 1024], BF16)
    wu2c = sb(ctx, "wu2c", [128, 8], BF16)
    bu1c = sb(ctx, "bu1c", [128, 8], F32)
    bu2_sb = sb(ctx, "bu2_sb", [1, 1], F32)
    gw1_sb = sb(ctx, "gw1_sb", [1, 1], F32)
    gb_sb = sb(ctx, "gb_sb", [1, 1], F32)
    MHC = sb(ctx, "MHC", [128, 32], F32)

    qmx = sb(ctx, "qmx", [1, 32], F32)

    knx = sb(ctx, "knx", [1, 8], F32)
    km4 = sb(ctx, "km4", [1, 32], F32)
    mrow = sb(ctx, "mrow", [1, 32], F32)
    mrow_bf = sb(ctx, "mrow_bf", [1, 32], BF16)
    stats = [sb(ctx, f"stats{i}", [128, 16], F32) for i in range(2)]
    rstd2_k = sb(ctx, "rstd2_k", [128, 8], F32)
    nmr2_k = sb(ctx, "nmr2_k", [128, 8], F32)
    sums_ao = sb(ctx, "sums_ao", [128, 8], F32)
    sumsq_ao = sb(ctx, "sumsq_ao", [128, 8], F32)
    lv_all = sb(ctx, "lv_all", [128, 8], F32)
    lffn_all = sb(ctx, "lffn_all", [128, 8], F32)
    gs_all = sb(ctx, "gs_all", [128, 8], F32)
    gcol = sb(ctx, "gcol", [128, 1], F32)
    rcol = sb(ctx, "rcol", [128, 1], F32)
    lvrow = sb(ctx, "lvrow", [1, 1024], F32)
    sigrow = sb(ctx, "sigrow", [1, 1024], F32)
    growA = sb(ctx, "growA", [1, 1024], F32)
    growB = sb(ctx, "growB", [1, 1024], F32)
    arsb = sb(ctx, "arsb", [1, 16], F32)
    negbig = sb(ctx, "negbig", [128, 1], F32)
    q_aug = sb(ctx, "q_aug", [128, 1024], BF16)
    rq_aug = sb(ctx, "rq_aug", [128, 1024], BF16)


    cscal = sb(ctx, "cscal", [128, 8], F32)
    CVALS = [EPS, 1.0, -0.5, 0.125, 1.0 / math.sqrt(48.0), ISQHD, GC1, 0.0]
    for i, vv in enumerate(CVALS):
        nc.const_aps.register(F32, vv, cscal[:, i:i + 1])

    e_id = dy(scst, id_bf[:, :], id_bf_d)
    dy(scst, cscal[:, :], cscal_d)
    dy(scst, id_f32[:, :], id_f32_d)
    dy(scst, ones_bf[:, :], ones_bf_d)
    dy(scst, rot_bf[:, :], rot_d)
    dy(scst, dmask[:, :], dmask_d)
    dy(scst, bu1c[:, :], bu1_d)
    dy(scst, bu2_sb[:, :], bu2_d)
    dy(scst, gw1_sb[:, :], gw1_d)
    dy(scst, gb_sb[:, :], gb_d)
    dg_split(scst, WqT[:, :].rearrange("p (c q) -> p c q", c=NFC),
       fWqT_d.rearrange("(c p) q -> p c q", p=128))
    dg(scst, WrT[:, :], fWrT_d)
    dg(scst, akT_raw[:, :], akT_d)
    dg(scst, akT_aug[64:65, :], lrel_d)
    dg(scst, rkT_aug[48:49, :], lrel_d)
    e_wu2 = dg(scst, wu2c[:, :], wu2c_d)
    e_cst = now(scst)

    e_ms1 = v_op(lambda e: e.memset(q_aug[64:96, :], 1.0), waits=(e_cst,), inc=True)
    v_op(lambda e: e.memset(rq_aug[32:64, :], 1.0), inc=True)
    e_ms2 = v_op(lambda e: e.memset(negbig[:, :], -1e30), inc=True)

    e_ak = s_op(lambda e: e.activation(akT_aug[0:64, :], akT_raw[:, :], AF.Copy,
                                       scale=0.125), waits=(e_cst,), inc=True)

    with nc.psum_tensor("p_rk", [48, 1024], F32) as p_rk:
        for half in range(2):
            t_op(lambda e, h=half: e.matmul(p_rk[:, h * 512:(h + 1) * 512],
                                            WrT[:, :], akT_raw[:, h * 512:(h + 1) * 512],
                                            start=True, stop=True),
                 waits=(e_cst,), inc=(half == 1))
        e_rk_mm = now(st_)
        e_rk = s_op(lambda e: e.activation(rkT_aug[0:48, :], p_rk[:, :], AF.Copy,
                                           scale=1.0 / math.sqrt(48.0)),
                    waits=(e_rk_mm,), inc=True)

    if stop_after == "p0":
        do = nc.dram_tensor("dbg_p0", [128, 1024], BF16, kind="ExternalOutput").ap()
        dy(sout, do, rkT_aug[:, :], waits=(e_rk, e_ak, e_ms1, e_ms2, e_cst))
        P.op("y", lambda e: e.nop(), waits=(now(sout),))
        with nc.Block() as block:
            P.emit(nc, block)
        ctx.close()
        return nc

    p1 = ExitStack()
    cos_t = sb(p1, "cos_t", [HD, OWN], F32)
    sin_t = sb(p1, "sin_t", [HD, OWN], F32)
    qm2 = sb(p1, "qm2", [1, 64], F32)
    kn2 = sb(p1, "kn2", [1, 16], F32)
    x_sb = [sb(p1, f"x_sb{i}", [128, D], F32) for i in range(2)]
    dy(scst, cos_t[:, :], cos_d)
    e_cos_ev = dy(scst, sin_t[:, :], sin_d)
    ln_tm = sb(p1, "ln_tm", [128, D], BF16)
    ln_fm = sb(p1, "ln_fm", [128, NFC * OWN], BF16)
    scrA = sb(p1, "scrA", [128, 512], F32)
    scrB = sb(p1, "scrB", [128, 512], F32)
    q0_bf = [sb(p1, f"q0_bf{i}", [128, 512], BF16) for i in range(2)]
    qf_bf = [sb(p1, f"qf_bf{i}", [128, 512], BF16) for i in range(2)]
    qsq_bf = [sb(p1, f"qsq_bf{i}", [128, 512], BF16) for i in range(2)]

    p1a = ExitStack()
    p_tr = [ps(p1a, f"p_tr{i}", [128, 512], BF16) for i in range(2)]

    ev_x_free = [None, None]
    ev_tr_evac = {0: None, 1: None}
    ev_lnfm = None
    last_stats = [None, None]
    last_tr_of_tile = None
    for j in range(NT):
        sl = j % 2
        stt_t = stats[sl]
        e_x = dy(sx2[sl], x_sb[sl][:, :], x_d[j * 128:(j + 1) * 128, :],
                 waits=(ev_x_free[sl],))
        e_sum = v_op(lambda e, sl=sl, stt_t=stt_t: e.reduce_sum(
            stt_t[:, 0:1], x_sb[sl][:, :], axis=AX.X),
            waits=(e_x, last_stats[sl]), inc=True)
        e_sq = s_op(lambda e, sl=sl, stt_t=stt_t: e.activation(
            ln_tm[:, :], x_sb[sl][:, :], AF.Square, accum_out=stt_t[:, 1:2]),
            waits=(e_x, last_stats[sl], last_tr_of_tile), inc=True)
        v_op(lambda e, stt_t=stt_t: e.tensor_scalar(stt_t[:, 2:3], stt_t[:, 0:1],
                                                    1.0 / D, None, ALU.mult),
             waits=(e_sum,))
        v_op(lambda e, stt_t=stt_t: e.tensor_scalar(stt_t[:, 3:4], stt_t[:, 1:2],
                                                    1.0 / D, None, ALU.mult),
             waits=(e_sq,))
        v_op(lambda e, stt_t=stt_t: e.tensor_tensor(stt_t[:, 4:5], stt_t[:, 2:3],
                                                    stt_t[:, 2:3], ALU.mult))
        e_var = v_op(lambda e, stt_t=stt_t: e.tensor_sub(stt_t[:, 5:6], stt_t[:, 3:4],
                                                         stt_t[:, 4:5]), inc=True)
        s_op(lambda e, stt_t=stt_t: e.activation(stt_t[:, 6:7], stt_t[:, 5:6], AF.Ln,
                                                 bias=EPS), waits=(e_var,))
        e_rstd = s_op(lambda e, stt_t=stt_t: e.activation(stt_t[:, 7:8], stt_t[:, 6:7],
                                                          AF.Exp, scale=-0.5), inc=True)
        e_nmr = v_op(lambda e, stt_t=stt_t: e.tensor_scalar(stt_t[:, 8:9], stt_t[:, 2:3],
                                                            stt_t[:, 7:8], -1.0,
                                                            ALU.mult, ALU.mult),
                     waits=(e_rstd,), inc=True)
        e_ln = s_op(lambda e, sl=sl, stt_t=stt_t: e.activation(
            ln_tm[:, :], x_sb[sl][:, :], AF.Identity, bias=stt_t[:, 8:9],
            scale=stt_t[:, 7:8]), waits=(e_nmr,), inc=True)
        ev_x_free[sl] = e_ln
        last_stats[sl] = e_ln
        lnfm3 = ln_fm[:, :].rearrange("p (fc t) -> p fc t", fc=NFC)
        for g in range(8):
            par = g % 2
            for k in range(4):
                fc = g * 4 + k
                t_op(lambda e, fc=fc, par=par, k=k: e.transpose(
                        p_tr[par][:, k * 128:(k + 1) * 128],
                        ln_tm[:, fc * 128:(fc + 1) * 128], id_bf[:, :]),
                     waits=(e_ln, e_id, ev_tr_evac[par]), inc=(k == 3))
            e_trg = now(st_)
            e_ev = v_op(lambda e, g=g, j=j, par=par, lnfm3=lnfm3: e.tensor_copy(
                            lnfm3[:, g * 4:(g + 1) * 4, j * 128:(j + 1) * 128],
                            p_tr[par][:, :].rearrange("p (k t) -> p k t", k=4)),
                        waits=(e_trg,), inc=True)
            ev_tr_evac[par] = e_ev
            ev_lnfm = e_ev
        last_tr_of_tile = now(st_)
    p1a.close()

    q0_used = [None, None]
    qf_used = [None, None]
    qsq_used = [None, None]

    pend_rope = []

    def rope_block(p_src, e_mm, head, half, is_k, p_rot, p_qn, war):
        par = half
        c0 = half * 512
        e_q0 = s_op(lambda e, par=par: e.activation(q0_bf[par][:, :], p_src[:, :],
                                                    AF.Copy),
                    waits=(e_mm, q0_used[par], war.get("rot")), inc=True)
        war["rot"] = e_q0

        def stage2():
            e_rot = t_op(lambda e, par=par: e.matmul(p_rot[:, :], rot_bf[:, :],
                                                     q0_bf[par][:, :],
                                                     start=True, stop=True),
                         waits=(e_q0, war.get("rotv")), inc=True)
            e_t1 = v_op(lambda e, par=par, c0=c0: e.tensor_mul(
                scrA[:, :], q0_bf[par][:, :], cos_t[:, c0:c0 + 512]),
                waits=(e_q0, e_cos_ev), inc=True)
            q0_used[par] = e_t1
            e_t2 = v_op(lambda e, c0=c0: e.tensor_mul(scrB[:, :], p_rot[:, :],
                                                      sin_t[:, c0:c0 + 512]),
                        waits=(e_rot,), inc=True)
            war["rotv"] = e_t2
            e_qf = v_op(lambda e, par=par: e.tensor_add(qf_bf[par][:, :], scrA[:, :],
                                                        scrB[:, :]),
                        waits=(e_t1, e_t2) + tuple(qf_used[par] or ()), inc=True)
            e_sq = s_op(lambda e, par=par: e.activation(qsq_bf[par][:, :],
                                                        qf_bf[par][:, :], AF.Square),
                        waits=(e_qf, qsq_used[par]), inc=True)
            e_qn = t_op(lambda e, par=par: e.matmul(p_qn[:, :], ones_bf[:, 0:1],
                                                    qsq_bf[par][:, :],
                                                    start=True, stop=True),
                        waits=(e_sq, war.get("qn")), inc=True)
            qsq_used[par] = e_qn
            col = head * 2 + half
            dstn = kn2 if is_k else qm2
            e_red = v_op(lambda e, col=col, dstn=dstn: e.reduce_max(
                dstn[0:1, col:col + 1], p_qn[:, :], axis=AX.X),
                waits=(e_qn,), inc=True)
            war["qn"] = e_red
            if is_k:
                dy(ssp[par], kvin_d[head * 128:(head + 1) * 128, c0:c0 + 512],
                   qf_bf[par][:, :], waits=(e_qf,))
            else:
                dy(ssp[par], qspill_d[head * 128:(head + 1) * 128, c0:c0 + 512],
                   qf_bf[par][:, :], waits=(e_qf,))
            qf_used[par] = [e_sq, (ssp[par], ssp[par].n)]
        pend_rope.append(stage2)
        return e_q0

    def flush_rope():
        while pend_rope:
            pend_rope.pop(0)()

    p1b = ExitStack()
    wkcol = [sb(p1b, f"wkcol{i}", [128, NFC * 128], BF16) for i in range(2)]
    p_pj = [ps(p1b, f"p_pj{i}", [128, 512]) for i in range(2)]
    p_rot1 = ps(p1b, "p_rot1", [128, 512])
    p_qn1 = ps(p1b, "p_qn1", [1, 512])

    war1 = {}
    ev_wk_free = [None, None]
    ev_pj_free = [None, None]
    wk_r = wk_d.rearrange("(fc p) n -> p fc n", p=128)
    for kvh in range(KVH):
        sl = kvh % 2
        e_wk = dg_split(swk2[sl], wkcol[sl][:, :].rearrange("p (c m) -> p c m", c=NFC),
                  wk_r[:, :, kvh * 128:(kvh + 1) * 128], waits=(ev_wk_free[sl],))
        last_mm = None
        for half in range(2):
            par = half
            for fc in range(NFC):
                t_op(lambda e, fc=fc, par=par, sl=sl, half=half: e.matmul(
                        p_pj[par][:, :], wkcol[sl][:, fc * 128:(fc + 1) * 128],
                        ln_fm[:, fc * OWN + half * 512: fc * OWN + half * 512 + 512],
                        start=(fc == 0), stop=(fc == NFC - 1)),
                     waits=(e_wk, ev_lnfm, ev_pj_free[par]), inc=(fc == NFC - 1))
                if fc == 8:
                    flush_rope()
            e_mm = now(st_)
            last_mm = e_mm
            ev_pj_free[par] = rope_block(p_pj[par], e_mm, kvh, half, True,
                                         p_rot1, p_qn1, war1)
        ev_wk_free[sl] = last_mm
    flush_rope()
    e_knx = v_op(lambda e: e.reduce_max(knx[0:1, :], kn2[0:1, :].rearrange(
        "p (h a) -> p h a", a=2), axis=AX.X), inc=True)
    dg(skv, kvin_d[2048:2049, 0:8], knx[0:1, :].rearrange("p (a c) -> p a c", a=1),
       waits=(e_knx,))
    e_p1b_s = now(ss_)
    e_p1b_v = now(sv_)
    p1b.close()

    p1c = ExitStack()
    wvh = sb(p1c, "wvh", [128, NFC * 512], BF16)
    v_bf = [sb(p1c, f"v_bf{i}", [128, 512], BF16) for i in range(2)]
    p_pv = [ps(p1c, f"p_pv{i}", [128, 512]) for i in range(2)]
    wv_r = wv_d.rearrange("(fc p) n -> p fc n", p=128)
    ev_wv_free = None
    ev_pv_free = [e_p1b_s, e_p1b_v]
    ev_vbf_free = [None, None]
    for vhalf in range(2):
        e_wv = dg_split(swv, wvh[:, :].rearrange("p (c m) -> p c m", c=NFC),
                  wv_r[:, :, vhalf * 512:(vhalf + 1) * 512],
                  waits=(ev_wv_free, e_p1b_s, e_p1b_v))
        last = None
        for j in range(NT):
            par = j % 2
            for fc in range(NFC):
                t_op(lambda e, fc=fc, par=par, j=j: e.matmul(
                        p_pv[par][:, :],
                        ln_fm[:, fc * OWN + j * 128: fc * OWN + j * 128 + 128],
                        wvh[:, fc * 512:(fc + 1) * 512],
                        start=(fc == 0), stop=(fc == NFC - 1)),
                     waits=(e_wv, ev_lnfm, ev_pv_free[par]), inc=(fc == NFC - 1))
            e_mm = now(st_)
            last = e_mm
            e_ev = s_op(lambda e, par=par: e.activation(v_bf[par][:, :], p_pv[par][:, :],
                                                        AF.Copy),
                        waits=(e_mm, ev_vbf_free[par]), inc=True)
            ev_pv_free[par] = e_ev
            e_vw = dy(svb[par], kvin_d[1024 + j * 128:1024 + (j + 1) * 128,
                                  vhalf * 512:(vhalf + 1) * 512],
                      v_bf[par][:, :], waits=(e_ev,))
            ev_vbf_free[par] = e_vw
        ev_wv_free = last
    e_p1c_s = now(ss_)
    p1c.close()

    kv_done_waits = (now(skv), now(ssp[0]), now(ssp[1]), now(svb[0]), now(svb[1]))

    p1d = ExitStack()
    wqcol = [sb(p1d, f"wqcol{i}", [128, NFC * 128], BF16) for i in range(4)]
    p_pj2 = [ps(p1d, f"p_pj2{i}", [128, 512]) for i in range(2)]
    p_rot2 = ps(p1d, "p_rot2", [128, 512])
    p_qn2 = ps(p1d, "p_qn2", [1, 512])
    wq_r = wq_d.rearrange("(fc p) n -> p fc n", p=128)
    war2 = {}
    ev_wq_free = [None, None, None, None]
    ev_pj2_free = [e_p1c_s, e_p1c_s]
    for h in range(H):
        sl = h % 4
        e_wq = dg_split(swq4[sl], wqcol[sl][:, :].rearrange("p (c m) -> p c m", c=NFC),
                  wq_r[:, :, h * 128:(h + 1) * 128], waits=(ev_wq_free[sl], e_p1c_s))
        last_mm = None
        for half in range(2):
            par = half
            for fc in range(NFC):
                t_op(lambda e, fc=fc, par=par, sl=sl, half=half: e.matmul(
                        p_pj2[par][:, :], wqcol[sl][:, fc * 128:(fc + 1) * 128],
                        ln_fm[:, fc * OWN + half * 512: fc * OWN + half * 512 + 512],
                        start=(fc == 0), stop=(fc == NFC - 1)),
                     waits=(e_wq, ev_lnfm, ev_pj2_free[par]), inc=(fc == NFC - 1))
                if fc == 8:
                    flush_rope()
            e_mm = now(st_)
            last_mm = e_mm
            ev_pj2_free[par] = rope_block(p_pj2[par], e_mm, h, half, False,
                                          p_rot2, p_qn2, war2)
        ev_wq_free[sl] = last_mm
    flush_rope()
    e_qmx = v_op(lambda e: e.reduce_max(qmx[0:1, :], qm2[0:1, :].rearrange(
        "p (h a) -> p h a", a=2), axis=AX.X), inc=True)
    e_p1_t = now(st_)
    e_p1_s = now(ss_)
    e_p1_v = now(sv_)
    p1d.close()
    p1.close()

    if stop_after == "p1":
        dq = nc.dram_tensor("dbg_qspill", [D, OWN], BF16, kind="ExternalOutput").ap()
        dk = nc.dram_tensor("dbg_kvin", [KVROWS, OWN], BF16, kind="ExternalOutput").ap()
        dy(sout, dq, qspill_d,
           waits=(now(ssp[0]), now(ssp[1]), now(svb[0]), now(svb[1]), now(skv),
                  e_p1_t, e_p1_s, e_p1_v))
        dy(sout, dk, kvin_d)
        P.op("y", lambda e: e.nop(), waits=(now(sout),))
        with nc.Block() as block:
            P.emit(nc, block)
        ctx.close()
        return nc

    P.op("g", lambda e: e.collective_compute(
        "AllGather", ALU.bypass,
        replica_groups=[[0, 1, 2, 3], [4, 5, 6, 7]],
        ins=[kvin_d.opt()], outs=[kvg_d.opt()]),
        waits=kv_done_waits, inc=cc_)
    e_ag = now(cc_)

    kvg_rr = kvg_d.rearrange("(rr rows) c -> rows rr c", rows=KVROWS)
    e_km = dg(skm, km4[0:1, :].rearrange("p (rr c) -> p rr c", rr=4),
              kvg_rr[2048:2049, :, 0:8], waits=(e_ag,))
    e_k1 = v_op(lambda e: e.tensor_max(km4[0:1, 0:8], km4[0:1, 8:16], km4[0:1, 0:8]),
                waits=(e_km,), inc=True)
    e_k2 = v_op(lambda e: e.tensor_max(km4[0:1, 16:24], km4[0:1, 24:32],
                                       km4[0:1, 16:24]), waits=(e_km,), inc=True)
    e_k3 = v_op(lambda e: e.tensor_max(km4[0:1, 0:8], km4[0:1, 0:8], km4[0:1, 16:24]),
                waits=(e_k1, e_k2), inc=True)
    e_ks = s_op(lambda e: e.activation(km4[0:1, 0:8], km4[0:1, 0:8], AF.Sqrt),
                waits=(e_k3,), inc=True)
    e_qs = s_op(lambda e: e.activation(qmx[0:1, :], qmx[0:1, :], AF.Sqrt),
                waits=(e_qmx,), inc=True)
    mrow3 = mrow[0:1, :].rearrange("p (h a) -> p h a", a=4)
    km3 = km4[0:1, 0:8].rearrange("p (h a) -> p h a", a=1)
    for sub in range(4):
        v_op(lambda e, sub=sub: e.tensor_copy(mrow3[:, :, sub:sub + 1], km3),
             waits=(e_ks,))
    e_mr = v_op(lambda e: e.scalar_tensor_tensor(mrow[0:1, :], qmx[0:1, :], NBOUND,
                                                 mrow[0:1, :], ALU.mult, ALU.mult),
                waits=(e_qs,), inc=True)
    e_mrb = s_op(lambda e: e.activation(mrow_bf[0:1, :], mrow[0:1, :], AF.Copy),
                 waits=(e_mr,), inc=True)
    with nc.psum_tensor("p_mhc", [128, 32], F32) as p_mhc:
        t_op(lambda e: e.matmul(p_mhc[:, :], ones_bf[0:1, :], mrow_bf[0:1, :],
                                start=True, stop=True),
             waits=(e_mrb, e_p1_s, e_p1_v), inc=True)
        e_mh1 = now(st_)
        e_mhc = s_op(lambda e: e.activation(MHC[:, :], p_mhc[:, :], AF.Copy),
                     waits=(e_mh1,), inc=True)

    sAO = ExitStack()
    AO_fm = sb(sAO, "AO_fm", [128, H * OWN], BF16)
    p3 = ExitStack()
    K_h = [sb(p3, f"K_h{i}", [128, 4 * OWN], BF16) for i in range(2)]
    V_h = [sb(p3, f"V_h{i}", [128, 32 * VST], BF16) for i in range(2)]
    Q_h = [sb(p3, f"Q_h{i}", [128, OWN], BF16) for i in range(2)]
    PT = sb(p3, "PT", [128, NT * 512], BF16)
    AOq = [sb(p3, f"AOq{i}", [128, 128], BF16) for i in range(2)]
    rl_sb = [sb(p3, f"rl_sb{i}", [128, 1], F32) for i in range(2)]
    p_S = [ps(p3, f"p_S{i}", [128, 512]) for i in range(4)]
    p_AV = [ps(p3, f"p_AV{i}", [128, 129]) for i in range(2)]
    p_AOT = [ps(p3, f"p_AOT{i}", [128, 128], BF16) for i in range(2)]

    ev_vones = []
    for i in range(2):
        vh3 = V_h[i][:, :].rearrange("p (m c) -> p m c", c=VST)
        ev_vones.append(v_op(lambda e, vh3=vh3: e.memset(vh3[:, :, 128:129], 1.0),
                             waits=(e_p1_t, e_p1_s, now(swq4[0]), now(swq4[1]), now(swq4[2]), now(swq4[3]),
                                    now(sx2[0]), now(sx2[1])), inc=True))

    ev_kh_free = [None, None]
    ev_vh_free = [None, None]
    ev_qh_free = [None, None]
    pend_tail = [None]
    exp_ev = {0: None, 1: None, 2: None, 3: None}
    av_war = {0: None, 1: None}
    aot_war = {0: None, 1: None}
    aoq_war = {0: None, 1: None}
    rl_war = {0: None, 1: None}
    for kvh in range(KVH):
        ks = kvh % 2
        e_kh = e_vh = None
        for rr in range(4):
            e_kh = dy(skh2[ks], K_h[ks][:, rr * OWN:(rr + 1) * OWN],
                      kvg_d[rr * KVROWS + kvh * 128: rr * KVROWS + (kvh + 1) * 128, :],
                      waits=(e_ag, ev_kh_free[ks], e_p1_t, e_p1_s, e_p1_v))
            vdst = V_h[ks][:, :].rearrange("p (m c) -> p m c", c=VST)
            vsrc = kvg_d[rr * KVROWS + 1024: rr * KVROWS + 2048,
                         kvh * 128:(kvh + 1) * 128].rearrange(
                             "(jj s) e -> s jj e", s=128)
            e_vh = dy(svh2[ks], vdst[:, rr * 8:(rr + 1) * 8, 0:128], vsrc,
                      waits=(e_ag, ev_vh_free[ks], ev_vones[ks]))
        for qh4 in range(4):
            h = kvh * 4 + qh4
            qs = h % 2
            e_qh = dy(sqr2[qs], Q_h[qs][:, :], qspill_d[h * 128:(h + 1) * 128, :],
                      waits=(now(ssp[0]), now(ssp[1]), ev_qh_free[qs]))
            last_t_of_head = None
            for j in range(NT):
                avp = j % 2
                ready = {}

                def emit_S(i, j=j, h=h, ks=ks, qs=qs):
                    spar = i % 4
                    for rr in range(4):
                        t_op(lambda e, i=i, rr=rr, spar=spar: e.matmul(
                                p_S[spar][:, rr * 128:(rr + 1) * 128],
                                K_h[ks][:, rr * OWN + i * 128: rr * OWN + i * 128 + 128],
                                Q_h[qs][:, j * 128:(j + 1) * 128],
                                start=True, stop=True),
                             waits=(e_kh, e_qh, exp_ev[spar], e_mhc),
                             inc=(rr == 3))
                    e_sc = now(st_)
                    e_exp = s_op(lambda e, i=i, spar=spar: e.activation(
                                    PT[:, i * 512:(i + 1) * 512], p_S[spar][:, :],
                                    AF.Exp, bias=MHC[:, h:h + 1], scale=ISQHD),
                                 waits=(e_sc, e_mhc), inc=True)
                    exp_ev[spar] = e_exp
                    if i == j:
                        ready[i] = v_op(lambda e, i=i: e.tensor_mul(
                                        PT[:, i * 512:(i + 1) * 512],
                                        PT[:, i * 512:(i + 1) * 512], dmask[:, :]),
                                        waits=(e_exp,), inc=True)
                    else:
                        ready[i] = e_exp

                def emit_AV(i, j=j, ks=ks, avp=avp):
                    for rr in range(4):
                        t_op(lambda e, i=i, rr=rr: e.matmul(
                                p_AV[avp][:, :],
                                PT[:, i * 512 + rr * 128: i * 512 + rr * 128 + 128],
                                V_h[ks][:, (rr * 8 + i) * VST: (rr * 8 + i) * VST + 129],
                                start=(i == 0 and rr == 0), stop=(i == j and rr == 3)),
                             waits=(ready[i], e_vh, av_war[avp]),
                             inc=(i == j and rr == 3))

                LOOKAHEAD = 3
                for i in range(min(LOOKAHEAD, j + 1)):
                    emit_S(i)
                emitted_tail = False
                for i in range(j + 1):
                    if i + LOOKAHEAD <= j:
                        emit_S(i + LOOKAHEAD)
                    if not emitted_tail and i >= 1 and pend_tail[0] is not None:
                        pend_tail[0]()
                        pend_tail[0] = None
                        emitted_tail = True
                    emit_AV(i)
                e_avd = now(st_)
                last_t_of_head = e_avd
                e_rl = v_op(lambda e, avp=avp: e.reciprocal(rl_sb[avp][:, :],
                                                            p_AV[avp][:, 128:129]),
                            waits=(e_avd, rl_war[avp]), inc=True)
                e_ao = s_op(lambda e, avp=avp: e.activation(
                                AOq[avp][:, :], p_AV[avp][:, 0:128], AF.Copy,
                                scale=rl_sb[avp][:, :]),
                            waits=(e_rl, aoq_war[avp]), inc=True)
                av_war[avp] = e_ao
                rl_war[avp] = e_ao

                def mk_tail(avp=avp, h=h, j=j, e_ao=e_ao):
                    def tail():
                        t_op(lambda e: e.transpose(p_AOT[avp][:, :], AOq[avp][:, :],
                                                   id_bf[:, :]),
                             waits=(e_ao, aot_war[avp]), inc=True)
                        aoq_war[avp] = now(st_)
                        e_fm = v_op(lambda e: e.tensor_copy(
                                        AO_fm[:, h * OWN + j * 128:
                                              h * OWN + j * 128 + 128],
                                        p_AOT[avp][:, :]),
                                    waits=(now(st_),), inc=True)
                        aot_war[avp] = e_fm
                    return tail
                if pend_tail[0] is not None:
                    pend_tail[0]()
                pend_tail[0] = mk_tail()
            ev_qh_free[qs] = last_t_of_head
        ev_kh_free[ks] = last_t_of_head
        ev_vh_free[ks] = last_t_of_head
    if pend_tail[0] is not None:
        pend_tail[0]()
        pend_tail[0] = None
    e_attn_done = now(sv_)
    e_p3_t = now(st_)
    e_p3_s = now(ss_)
    p3.close()

    p4 = ExitStack()
    wosl = [sb(p4, "wosl0", [128, NFC * 512], BF16, side="right"),
            sb(p4, "wosl1", [128, NFC * 512], BF16)]
    aost = [sb(p4, f"aost{i}", [128, 512], F32) for i in range(2)]
    sink4 = sb(p4, "sink4", [128, 512], F32)
    p_o2 = [ps(p4, f"p_o2{i}", [128, 512]) for i in range(2)]
    wo_r = wo_d.rearrange("(fc p) n -> p fc n", p=128)
    ev_wo_free = [None, None]
    ev_o2_free = [None, None]
    ev_aost_free = [None, None]
    acc_war = [None, None]
    for oc in range(8):
        ws = oc % 2
        w_extra = (e_p1_t, e_p1_s, e_p1_v) if ws == 0 else (e_p3_t, e_p3_s,
                                                            e_attn_done)
        e_wo = dg_split(swo2[ws], wosl[ws][:, :].rearrange("p (c m) -> p c m", c=NFC),
                  wo_r[:, :, oc * 512:(oc + 1) * 512],
                  waits=(ev_wo_free[ws],) + w_extra)
        last = None
        for j in range(NT):
            par = j % 2
            stt_t = stats[par]
            for fc in range(NFC):
                t_op(lambda e, fc=fc, par=par, j=j, ws=ws: e.matmul(
                        p_o2[par][:, :],
                        AO_fm[:, fc * OWN + j * 128: fc * OWN + j * 128 + 128],
                        wosl[ws][:, fc * 512:(fc + 1) * 512],
                        start=(fc == 0), stop=(fc == NFC - 1)),
                     waits=(e_wo, e_attn_done, ev_o2_free[par]), inc=(fc == NFC - 1))
            e_mm = now(st_)
            last = e_mm
            e_ev = s_op(lambda e, par=par, stt_t=stt_t: e.activation(
                            aost[par][:, :], p_o2[par][:, :], AF.Copy,
                            accum_out=stt_t[:, 9:10]),
                        waits=(e_mm, ev_aost_free[par], acc_war[par]), inc=True)
            ev_o2_free[par] = e_ev
            if oc == 0:
                v_op(lambda e, j=j, stt_t=stt_t: e.tensor_copy(
                    sums_ao[:, j:j + 1], stt_t[:, 9:10]), waits=(e_ev,))
            else:
                v_op(lambda e, j=j, stt_t=stt_t: e.tensor_add(
                    sums_ao[:, j:j + 1], sums_ao[:, j:j + 1], stt_t[:, 9:10]),
                    waits=(e_ev,))
            e_sq = s_op(lambda e, par=par, stt_t=stt_t: e.activation(
                            sink4[:, :], aost[par][:, :], AF.Square,
                            accum_out=stt_t[:, 10:11]), inc=True)
            if oc == 0:
                e_ac = v_op(lambda e, j=j, stt_t=stt_t: e.tensor_copy(
                    sumsq_ao[:, j:j + 1], stt_t[:, 10:11]), waits=(e_sq,), inc=True)
            else:
                e_ac = v_op(lambda e, j=j, stt_t=stt_t: e.tensor_add(
                    sumsq_ao[:, j:j + 1], sumsq_ao[:, j:j + 1], stt_t[:, 10:11]),
                    waits=(e_sq,), inc=True)
            acc_war[par] = e_ac
            e_w = dy(sat[par], ao_d[j * 128:(j + 1) * 128, oc * 512:(oc + 1) * 512],
                     aost[par][:, :], waits=(e_sq,))
            ev_aost_free[par] = e_w
        ev_wo_free[ws] = last
    e_p4_v = now(sv_)
    e_p4_t = now(st_)
    e_p4_s = now(ss_)
    p4.close()
    sAO.close()

    sHF = ExitStack()
    h_fm = sb(sHF, "h_fm", [128, NFC * OWN], BF16, side="right")
    p5 = ExitStack()
    xp_sb = sb(p5, "xp_sb", [128, D], F32)
    x2_sb = [sb(p5, "x2_sb0", [128, D], F32)]
    ao_rd = sb(p5, "ao_rd", [128, D], F32)
    adj_sb = sb(p5, "adj_sb", [128, D], F32)
    ln2_tm = sb(p5, "ln2_tm", [128, D], BF16)
    p_t5 = [ps(p5, f"p_t5{i}", [128, 512], BF16) for i in range(2)]
    a1 = ExitStack()
    wu1c = [sb(a1, f"wu1c{i}", [128, NFC * 128], BF16) for i in range(2)]
    gg = sb(a1, "gg", [128, 8 * OWN], BF16)
    gb_s = sb(a1, "gb_s", [128, 512], F32)
    x3g = sb(a1, "x3g", [128, 512], F32)
    thg = sb(a1, "thg", [128, 512], F32)
    p_g = [ps(a1, f"p_g{i}", [128, 512]) for i in range(2)]
    p_le = [ps(a1, f"p_le{i}", [1, 512]) for i in range(2)]

    ev_x2_free = [None]
    ev_ao_free = None
    ev_tr5 = {0: None, 1: None}
    ev_hfm = None
    ev_xp_w = None
    last_stats5 = [None, None]
    ev_hfm_half = {0: None, 1: None}
    last_tr5 = None
    prev_xp = None
    prev_ln2 = None
    for j in range(NT):
        sl = j % 2
        stt_t = stats[sl]
        e_x = dy(sx2[0], x2_sb[0][:, :], x_d[j * 128:(j + 1) * 128, :],
                 waits=(ev_x2_free[0], e_p4_t, e_p4_s, e_p4_v))
        e_a = dy(sao, ao_rd[:, :], ao_d[j * 128:(j + 1) * 128, :],
                 waits=(now(sat[0]), now(sat[1]), ev_ao_free, e_p4_t, e_p4_s,
                        e_p4_v))
        v_op(lambda e, j=j, stt_t=stt_t: e.tensor_scalar(
            stt_t[:, 2:3], sums_ao[:, j:j + 1], 1.0 / D, None, ALU.mult),
            waits=(e_p4_v, last_stats5[sl]))
        v_op(lambda e, j=j, stt_t=stt_t: e.tensor_scalar(
            stt_t[:, 3:4], sumsq_ao[:, j:j + 1], 1.0 / D, None, ALU.mult))
        v_op(lambda e, stt_t=stt_t: e.tensor_tensor(stt_t[:, 4:5], stt_t[:, 2:3],
                                                    stt_t[:, 2:3], ALU.mult))
        e_var = v_op(lambda e, stt_t=stt_t: e.tensor_sub(stt_t[:, 5:6], stt_t[:, 3:4],
                                                         stt_t[:, 4:5]), inc=True)
        s_op(lambda e, stt_t=stt_t: e.activation(stt_t[:, 6:7], stt_t[:, 5:6], AF.Ln,
                                                 bias=EPS),
             waits=(e_var, last_stats5[sl]))
        e_rstd = s_op(lambda e, stt_t=stt_t: e.activation(stt_t[:, 7:8], stt_t[:, 6:7],
                                                          AF.Exp, scale=-0.5), inc=True)
        e_nmr = v_op(lambda e, stt_t=stt_t: e.tensor_scalar(
            stt_t[:, 8:9], stt_t[:, 2:3], stt_t[:, 7:8], -1.0, ALU.mult, ALU.mult),
            waits=(e_rstd,), inc=True)
        e_adj = s_op(lambda e, stt_t=stt_t: e.activation(
            adj_sb[:, :], x2_sb[0][:, :], AF.Identity, bias=stt_t[:, 8:9]),
            waits=(e_x, e_nmr, prev_xp), inc=True)
        e_xp = v_op(lambda e, stt_t=stt_t: e.scalar_tensor_tensor(
            xp_sb[:, :], ao_rd[:, :], stt_t[:, 7:8], adj_sb[:, :],
            ALU.mult, ALU.add, accum_out=stt_t[:, 11:12]),
            waits=(e_a, e_adj, prev_ln2, ev_xp_w), inc=True)
        prev_xp = e_xp
        ev_x2_free[0] = e_adj
        ev_ao_free = e_xp
        ev_xp_w = dy(sxp, xp_d[j * 128:(j + 1) * 128, :], xp_sb[:, :], waits=(e_xp,))
        e_sq2 = s_op(lambda e, stt_t=stt_t: e.activation(
            ln2_tm[:, :], xp_sb[:, :], AF.Square, accum_out=stt_t[:, 12:13]),
            waits=(e_xp, last_tr5), inc=True)
        v_op(lambda e, stt_t=stt_t: e.tensor_scalar(stt_t[:, 2:3], stt_t[:, 11:12],
                                                    1.0 / D, None, ALU.mult),
             waits=(e_xp,))
        v_op(lambda e, stt_t=stt_t: e.tensor_scalar(stt_t[:, 3:4], stt_t[:, 12:13],
                                                    1.0 / D, None, ALU.mult),
             waits=(e_sq2,))
        v_op(lambda e, stt_t=stt_t: e.tensor_tensor(stt_t[:, 4:5], stt_t[:, 2:3],
                                                    stt_t[:, 2:3], ALU.mult))
        e_var2 = v_op(lambda e, stt_t=stt_t: e.tensor_sub(stt_t[:, 5:6], stt_t[:, 3:4],
                                                          stt_t[:, 4:5]), inc=True)
        s_op(lambda e, stt_t=stt_t: e.activation(stt_t[:, 6:7], stt_t[:, 5:6], AF.Ln,
                                                 bias=EPS), waits=(e_var2,))
        e_rstd2 = s_op(lambda e, j=j, stt_t=stt_t: e.activation(
            rstd2_k[:, j:j + 1], stt_t[:, 6:7], AF.Exp, scale=-0.5), inc=True)
        e_nmr2 = v_op(lambda e, j=j, stt_t=stt_t: e.tensor_scalar(
            nmr2_k[:, j:j + 1], stt_t[:, 2:3], rstd2_k[:, j:j + 1], -1.0,
            ALU.mult, ALU.mult), waits=(e_rstd2,), inc=True)
        e_ln2 = s_op(lambda e, j=j: e.activation(ln2_tm[:, :], xp_sb[:, :], AF.Identity,
                                                 bias=nmr2_k[:, j:j + 1],
                                                 scale=rstd2_k[:, j:j + 1]),
                     waits=(e_nmr2,), inc=True)
        prev_ln2 = e_ln2
        v_op(lambda e, stt_t=stt_t: e.tensor_scalar(stt_t[:, 13:14], stt_t[:, 5:6],
                                                    EPS, None, ALU.add),
             waits=(e_var2,))
        e_rec = v_op(lambda e, stt_t=stt_t: e.reciprocal(stt_t[:, 14:15],
                                                         stt_t[:, 13:14]), inc=True)
        e_vh2 = v_op(lambda e, stt_t=stt_t: e.tensor_tensor(
            stt_t[:, 15:16], stt_t[:, 5:6], stt_t[:, 14:15], ALU.mult),
            waits=(e_rec,), inc=True)
        e_lv = s_op(lambda e, j=j, stt_t=stt_t: e.activation(
            lv_all[:, j:j + 1], stt_t[:, 15:16], AF.Ln, bias=1.0),
            waits=(e_vh2,), inc=True)
        last_stats5[sl] = e_lv
        hfm3 = h_fm[:, :].rearrange("p (fc t) -> p fc t", fc=NFC)
        for g in range(8):
            par = g % 2
            for k in range(4):
                fc = g * 4 + k
                t_op(lambda e, fc=fc, par=par, k=k: e.transpose(
                        p_t5[par][:, k * 128:(k + 1) * 128],
                        ln2_tm[:, fc * 128:(fc + 1) * 128], id_bf[:, :]),
                     waits=(e_ln2, ev_tr5[par]), inc=(k == 3))
            e_trg = now(st_)
            e_ev = v_op(lambda e, g=g, j=j, par=par, hfm3=hfm3: e.tensor_copy(
                            hfm3[:, g * 4:(g + 1) * 4, j * 128:(j + 1) * 128],
                            p_t5[par][:, :].rearrange("p (k t) -> p k t", k=4)),
                        waits=(e_trg,), inc=True)
            ev_tr5[par] = e_ev
            ev_hfm = e_ev
        if j == 3:
            ev_hfm_half[0] = e_ev
        if j == 7:
            ev_hfm_half[1] = e_ev
        last_tr5 = now(st_)
    e_lv_done = now(ss_)


    wu1_r = Wu1T_d.rearrange("(fc p) n -> p fc n", p=128)
    ev_wu_free = [None, None]
    ev_g_free = [None, None]
    prev_ggv = None
    prev_x3 = None
    for uc in range(8):
        ws = uc % 2
        e_wu = dg_split(swu2[ws], wu1c[ws][:, :].rearrange("p (c m) -> p c m", c=NFC),
                  wu1_r[:, :, uc * 128:(uc + 1) * 128],
                  waits=(ev_wu_free[ws], e_p4_t, e_p4_s, e_p4_v,
                         now(sat[0]), now(sat[1])))
        last = None
        for half in range(2):
            par = half
            for fc in range(NFC):
                t_op(lambda e, fc=fc, par=par, ws=ws, half=half: e.matmul(
                        p_g[par][:, :], wu1c[ws][:, fc * 128:(fc + 1) * 128],
                        h_fm[:, fc * OWN + half * 512: fc * OWN + half * 512 + 512],
                        start=(fc == 0), stop=(fc == NFC - 1)),
                     waits=(e_wu, ev_hfm_half[half], ev_g_free[par]),
                     inc=(fc == NFC - 1))
            e_mm = now(st_)
            last = e_mm
            e_gb = s_op(lambda e, par=par, uc=uc: e.activation(
                gb_s[:, :], p_g[par][:, :], AF.Identity, bias=bu1c[:, uc:uc + 1]),
                waits=(e_mm, e_cst, prev_ggv), inc=True)
            ev_g_free[par] = e_gb
            e_sqg = s_op(lambda e: e.activation(x3g[:, :], gb_s[:, :], AF.Square),
                         waits=(prev_x3,), inc=True)
            e_x3 = v_op(lambda e: e.tensor_mul(x3g[:, :], x3g[:, :], gb_s[:, :]),
                        waits=(e_sqg,), inc=True)
            prev_x3 = e_x3
            e_ti = v_op(lambda e: e.scalar_tensor_tensor(thg[:, :], x3g[:, :], GC2,
                                                         gb_s[:, :], ALU.mult, ALU.add),
                        waits=(e_x3,), inc=True)
            e_th = s_op(lambda e: e.activation(thg[:, :], thg[:, :], AF.Tanh, scale=GC1),
                        waits=(e_ti,), inc=True)
            e_t5v = v_op(lambda e: e.tensor_scalar(thg[:, :], thg[:, :], 0.5, 0.5,
                                                   ALU.mult, ALU.add),
                         waits=(e_th,), inc=True)
            e_ggv = v_op(lambda e, uc=uc, half=half: e.tensor_mul(
                            gg[:, uc * OWN + half * 512: uc * OWN + half * 512 + 512],
                            thg[:, :], gb_s[:, :]),
                         waits=(e_t5v,), inc=True)
            prev_ggv = e_ggv
            t_op(lambda e, uc=uc, half=half: e.matmul(
                    p_le[half][:, :], wu2c[:, uc:uc + 1],
                    gg[:, uc * OWN + half * 512: uc * OWN + half * 512 + 512],
                    start=(uc == 0), stop=(uc == 7)),
                 waits=(e_ggv, e_wu2), inc=(uc == 7))
        ev_wu_free[ws] = last
    e_led = now(st_)
    for half in range(2):
        s_op(lambda e, half=half: e.activation(
            sigrow[0:1, half * 512:(half + 1) * 512], p_le[half][0:1, :],
            AF.Sigmoid, bias=bu2_sb[0:1, :]),
            waits=(e_led, e_cst), inc=(half == 1))
    e_sig = now(ss_)

    with nc.psum_tensor("p_lvT", [1, 128], F32) as p_lvT:
        e_lvc = None
        for j in range(NT):
            t_op(lambda e, j=j: e.transpose(p_lvT[:, :], lv_all[:, j:j + 1],
                                            id_f32[:, :]),
                 waits=(e_lv_done, e_lvc), inc=True)
            e_lvc = v_op(lambda e, j=j: e.tensor_copy(
                lvrow[0:1, j * 128:(j + 1) * 128], p_lvT[0:1, :]),
                waits=(now(st_),), inc=True)
    v_op(lambda e: e.memset(arsb[0:1, :], 0.0), waits=(e_lvc,))
    e_sumlv = v_op(lambda e: e.reduce_sum(arsb[0:1, 0:1], lvrow[0:1, :], axis=AX.X),
                   inc=True)
    dy(sar, arin_d, arsb[0:1, :], waits=(e_sumlv,))

    e_a1_t = now(st_)
    e_a1_v = now(sv_)
    e_p5_t = e_a1_t
    e_p5_s = now(ss_)
    e_p5_v = e_a1_v
    a1.close()
    p5.close()

    sA2 = ExitStack()
    wT = sb(sA2, "wT", [128, 8 * OWN], BF16)
    w_bf = sb(sA2, "w_bf", [128, 1024], BF16)
    srf = sb(sA2, "srf", [128, 1024], F32)
    mask_u8 = sb(sA2, "mask_u8", [128, 1024], mybir.dt.uint8)
    alm = sb(sA2, "alm", [128, 1024], F32)
    top8 = sb(sA2, "top8", [128, 8], F32)
    a2p = ExitStack()
    p_qy = ps(a2p, "p_qy", [64, 512])
    p_rq = ps(a2p, "p_rq", [48, 512])
    p_sr = ps(a2p, "p_sr", [128, 1024])
    p_al = ps(a2p, "p_al", [128, 1024])
    p_wt = [ps(a2p, f"p_wt{i}", [128, 512], BF16) for i in range(2)]

    e_qy_ev = None
    e_rq_ev = None
    ev_sr_free = None
    ev_al_free = None
    ev_wt_war = {0: None, 1: None}
    prev_wx = None
    prev_sel = None
    last_wtr = None
    for half in range(2):
        for fc in range(NFC):
            t_op(lambda e, fc=fc, half=half: e.matmul(
                    p_qy[:, :], WqT[:, fc * 64:(fc + 1) * 64],
                    h_fm[:, fc * OWN + half * 512: fc * OWN + half * 512 + 512],
                    start=(fc == 0), stop=(fc == NFC - 1)),
                 waits=(e_cst, ev_hfm, e_qy_ev, e_sig, e_a1_v), inc=(fc == NFC - 1))
        e_qmm = now(st_)
        e_qy_ev = s_op(lambda e, half=half: e.activation(
            q_aug[0:64, half * 512:(half + 1) * 512], p_qy[:, :], AF.Copy),
            waits=(e_qmm, e_ms1), inc=True)
        t_op(lambda e, half=half: e.matmul(p_rq[:, :], WrT[:, :],
                                           q_aug[0:64, half * 512:(half + 1) * 512],
                                           start=True, stop=True),
             waits=(e_qy_ev, e_rq_ev), inc=True)
        e_rqm = now(st_)
        e_rq_ev = s_op(lambda e, half=half: e.activation(
            rq_aug[0:48, half * 512:(half + 1) * 512], p_rq[:, :], AF.Copy),
            waits=(e_rqm,), inc=True)
        for jj in range(4):
            j = half * 4 + jj
            for hh in range(2):
                t_op(lambda e, jj=jj, half=half, hh=hh: e.matmul(
                        p_sr[:, hh * 512:(hh + 1) * 512],
                        rq_aug[0:49, half * 512 + jj * 128: half * 512 + jj * 128 + 128],
                        rkT_aug[0:49, hh * 512:(hh + 1) * 512],
                        start=True, stop=True),
                     waits=(e_rq_ev, e_rk, ev_sr_free), inc=(hh == 1))
            e_srm = now(st_)
            e_srf = s_op(lambda e: e.activation(srf[:, :], p_sr[:, :], AF.Copy),
                         waits=(e_srm, prev_sel), inc=True)
            ev_sr_free = e_srf
            e_t8 = v_op(lambda e: e.max(out=top8[:, :], in_=srf[:, :]),
                        waits=(e_srf,), inc=True)
            e_msk = v_op(lambda e: e.tensor_scalar(mask_u8[:, :], srf[:, :],
                                                   top8[:, 7:8], None, ALU.is_ge),
                         waits=(e_t8,), inc=True)
            for hh in range(2):
                t_op(lambda e, jj=jj, half=half, hh=hh: e.matmul(
                        p_al[:, hh * 512:(hh + 1) * 512],
                        q_aug[0:65, half * 512 + jj * 128: half * 512 + jj * 128 + 128],
                        akT_aug[0:65, hh * 512:(hh + 1) * 512],
                        start=True, stop=True),
                     waits=(e_qy_ev, e_ak, ev_al_free), inc=(hh == 1))
            e_almm = now(st_)
            e_sel = v_op(lambda e: e.select(alm[:, :], mask_u8[:, :], p_al[:, :],
                                            negbig[:, :].to_broadcast([128, 1024])),
                         waits=(e_almm, e_msk, e_ms2, prev_wx), inc=True)
            prev_sel = e_sel
            ev_al_free = e_sel
            e_nm = v_op(lambda e: e.reduce_max(stats[0][:, 9:10], alm[:, :], axis=AX.X,
                                               negate=True),
                        waits=(e_sel,), inc=True)
            e_wx = s_op(lambda e, j=j: e.activation(w_bf[:, :], alm[:, :], AF.Exp,
                                                    bias=stats[0][:, 9:10],
                                                    accum_out=lffn_all[:, j:j + 1]),
                        waits=(e_nm, last_wtr), inc=True)
            prev_wx = e_wx
            wt3 = wT[:, :].rearrange("p (sc t) -> p sc t", sc=8)
            for g in range(2):
                par = g
                for k in range(4):
                    sc = g * 4 + k
                    t_op(lambda e, sc=sc, par=par, k=k: e.transpose(
                            p_wt[par][:, k * 128:(k + 1) * 128],
                            w_bf[:, sc * 128:(sc + 1) * 128], id_bf[:, :]),
                         waits=(e_wx, ev_wt_war[par]), inc=(k == 3))
                e_trg = now(st_)
                e_evw = v_op(lambda e, g=g, j=j, par=par, wt3=wt3: e.tensor_copy(
                                wt3[:, g * 4:(g + 1) * 4, j * 128:(j + 1) * 128],
                                p_wt[par][:, :].rearrange("p (k t) -> p k t", k=4)),
                             waits=(e_trg,), inc=True)
                ev_wt_war[par] = e_evw
            last_wtr = now(st_)
    e_wT_done = now(sv_)
    e_a2_s = now(ss_)
    e_a2_t = now(st_)
    a2p.close()
    sHF.close()

    sKeep = ExitStack()
    WvuT = sb(sKeep, "WvuT", [128, 2 * D], BF16, side="right")
    a2_all = sb(sKeep, "a2_all", [128, 2048], BF16, side="right")
    a3 = ExitStack()
    av_sb = sb(a3, "av_sb", [128, 8 * D], BF16)
    WvdT = sb(a3, "WvdT", [128, NFC * 256], BF16)
    aux1 = sb(a3, "aux1", [128, NFC * 512], BF16)
    a3p = ExitStack()
    p_x1 = [ps(a3p, f"p_x1{i}", [128, 512]) for i in range(2)]
    p_a2 = [ps(a3p, f"p_a2{i}", [128, 512]) for i in range(2)]

    av_r = av_d.rearrange("(sc p) n -> p sc n", p=128)
    for sc in range(8):
        dg(sav, av_sb[:, sc * D:(sc + 1) * D], av_r[:, sc, :],
           waits=(e_a2_t, e_a2_s, e_wT_done))
    e_wvd = dg_split(swvd, WvdT[:, :].rearrange("p (c m) -> p c m", c=NFC),
               WvdT_d.rearrange("(c p) n -> p c n", p=128))
    e_wvu = dg(swvu, WvuT[:, :].rearrange("p (c m) -> p c m", c=2),
               WvuT_d.rearrange("(c p) n -> p c n", p=128))

    ev_x1_free = [None, None]
    ev_a2_free = [None, None]
    for half in range(2):
        for hc in range(NFC):
            par = hc % 2
            for sc in range(8):
                t_op(lambda e, hc=hc, sc=sc, half=half, par=par: e.matmul(
                        p_x1[par][:, :],
                        av_sb[:, sc * D + hc * 128: sc * D + hc * 128 + 128],
                        wT[:, sc * OWN + half * 512: sc * OWN + half * 512 + 512],
                        start=(sc == 0), stop=(sc == 7)),
                     waits=(now(sav), e_wT_done, ev_x1_free[par]), inc=(sc == 7))
            e_mm = now(st_)
            e_ev = s_op(lambda e, hc=hc, par=par: e.activation(
                            aux1[:, hc * 512:(hc + 1) * 512], p_x1[par][:, :], AF.Copy),
                        waits=(e_mm,), inc=True)
            ev_x1_free[par] = e_ev
        e_aux1 = now(ss_)
        for vbc in range(2):
            par = vbc
            for hc in range(NFC):
                t_op(lambda e, vbc=vbc, hc=hc, par=par: e.matmul(
                        p_a2[par][:, :],
                        WvdT[:, hc * 256 + vbc * 128: hc * 256 + vbc * 128 + 128],
                        aux1[:, hc * 512:(hc + 1) * 512],
                        start=(hc == 0), stop=(hc == NFC - 1)),
                     waits=(e_wvd, e_aux1, ev_a2_free[par]), inc=(hc == NFC - 1))
            e_mm = now(st_)
            e_ev = s_op(lambda e, half=half, vbc=vbc, par=par: e.activation(
                            a2_all[:, half * 1024 + vbc * 512:
                                   half * 1024 + vbc * 512 + 512],
                            p_a2[par][:, :], AF.Copy),
                        waits=(e_mm,), inc=True)
            ev_a2_free[par] = e_ev
    e_a2_done = now(ss_)
    e_a3_t = now(st_)
    a3p.close()
    a3.close()
    sA2.close()

    P.op("g", lambda e: e.collective_compute(
        "AllReduce", ALU.add, replica_groups=[list(range(8))],
        ins=[arin_d.opt()], outs=[arout_d.opt()]),
        waits=(now(sar),), inc=cc_)
    e_arr = now(cc_)
    e_arl = dg(sar, arsb[0:1, :], arout_d, waits=(e_arr,))
    v_op(lambda e: e.tensor_scalar(growA[0:1, 0:1], arsb[0:1, 0:1], 0.008192, None,
                                   ALU.add), waits=(e_arl,))
    v_op(lambda e: e.reciprocal(growA[0:1, 1:2], growA[0:1, 0:1]))
    e_hi = v_op(lambda e: e.tensor_scalar(growA[0:1, 2:3], growA[0:1, 1:2], 4096.0,
                                          None, ALU.mult), inc=True)
    v_op(lambda e: e.tensor_scalar(growB[0:1, :], sigrow[0:1, :], 2.5, None, ALU.mult),
         waits=(e_sig,))
    v_op(lambda e: e.scalar_tensor_tensor(growA[0:1, :], lvrow[0:1, :], growA[0:1, 2:3],
                                          growB[0:1, :], ALU.mult, ALU.add),
         waits=(e_hi, e_sumlv))
    v_op(lambda e: e.tensor_scalar_max(growA[0:1, :], growA[0:1, :], 0.0))
    v_op(lambda e: e.tensor_scalar_min(growA[0:1, :], growA[0:1, :], 5.0))
    v_op(lambda e: e.tensor_scalar(growA[0:1, :], growA[0:1, :], 2.0 / 3.0, -1.0 / 3.0,
                                   ALU.mult, ALU.add))
    v_op(lambda e: e.tensor_scalar_max(growA[0:1, :], growA[0:1, :], 0.0))
    v_op(lambda e: e.tensor_scalar_min(growA[0:1, :], growA[0:1, :], 1.0))
    e_tg = v_op(lambda e: e.tensor_scalar(growA[0:1, :], growA[0:1, :], gw1_sb[0:1, :],
                                          gb_sb[0:1, :], ALU.mult, ALU.add),
                waits=(e_cst,), inc=True)
    e_gt = s_op(lambda e: e.activation(growB[0:1, :], growA[0:1, :], AF.Sigmoid),
                waits=(e_tg, e_a2_done), inc=True)
    v_op(lambda e: e.tensor_scalar(growA[0:1, :], growB[0:1, :], 0.05, None, ALU.is_ge),
         waits=(e_gt,))
    e_gate = v_op(lambda e: e.tensor_mul(growB[0:1, :], growB[0:1, :], growA[0:1, :]),
                  inc=True)
    with nc.psum_tensor("p_gT", [128, 1], F32) as p_gT:
        ev_gTc = None
        for j in range(NT):
            t_op(lambda e, j=j: e.transpose(p_gT[:, :],
                                            growB[0:1, j * 128:(j + 1) * 128],
                                            id_f32[0:1, 0:1]),
                 waits=(e_gate, ev_gTc, e_a3_t, e_a2_done), inc=True)
            e_gtt = now(st_)
            e_gc = v_op(lambda e: e.tensor_copy(gcol[:, :], p_gT[:, :]),
                        waits=(e_gtt,), inc=True)
            ev_gTc = e_gc
            v_op(lambda e, j=j: e.reciprocal(rcol[:, :], lffn_all[:, j:j + 1]),
                 waits=(e_a2_s,))
            v_op(lambda e, j=j: e.tensor_tensor(gs_all[:, j:j + 1], gcol[:, :],
                                                rcol[:, :], ALU.mult))
    e_gs = now(sv_)

    a4 = ExitStack()
    xp2 = [sb(a4, f"xp2{i}", [128, D], F32) for i in range(2)]
    hh_sb = sb(a4, "hh_sb", [128, D], F32)
    ffn_sb = sb(a4, "ffn_sb", [128, D], F32)
    adj2 = sb(a4, "adj2", [128, D], F32)
    outsb = [sb(a4, f"outsb{i}", [128, D], F32) for i in range(2)]
    sink2 = sb(a4, "sink2", [128, D], BF16)
    a4p = ExitStack()
    p_ao4 = [ps(a4p, f"p_ao4{i}", [128, 512]) for i in range(2)]

    ev_xp2_free = [None, None]
    ev_out_free = [None, None]
    ev_ao4_free = [None, None]
    prev_hh_users = None
    prev_sq3 = None
    prev_fin = None
    last_stats4 = [None, None]
    for j in range(NT):
        sl = j % 2
        half = j // 4
        jj = j % 4
        stt_t = stats[sl]
        e_xl = dy(sxr2[sl], xp2[sl][:, :], xp_d[j * 128:(j + 1) * 128, :],
                  waits=(now(sxp), ev_xp2_free[sl], e_a3_t, e_a2_done, e_gs))
        e_hh = s_op(lambda e, sl=sl, j=j: e.activation(
            hh_sb[:, :], xp2[sl][:, :], AF.Identity, bias=nmr2_k[:, j:j + 1],
            scale=rstd2_k[:, j:j + 1]),
            waits=(e_xl, e_p5_v, prev_hh_users), inc=True)
        cmb_evs = []
        for hc in range(8):
            par = hc % 2
            for vbc in range(2):
                t_op(lambda e, half=half, vbc=vbc, hc=hc, par=par, jj=jj: e.matmul(
                        p_ao4[par][:, :],
                        a2_all[:, half * 1024 + vbc * 512 + jj * 128:
                               half * 1024 + vbc * 512 + jj * 128 + 128],
                        WvuT[:, vbc * D + hc * 512: vbc * D + hc * 512 + 512],
                        start=(vbc == 0), stop=(vbc == 1)),
                     waits=(e_a2_done, e_wvu, ev_ao4_free[par], e_gs),
                     inc=(vbc == 1))
            e_mm = now(st_)
            e_cmb = v_op(lambda e, hc=hc, par=par, j=j, stt_t=stt_t: e.scalar_tensor_tensor(
                            ffn_sb[:, hc * 512:(hc + 1) * 512], p_ao4[par][:, :],
                            gs_all[:, j:j + 1], hh_sb[:, hc * 512:(hc + 1) * 512],
                            ALU.mult, ALU.add,
                            accum_out=stt_t[:, 9 + par:10 + par]),
                         waits=(e_mm, e_hh, e_gs, prev_sq3, last_stats4[sl]),
                         inc=True)
            ev_ao4_free[par] = e_cmb
            cmb_evs.append(e_cmb)
            if hc == 0:
                v_op(lambda e, stt_t=stt_t: e.tensor_copy(stt_t[:, 1:2],
                                                          stt_t[:, 9:10]))
            else:
                v_op(lambda e, par=par, stt_t=stt_t: e.tensor_add(
                    stt_t[:, 1:2], stt_t[:, 1:2], stt_t[:, 9 + par:10 + par]))
        e_ffull = now(sv_)
        e_sq3 = s_op(lambda e, stt_t=stt_t: e.activation(
            sink2[:, :], ffn_sb[:, :], AF.Square, accum_out=stt_t[:, 2:3]),
            waits=(e_ffull,), inc=True)
        prev_sq3 = e_sq3
        v_op(lambda e, stt_t=stt_t: e.tensor_scalar(stt_t[:, 3:4], stt_t[:, 1:2],
                                                    1.0 / D, None, ALU.mult))
        v_op(lambda e, stt_t=stt_t: e.tensor_scalar(stt_t[:, 4:5], stt_t[:, 2:3],
                                                    1.0 / D, None, ALU.mult),
             waits=(e_sq3,))
        v_op(lambda e, stt_t=stt_t: e.tensor_tensor(stt_t[:, 5:6], stt_t[:, 3:4],
                                                    stt_t[:, 3:4], ALU.mult))
        e_var3 = v_op(lambda e, stt_t=stt_t: e.tensor_sub(stt_t[:, 6:7], stt_t[:, 4:5],
                                                          stt_t[:, 5:6]), inc=True)
        s_op(lambda e, stt_t=stt_t: e.activation(stt_t[:, 7:8], stt_t[:, 6:7], AF.Ln,
                                                 bias=EPS), waits=(e_var3,))
        e_rstd3 = s_op(lambda e, stt_t=stt_t: e.activation(stt_t[:, 7:8], stt_t[:, 7:8],
                                                           AF.Exp, scale=-0.5), inc=True)
        e_nmr3 = v_op(lambda e, stt_t=stt_t: e.tensor_scalar(
            stt_t[:, 8:9], stt_t[:, 3:4], stt_t[:, 7:8], -1.0, ALU.mult, ALU.mult),
            waits=(e_rstd3,), inc=True)
        e_adj3 = s_op(lambda e, sl=sl, stt_t=stt_t: e.activation(
            adj2[:, :], xp2[sl][:, :], AF.Identity, bias=stt_t[:, 8:9]),
            waits=(e_nmr3, prev_fin), inc=True)
        ev_xp2_free[sl] = e_adj3
        e_fin = v_op(lambda e, sl=sl, stt_t=stt_t: e.scalar_tensor_tensor(
            outsb[sl][:, :], ffn_sb[:, :], stt_t[:, 7:8], adj2[:, :],
            ALU.mult, ALU.add),
            waits=(e_adj3, ev_out_free[sl]), inc=True)
        prev_fin = e_fin
        prev_hh_users = e_fin
        last_stats4[sl] = e_fin
        e_ow = dy(soo[sl], out_d[j * 128:(j + 1) * 128, :], outsb[sl][:, :],
                  waits=(e_fin,))
        ev_out_free[sl] = e_ow
    if debug:
        dq = nc.dram_tensor("dbg_qspill", [D, OWN], BF16, kind="ExternalOutput").ap()
        dk = nc.dram_tensor("dbg_kvin", [KVROWS, OWN], BF16, kind="ExternalOutput").ap()
        da = nc.dram_tensor("dbg_ao", [OWN, D], F32, kind="ExternalOutput").ap()
        dx = nc.dram_tensor("dbg_xp", [OWN, D], F32, kind="ExternalOutput").ap()
        dr = nc.dram_tensor("dbg_ar", [2, 16], F32, kind="ExternalOutput").ap()
        dy(sout, dq, qspill_d, waits=(now(soo[0]), now(soo[1]),))
        dy(sout, dk, kvin_d)
        dy(sout, da, ao_d)
        dy(sout, dx, xp_d)
        dy(sout, dr[0:1, :], arin_d)
        dy(sout, dr[1:2, :], arout_d)
        ds = nc.dram_tensor("dbg_stats", [128, 64], F32, kind="ExternalOutput").ap()
        dy(sout, ds[:, 0:8], sums_ao[:, :])
        dy(sout, ds[:, 8:16], sumsq_ao[:, :])
        dy(sout, ds[:, 16:24], lv_all[:, :])
        dy(sout, ds[:, 24:32], rstd2_k[:, :])
        dy(sout, ds[:, 32:40], nmr2_k[:, :])
        dy(sout, ds[:, 40:48], lffn_all[:, :])
        dy(sout, ds[:, 48:56], gs_all[:, :])
        dy(sout, ds[:, 56:64], stats[0][:, 0:8])

    P.op("y", lambda e: e.nop(), waits=(now(soo[0]), now(soo[1]), now(sout)))
    a4p.close()
    a4.close()
    sKeep.close()

    with nc.Block() as block:
        P.emit(nc, block)
    ctx.close()
    return nc


_NC_CACHE = [None]


def prepare_in_maps(L0_attn_x, L0_attn_wq, L0_attn_wk, L0_attn_wv, L0_attn_wo,
                    L0_ffn_Wq, L0_ffn_Wr, L0_ffn_aux_keys, L0_ffn_aux_values,
                    L0_ffn_Wvd, L0_ffn_Wvu, L0_ffn_Wu1, L0_ffn_bu1, L0_ffn_Wu2,
                    L0_ffn_bu2, L0_ffn_gate_w1, L0_ffn_gate_bias, L0_ffn_log_rel):
    f = np.float32
    x = np.asarray(L0_attn_x, f)
    shared = dict(
        i_wq=np.ascontiguousarray(np.asarray(L0_attn_wq, f)),
        i_wk=np.ascontiguousarray(np.asarray(L0_attn_wk, f)),
        i_wv=np.ascontiguousarray(np.asarray(L0_attn_wv, f)),
        i_wo=np.ascontiguousarray(np.asarray(L0_attn_wo, f)),
        i_fWqT=np.ascontiguousarray(np.asarray(L0_ffn_Wq, f).T),
        i_fWrT=np.ascontiguousarray(np.asarray(L0_ffn_Wr, f).T),
        i_akT=np.ascontiguousarray(np.asarray(L0_ffn_aux_keys, f).T),
        i_av=np.ascontiguousarray(np.asarray(L0_ffn_aux_values, f)),
        i_WvdT=np.ascontiguousarray(np.asarray(L0_ffn_Wvd, f).T),
        i_WvuT=np.ascontiguousarray(np.asarray(L0_ffn_Wvu, f).T),
        i_Wu1T=np.ascontiguousarray(np.asarray(L0_ffn_Wu1, f).T),
        i_bu1c=np.ascontiguousarray(np.asarray(L0_ffn_bu1, f).reshape(8, 128).T),
        i_wu2c=np.ascontiguousarray(np.asarray(L0_ffn_Wu2, f).reshape(8, 128).T),
        i_bu2=np.asarray(L0_ffn_bu2, f).reshape(1, 1),
        i_gw1=np.asarray(L0_ffn_gate_w1, f).reshape(1, 1),
        i_gb=np.asarray(L0_ffn_gate_bias, f).reshape(1, 1),
        i_lrel=np.ascontiguousarray(np.asarray(L0_ffn_log_rel, f).reshape(1, 1024)),
        c_scal=np.ascontiguousarray(np.tile(np.array(
            [EPS, 1.0, -0.5, 0.125, 1.0 / math.sqrt(48.0), ISQHD, GC1, 0.0],
            dtype=np.float32), (128, 1))),
        c_idbf=np.eye(128, dtype=ml_dtypes.bfloat16),
        c_idf32=np.eye(128, dtype=np.float32),
        c_ones=np.ones((128, 128), dtype=ml_dtypes.bfloat16),
        c_rot=_rot_matrix(),
    )
    in_maps = []
    idx_by_r = {}
    for c in range(8):
        b, r = c // 4, c % 4
        if r not in idx_by_r:
            j = np.arange(NT)[:, None]
            s = np.arange(TS)[None, :]
            idx_by_r[r] = (512 * j + 4 * s + r).reshape(-1)
        idx = idx_by_r[r]
        cos, sin, dm = _host_tables(r)
        m = dict(shared)
        m["x_own"] = np.ascontiguousarray(x[b][idx])
        m["c_cos"] = cos
        m["c_sin"] = sin
        m["c_dmask"] = dm
        in_maps.append(m)
    return in_maps, idx_by_r


def kernel(**inputs):
    in_maps, idx_by_r = prepare_in_maps(**inputs)
    if _NC_CACHE[0] is None:
        _NC_CACHE[0] = build_program()
    nc = _NC_CACHE[0]
    res = run_bass_kernel_spmd(nc, in_maps, core_ids=list(range(8)))
    out = np.empty((B, T, D), dtype=np.float32)
    for c in range(8):
        b, r = c // 4, c % 4
        out[b][idx_by_r[r]] = np.asarray(res.results[c]["o_out"], np.float32)
    return out


# version 10
# speedup vs baseline: 1.5970x; 1.0388x over previous
import math
import numpy as np
import ml_dtypes
from contextlib import ExitStack

import concourse.bass as bass
import concourse.mybir as mybir
from concourse.bass_utils import run_bass_kernel_spmd

F32 = mybir.dt.float32
BF16 = mybir.dt.bfloat16
AF = mybir.ActivationFunctionType
ALU = mybir.AluOpType
AX = mybir.AxisListType

B, T, D = 2, 4096, 4096
H, KVH, HD = 32, 8, 128
NT, TS = 8, 128
OWN = NT * TS
NFC = D // 128
EPS = 1e-5
ISQHD = 1.0 / math.sqrt(HD)
NBOUND = -1.0 / HD
GC1 = 0.7978845608028654
GC2 = 0.044715
KVROWS = 2052
VST = 132


def _inv_freq():
    THETA, FACTOR, HI_FF, LO_FF, ORIG_MAX = 500000.0, 8.0, 4.0, 1.0, 8192
    inv = 1.0 / THETA ** (np.arange(0, HD, 2, dtype=np.float64) / HD)
    wavelen = 2.0 * np.pi / inv
    low_wl = ORIG_MAX / LO_FF
    high_wl = ORIG_MAX / HI_FF
    smooth = (ORIG_MAX / wavelen - LO_FF) / (HI_FF - LO_FF)
    scaled = np.where(wavelen > low_wl, inv / FACTOR, inv)
    mid = (wavelen <= low_wl) & (wavelen >= high_wl)
    scaled = np.where(mid, (1 - smooth) * inv / FACTOR + smooth * inv, scaled)
    return scaled.astype(np.float32)


def _host_tables(r):
    inv = _inv_freq()
    j = np.arange(NT)[:, None]
    s = np.arange(TS)[None, :]
    pos = (512 * j + 4 * s + r).reshape(-1).astype(np.float32)
    fr = pos[None, :] * inv[:, None]
    cos = np.concatenate([np.cos(fr), np.cos(fr)], axis=0).astype(np.float32)
    sin = np.concatenate([np.sin(fr), np.sin(fr)], axis=0).astype(np.float32)
    sp = np.arange(TS)[:, None]
    dm = np.zeros((TS, 4 * TS), dtype=np.float32)
    for rr in range(4):
        dm[:, rr * TS:(rr + 1) * TS] = (sp < s) | ((sp == s) & (rr <= r))
    return cos, sin, dm.astype(ml_dtypes.bfloat16)


def _rot_matrix():
    R = np.zeros((HD, HD), dtype=np.float32)
    for dp in range(64):
        R[dp + 64, dp] = -1.0
    for dp in range(64, 128):
        R[dp - 64, dp] = 1.0
    return R.astype(ml_dtypes.bfloat16)


class Stream:
    __slots__ = ("h", "step", "n", "name")

    def __init__(self, h, step, name):
        self.h, self.step, self.n, self.name = h, step, 0, name


class Prog:
    def __init__(self):
        self.q = {e: [] for e in "tvsyg"}
        self.hw = {e: {} for e in "tvsyg"}
        self.all_sems = []
        self.boot = None
        self.chain = {}

    def op(self, e, fn, waits=(), inc=None, step=None):
        ch = self.chain.get(e)
        if ch is not None:
            waits = tuple(waits) + ((ch, ch.n),)
            if inc is None:
                inc = ch
        ws = []
        for w in waits:
            if w is None:
                continue
            sem, val = w
            if self.hw[e].get(sem, 0) < val:
                self.hw[e][sem] = val
                ws.append((sem, val))
        st = step if step is not None else (inc.step if inc is not None else 0)
        ev = None
        if inc is not None:
            inc.n += st
            ev = (inc, inc.n)
        self.q[e].append((ws, fn, inc, st))
        return ev

    def emit(self, nc, block):
        def mk(e):
            def body(eng):
                for (ws, fn, inc, st) in self.q[e]:
                    for (sem, val) in ws:
                        eng.wait_ge(sem.h, val)
                    ins = fn(eng)
                    if inc is not None:
                        ins.then_inc(inc.h, st)
            return body
        block.tensor(mk("t"))
        block.vector(mk("v"))
        block.scalar(mk("s"))
        block.sync(mk("y"))
        block.gpsimd(mk("g"))


def build_program(debug=False, stop_after=None):
    nc = bass.Bass("TRN2", target_bir_lowering=False)

    def din(name, shape, dt=F32):
        return nc.dram_tensor(name, shape, dt, kind="ExternalInput").ap()

    x_d = din("x_own", [OWN, D])
    wq_d = din("i_wq", [D, D])
    wk_d = din("i_wk", [D, KVH * HD])
    wv_d = din("i_wv", [D, KVH * HD])
    wo_d = din("i_wo", [D, D])
    fWqT_d = din("i_fWqT", [D, 64])
    fWrT_d = din("i_fWrT", [64, 48])
    akT_d = din("i_akT", [64, 1024])
    av_d = din("i_av", [1024, D])
    WvdT_d = din("i_WvdT", [D, 256])
    WvuT_d = din("i_WvuT", [256, D])
    Wu1T_d = din("i_Wu1T", [D, 1024])
    bu1_d = din("i_bu1c", [128, 8])
    wu2c_d = din("i_wu2c", [128, 8])
    bu2_d = din("i_bu2", [1, 1])
    gw1_d = din("i_gw1", [1, 1])
    gb_d = din("i_gb", [1, 1])
    lrel_d = din("i_lrel", [1, 1024])
    cos_d = din("c_cos", [HD, OWN])
    sin_d = din("c_sin", [HD, OWN])
    id_bf_d = din("c_idbf", [128, 128], BF16)
    id_f32_d = din("c_idf32", [128, 128], F32)
    ones_bf_d = din("c_ones", [128, 128], BF16)
    rot_d = din("c_rot", [128, 128], BF16)
    dmask_d = din("c_dmask", [128, 512], BF16)
    cscal_d = din("c_scal", [128, 8], F32)
    out_d = nc.dram_tensor("o_out", [OWN, D], F32, kind="ExternalOutput").ap()

    qspill_d = nc.dram_tensor("z_qspill", [D, OWN], BF16).ap()
    kvin_d = nc.dram_tensor("z_kvin", [KVROWS, OWN], BF16).ap()
    kvg_d = nc.dram_tensor("z_kvg", [4 * KVROWS, OWN], BF16).ap()
    ao_d = nc.dram_tensor("z_ao", [OWN, D], F32).ap()
    xp_d = nc.dram_tensor("z_xp", [OWN, D], F32).ap()
    arin_d = nc.dram_tensor("z_arin", [1, 16], F32).ap()
    arout_d = nc.dram_tensor("z_arout", [1, 16], F32).ap()

    P = Prog()
    ctx = ExitStack()

    def sem(name, step):
        s = Stream(ctx.enter_context(nc.semaphore(name)), step, name)
        P.all_sems.append(s)
        return s

    def sem2(name):
        return [sem(name + "0", 16), sem(name + "1", 16)]

    sx2 = sem2("sx"); swk2 = sem2("swk"); swv = sem("swv", 16)
    swq4 = [sem(f"swq{i}", 16) for i in range(4)]; skv = sem("skv", 16)
    skh2 = sem2("skh"); svh2 = sem2("svh"); sqr2 = sem2("sqr")
    sao = sem("sao", 16); sxp = sem("sxp", 16); sxr2 = sem2("sxr")
    swo2 = sem2("swo"); sav = sem("sav", 16); swu2 = sem2("swu")
    scst = sem("scst", 16); skm = sem("skm", 16)
    swvd = sem("swvd", 16); swvu = sem("swvu", 16)
    sout = sem("sout", 16); sar = sem("sar", 16)
    ssp = sem2("ssp"); svb = sem2("svb"); sat = sem2("sat"); soo = sem2("soo")
    st_ = sem("st", 1); sv_ = sem("sv", 1); ss_ = sem("ss", 1); cc_ = sem("cc", 1)
    P.chain = {"v": sv_, "s": ss_}

    def sb(stack, name, shape, dt, side=None):
        kw = {"side": side} if side else {}
        return stack.enter_context(nc.sbuf_tensor(name, shape, dt, **kw))

    def ps(stack, name, shape, dt=F32):
        return stack.enter_context(nc.psum_tensor(name, shape, dt))

    def dy(stream, out, in_, waits=(), **kw):
        return P.op("y", lambda e: e.dma_start(out, in_, **kw), waits, inc=stream)

    def dg(stream, out, in_, waits=(), **kw):
        return P.op("g", lambda e: e.dma_start(out, in_, **kw), waits, inc=stream)

    def dg_split(stream, out3, in3, waits=(), parts=2):
        C = out3.shape[1]
        step = C // parts
        ev = None
        for i in range(parts):
            ev = dg(stream, out3[:, i * step:(i + 1) * step, :],
                    in3[:, i * step:(i + 1) * step, :], waits=waits)
        return ev

    def t_op(fn, waits=(), inc=False):
        return P.op("t", fn, waits, inc=st_ if inc else None)

    def v_op(fn, waits=(), inc=False):
        return P.op("v", fn, waits, inc=sv_ if inc else None)

    def s_op(fn, waits=(), inc=False):
        return P.op("s", fn, waits, inc=ss_ if inc else None)

    def now(s):
        return (s, s.n)

    id_bf = sb(ctx, "id_bf", [128, 128], BF16)
    id_f32 = sb(ctx, "id_f32", [128, 128], F32)
    ones_bf = sb(ctx, "ones_bf", [128, 128], BF16)
    rot_bf = sb(ctx, "rot_bf", [128, 128], BF16)
    dmask = sb(ctx, "dmask", [128, 512], BF16)

    WqT = sb(ctx, "WqT", [128, NFC * 64], BF16)
    WrT = sb(ctx, "WrT", [64, 48], BF16)
    akT_raw = sb(ctx, "akT_raw", [64, 1024], BF16)
    akT_aug = sb(ctx, "akT_aug", [128, 1024], BF16)
    rkT_aug = sb(ctx, "rkT_aug", [128, 1024], BF16)
    wu2c = sb(ctx, "wu2c", [128, 8], BF16)
    bu1c = sb(ctx, "bu1c", [128, 8], F32)
    bu2_sb = sb(ctx, "bu2_sb", [1, 1], F32)
    gw1_sb = sb(ctx, "gw1_sb", [1, 1], F32)
    gb_sb = sb(ctx, "gb_sb", [1, 1], F32)
    MHC = sb(ctx, "MHC", [128, 32], F32)

    qmx = sb(ctx, "qmx", [1, 32], F32)

    knx = sb(ctx, "knx", [1, 8], F32)
    km4 = sb(ctx, "km4", [1, 32], F32)
    mrow = sb(ctx, "mrow", [1, 32], F32)
    mrow_bf = sb(ctx, "mrow_bf", [1, 32], BF16)
    stats = [sb(ctx, f"stats{i}", [128, 16], F32) for i in range(2)]
    rstd2_k = sb(ctx, "rstd2_k", [128, 8], F32)
    nmr2_k = sb(ctx, "nmr2_k", [128, 8], F32)
    sums_ao = sb(ctx, "sums_ao", [128, 8], F32)
    sumsq_ao = sb(ctx, "sumsq_ao", [128, 8], F32)
    lv_all = sb(ctx, "lv_all", [128, 8], F32)
    lffn_all = sb(ctx, "lffn_all", [128, 8], F32)
    gs_all = sb(ctx, "gs_all", [128, 8], F32)
    gcol = sb(ctx, "gcol", [128, 1], F32)
    rcol = sb(ctx, "rcol", [128, 1], F32)
    lvrow = sb(ctx, "lvrow", [1, 1024], F32)
    sigrow = sb(ctx, "sigrow", [1, 1024], F32)
    growA = sb(ctx, "growA", [1, 1024], F32)
    growB = sb(ctx, "growB", [1, 1024], F32)
    arsb = sb(ctx, "arsb", [1, 16], F32)
    negbig = sb(ctx, "negbig", [128, 1], F32)
    q_aug = sb(ctx, "q_aug", [128, 1024], BF16)
    rq_aug = sb(ctx, "rq_aug", [128, 1024], BF16)


    cscal = sb(ctx, "cscal", [128, 8], F32)
    CVALS = [EPS, 1.0, -0.5, 0.125, 1.0 / math.sqrt(48.0), ISQHD, GC1, 0.0]
    for i, vv in enumerate(CVALS):
        nc.const_aps.register(F32, vv, cscal[:, i:i + 1])

    e_id = dy(scst, id_bf[:, :], id_bf_d)
    dy(scst, cscal[:, :], cscal_d)
    dy(scst, id_f32[:, :], id_f32_d)
    dy(scst, ones_bf[:, :], ones_bf_d)
    dy(scst, rot_bf[:, :], rot_d)
    dy(scst, dmask[:, :], dmask_d)
    dy(scst, bu1c[:, :], bu1_d)
    dy(scst, bu2_sb[:, :], bu2_d)
    dy(scst, gw1_sb[:, :], gw1_d)
    dy(scst, gb_sb[:, :], gb_d)
    dg_split(scst, WqT[:, :].rearrange("p (c q) -> p c q", c=NFC),
       fWqT_d.rearrange("(c p) q -> p c q", p=128))
    dg(scst, WrT[:, :], fWrT_d)
    dg(scst, akT_raw[:, :], akT_d)
    dg(scst, akT_aug[64:65, :], lrel_d)
    dg(scst, rkT_aug[48:49, :], lrel_d)
    e_wu2 = dg(scst, wu2c[:, :], wu2c_d)
    e_cst = now(scst)

    e_ms1 = v_op(lambda e: e.memset(q_aug[64:96, :], 1.0), waits=(e_cst,), inc=True)
    v_op(lambda e: e.memset(rq_aug[32:64, :], 1.0), inc=True)
    e_ms2 = v_op(lambda e: e.memset(negbig[:, :], -1e30), inc=True)

    e_ak = s_op(lambda e: e.activation(akT_aug[0:64, :], akT_raw[:, :], AF.Copy,
                                       scale=0.125), waits=(e_cst,), inc=True)

    with nc.psum_tensor("p_rk", [48, 1024], F32) as p_rk:
        for half in range(2):
            t_op(lambda e, h=half: e.matmul(p_rk[:, h * 512:(h + 1) * 512],
                                            WrT[:, :], akT_raw[:, h * 512:(h + 1) * 512],
                                            start=True, stop=True),
                 waits=(e_cst,), inc=(half == 1))
        e_rk_mm = now(st_)
        e_rk = s_op(lambda e: e.activation(rkT_aug[0:48, :], p_rk[:, :], AF.Copy,
                                           scale=1.0 / math.sqrt(48.0)),
                    waits=(e_rk_mm,), inc=True)

    if stop_after == "p0":
        do = nc.dram_tensor("dbg_p0", [128, 1024], BF16, kind="ExternalOutput").ap()
        dy(sout, do, rkT_aug[:, :], waits=(e_rk, e_ak, e_ms1, e_ms2, e_cst))
        P.op("y", lambda e: e.nop(), waits=(now(sout),))
        with nc.Block() as block:
            P.emit(nc, block)
        ctx.close()
        return nc

    p1 = ExitStack()
    cos_t = sb(p1, "cos_t", [HD, OWN], F32)
    sin_t = sb(p1, "sin_t", [HD, OWN], F32)
    qm2 = sb(p1, "qm2", [1, 64], F32)
    kn2 = sb(p1, "kn2", [1, 16], F32)
    x_sb = [sb(p1, f"x_sb{i}", [128, D], F32) for i in range(2)]
    dy(scst, cos_t[:, :], cos_d)
    e_cos_ev = dy(scst, sin_t[:, :], sin_d)
    ln_tm = sb(p1, "ln_tm", [128, D], BF16)
    ln_fm = sb(p1, "ln_fm", [128, NFC * OWN], BF16)
    scrA = sb(p1, "scrA", [128, 512], F32)
    scrB = sb(p1, "scrB", [128, 512], F32)
    q0_bf = [sb(p1, f"q0_bf{i}", [128, 512], BF16) for i in range(2)]
    qf_bf = [sb(p1, f"qf_bf{i}", [128, 512], BF16) for i in range(2)]
    qsq_bf = [sb(p1, f"qsq_bf{i}", [128, 512], BF16) for i in range(2)]

    p1a = ExitStack()
    p_tr = [ps(p1a, f"p_tr{i}", [128, 512], BF16) for i in range(2)]

    ev_x_free = [None, None]
    ev_tr_evac = {0: None, 1: None}
    ev_lnfm = None
    last_stats = [None, None]
    last_tr_of_tile = None
    for j in range(NT):
        sl = j % 2
        stt_t = stats[sl]
        e_x = dy(sx2[sl], x_sb[sl][:, :], x_d[j * 128:(j + 1) * 128, :],
                 waits=(ev_x_free[sl],))
        e_sum = v_op(lambda e, sl=sl, stt_t=stt_t: e.reduce_sum(
            stt_t[:, 0:1], x_sb[sl][:, :], axis=AX.X),
            waits=(e_x, last_stats[sl]), inc=True)
        e_sq = s_op(lambda e, sl=sl, stt_t=stt_t: e.activation(
            ln_tm[:, :], x_sb[sl][:, :], AF.Square, accum_out=stt_t[:, 1:2]),
            waits=(e_x, last_stats[sl], last_tr_of_tile), inc=True)
        v_op(lambda e, stt_t=stt_t: e.tensor_scalar(stt_t[:, 2:3], stt_t[:, 0:1],
                                                    1.0 / D, None, ALU.mult),
             waits=(e_sum,))
        v_op(lambda e, stt_t=stt_t: e.tensor_scalar(stt_t[:, 3:4], stt_t[:, 1:2],
                                                    1.0 / D, None, ALU.mult),
             waits=(e_sq,))
        v_op(lambda e, stt_t=stt_t: e.tensor_tensor(stt_t[:, 4:5], stt_t[:, 2:3],
                                                    stt_t[:, 2:3], ALU.mult))
        e_var = v_op(lambda e, stt_t=stt_t: e.tensor_sub(stt_t[:, 5:6], stt_t[:, 3:4],
                                                         stt_t[:, 4:5]), inc=True)
        s_op(lambda e, stt_t=stt_t: e.activation(stt_t[:, 6:7], stt_t[:, 5:6], AF.Ln,
                                                 bias=EPS), waits=(e_var,))
        e_rstd = s_op(lambda e, stt_t=stt_t: e.activation(stt_t[:, 7:8], stt_t[:, 6:7],
                                                          AF.Exp, scale=-0.5), inc=True)
        e_nmr = v_op(lambda e, stt_t=stt_t: e.tensor_scalar(stt_t[:, 8:9], stt_t[:, 2:3],
                                                            stt_t[:, 7:8], -1.0,
                                                            ALU.mult, ALU.mult),
                     waits=(e_rstd,), inc=True)
        e_ln = s_op(lambda e, sl=sl, stt_t=stt_t: e.activation(
            ln_tm[:, :], x_sb[sl][:, :], AF.Identity, bias=stt_t[:, 8:9],
            scale=stt_t[:, 7:8]), waits=(e_nmr,), inc=True)
        ev_x_free[sl] = e_ln
        last_stats[sl] = e_ln
        lnfm3 = ln_fm[:, :].rearrange("p (fc t) -> p fc t", fc=NFC)
        for g in range(8):
            par = g % 2
            for k in range(4):
                fc = g * 4 + k
                t_op(lambda e, fc=fc, par=par, k=k: e.transpose(
                        p_tr[par][:, k * 128:(k + 1) * 128],
                        ln_tm[:, fc * 128:(fc + 1) * 128], id_bf[:, :]),
                     waits=(e_ln, e_id, ev_tr_evac[par]), inc=(k == 3))
            e_trg = now(st_)
            e_ev = v_op(lambda e, g=g, j=j, par=par, lnfm3=lnfm3: e.tensor_copy(
                            lnfm3[:, g * 4:(g + 1) * 4, j * 128:(j + 1) * 128],
                            p_tr[par][:, :].rearrange("p (k t) -> p k t", k=4)),
                        waits=(e_trg,), inc=True)
            ev_tr_evac[par] = e_ev
            ev_lnfm = e_ev
        last_tr_of_tile = now(st_)
    p1a.close()

    q0_used = [None, None]
    qf_used = [None, None]
    qsq_used = [None, None]

    pend_rope = []

    def rope_block(p_src, e_mm, head, half, is_k, p_rot, p_qn, war):
        par = half
        c0 = half * 512
        e_q0 = s_op(lambda e, par=par: e.activation(q0_bf[par][:, :], p_src[:, :],
                                                    AF.Copy),
                    waits=(e_mm, q0_used[par], war.get("rot")), inc=True)
        war["rot"] = e_q0

        def stage2():
            e_rot = t_op(lambda e, par=par: e.matmul(p_rot[:, :], rot_bf[:, :],
                                                     q0_bf[par][:, :],
                                                     start=True, stop=True),
                         waits=(e_q0, war.get("rotv")), inc=True)
            e_t1 = v_op(lambda e, par=par, c0=c0: e.tensor_mul(
                scrA[:, :], q0_bf[par][:, :], cos_t[:, c0:c0 + 512]),
                waits=(e_q0, e_cos_ev), inc=True)
            q0_used[par] = e_t1
            e_t2 = v_op(lambda e, c0=c0: e.tensor_mul(scrB[:, :], p_rot[:, :],
                                                      sin_t[:, c0:c0 + 512]),
                        waits=(e_rot,), inc=True)
            war["rotv"] = e_t2
            e_qf = v_op(lambda e, par=par: e.tensor_add(qf_bf[par][:, :], scrA[:, :],
                                                        scrB[:, :]),
                        waits=(e_t1, e_t2) + tuple(qf_used[par] or ()), inc=True)
            e_sq = s_op(lambda e, par=par: e.activation(qsq_bf[par][:, :],
                                                        qf_bf[par][:, :], AF.Square),
                        waits=(e_qf, qsq_used[par]), inc=True)
            e_qn = t_op(lambda e, par=par: e.matmul(p_qn[:, :], ones_bf[:, 0:1],
                                                    qsq_bf[par][:, :],
                                                    start=True, stop=True),
                        waits=(e_sq, war.get("qn")), inc=True)
            qsq_used[par] = e_qn
            col = head * 2 + half
            dstn = kn2 if is_k else qm2
            e_red = v_op(lambda e, col=col, dstn=dstn: e.reduce_max(
                dstn[0:1, col:col + 1], p_qn[:, :], axis=AX.X),
                waits=(e_qn,), inc=True)
            war["qn"] = e_red
            if is_k:
                dy(ssp[par], kvin_d[head * 128:(head + 1) * 128, c0:c0 + 512],
                   qf_bf[par][:, :], waits=(e_qf,))
            else:
                dy(ssp[par], qspill_d[head * 128:(head + 1) * 128, c0:c0 + 512],
                   qf_bf[par][:, :], waits=(e_qf,))
            qf_used[par] = [e_sq, (ssp[par], ssp[par].n)]
        pend_rope.append(stage2)
        return e_q0

    def flush_rope():
        while pend_rope:
            pend_rope.pop(0)()

    p1b = ExitStack()
    wkcol = [sb(p1b, f"wkcol{i}", [128, NFC * 128], BF16) for i in range(2)]
    p_pj = [ps(p1b, f"p_pj{i}", [128, 512]) for i in range(2)]
    p_rot1 = ps(p1b, "p_rot1", [128, 512])
    p_qn1 = ps(p1b, "p_qn1", [1, 512])

    war1 = {}
    ev_wk_free = [None, None]
    ev_pj_free = [None, None]
    wk_r = wk_d.rearrange("(fc p) n -> p fc n", p=128)
    for kvh in range(KVH):
        sl = kvh % 2
        e_wk = dg_split(swk2[sl], wkcol[sl][:, :].rearrange("p (c m) -> p c m", c=NFC),
                  wk_r[:, :, kvh * 128:(kvh + 1) * 128], waits=(ev_wk_free[sl],))
        last_mm = None
        for half in range(2):
            par = half
            for fc in range(NFC):
                t_op(lambda e, fc=fc, par=par, sl=sl, half=half: e.matmul(
                        p_pj[par][:, :], wkcol[sl][:, fc * 128:(fc + 1) * 128],
                        ln_fm[:, fc * OWN + half * 512: fc * OWN + half * 512 + 512],
                        start=(fc == 0), stop=(fc == NFC - 1)),
                     waits=(e_wk, ev_lnfm, ev_pj_free[par]), inc=(fc == NFC - 1))
                if fc == 8:
                    flush_rope()
            e_mm = now(st_)
            last_mm = e_mm
            ev_pj_free[par] = rope_block(p_pj[par], e_mm, kvh, half, True,
                                         p_rot1, p_qn1, war1)
        ev_wk_free[sl] = last_mm
    flush_rope()
    e_knx = v_op(lambda e: e.reduce_max(knx[0:1, :], kn2[0:1, :].rearrange(
        "p (h a) -> p h a", a=2), axis=AX.X), inc=True)
    dg(skv, kvin_d[2048:2049, 0:8], knx[0:1, :].rearrange("p (a c) -> p a c", a=1),
       waits=(e_knx,))
    e_p1b_s = now(ss_)
    e_p1b_v = now(sv_)
    p1b.close()

    p1c = ExitStack()
    wvh = sb(p1c, "wvh", [128, NFC * 512], BF16)
    v_bf = [sb(p1c, f"v_bf{i}", [128, 512], BF16) for i in range(2)]
    p_pv = [ps(p1c, f"p_pv{i}", [128, 512]) for i in range(2)]
    wv_r = wv_d.rearrange("(fc p) n -> p fc n", p=128)
    ev_wv_free = None
    ev_pv_free = [e_p1b_s, e_p1b_v]
    ev_vbf_free = [None, None]
    for vhalf in range(2):
        e_wv = dg_split(swv, wvh[:, :].rearrange("p (c m) -> p c m", c=NFC),
                  wv_r[:, :, vhalf * 512:(vhalf + 1) * 512],
                  waits=(ev_wv_free, e_p1b_s, e_p1b_v))
        last = None
        for j in range(NT):
            par = j % 2
            for fc in range(NFC):
                t_op(lambda e, fc=fc, par=par, j=j: e.matmul(
                        p_pv[par][:, :],
                        ln_fm[:, fc * OWN + j * 128: fc * OWN + j * 128 + 128],
                        wvh[:, fc * 512:(fc + 1) * 512],
                        start=(fc == 0), stop=(fc == NFC - 1)),
                     waits=(e_wv, ev_lnfm, ev_pv_free[par]), inc=(fc == NFC - 1))
            e_mm = now(st_)
            last = e_mm
            e_ev = s_op(lambda e, par=par: e.activation(v_bf[par][:, :], p_pv[par][:, :],
                                                        AF.Copy),
                        waits=(e_mm, ev_vbf_free[par]), inc=True)
            ev_pv_free[par] = e_ev
            e_vw = dy(svb[par], kvin_d[1024 + j * 128:1024 + (j + 1) * 128,
                                  vhalf * 512:(vhalf + 1) * 512],
                      v_bf[par][:, :], waits=(e_ev,))
            ev_vbf_free[par] = e_vw
        ev_wv_free = last
    e_p1c_s = now(ss_)
    p1c.close()

    kv_done_waits = (now(skv), now(ssp[0]), now(ssp[1]), now(svb[0]), now(svb[1]))

    p1d = ExitStack()
    wqcol = [sb(p1d, f"wqcol{i}", [128, NFC * 128], BF16) for i in range(4)]
    p_pj2 = [ps(p1d, f"p_pj2{i}", [128, 512]) for i in range(2)]
    p_rot2 = ps(p1d, "p_rot2", [128, 512])
    p_qn2 = ps(p1d, "p_qn2", [1, 512])
    wq_r = wq_d.rearrange("(fc p) n -> p fc n", p=128)
    war2 = {}
    ev_wq_free = [None, None, None, None]
    ev_pj2_free = [e_p1c_s, e_p1c_s]
    for h in range(H):
        sl = h % 4
        e_wq = dg_split(swq4[sl], wqcol[sl][:, :].rearrange("p (c m) -> p c m", c=NFC),
                  wq_r[:, :, h * 128:(h + 1) * 128], waits=(ev_wq_free[sl], e_p1c_s))
        last_mm = None
        for half in range(2):
            par = half
            for fc in range(NFC):
                t_op(lambda e, fc=fc, par=par, sl=sl, half=half: e.matmul(
                        p_pj2[par][:, :], wqcol[sl][:, fc * 128:(fc + 1) * 128],
                        ln_fm[:, fc * OWN + half * 512: fc * OWN + half * 512 + 512],
                        start=(fc == 0), stop=(fc == NFC - 1)),
                     waits=(e_wq, ev_lnfm, ev_pj2_free[par]), inc=(fc == NFC - 1))
                if fc == 8:
                    flush_rope()
            e_mm = now(st_)
            last_mm = e_mm
            ev_pj2_free[par] = rope_block(p_pj2[par], e_mm, h, half, False,
                                          p_rot2, p_qn2, war2)
        ev_wq_free[sl] = last_mm
    flush_rope()
    e_qmx = v_op(lambda e: e.reduce_max(qmx[0:1, :], qm2[0:1, :].rearrange(
        "p (h a) -> p h a", a=2), axis=AX.X), inc=True)
    e_p1_t = now(st_)
    e_p1_s = now(ss_)
    e_p1_v = now(sv_)
    p1d.close()
    p1.close()

    if stop_after == "p1":
        dq = nc.dram_tensor("dbg_qspill", [D, OWN], BF16, kind="ExternalOutput").ap()
        dk = nc.dram_tensor("dbg_kvin", [KVROWS, OWN], BF16, kind="ExternalOutput").ap()
        dy(sout, dq, qspill_d,
           waits=(now(ssp[0]), now(ssp[1]), now(svb[0]), now(svb[1]), now(skv),
                  e_p1_t, e_p1_s, e_p1_v))
        dy(sout, dk, kvin_d)
        P.op("y", lambda e: e.nop(), waits=(now(sout),))
        with nc.Block() as block:
            P.emit(nc, block)
        ctx.close()
        return nc

    P.op("g", lambda e: e.collective_compute(
        "AllGather", ALU.bypass,
        replica_groups=[[0, 1, 2, 3], [4, 5, 6, 7]],
        ins=[kvin_d.opt()], outs=[kvg_d.opt()]),
        waits=kv_done_waits, inc=cc_)
    e_ag = now(cc_)

    kvg_rr = kvg_d.rearrange("(rr rows) c -> rows rr c", rows=KVROWS)
    e_km = dg(skm, km4[0:1, :].rearrange("p (rr c) -> p rr c", rr=4),
              kvg_rr[2048:2049, :, 0:8], waits=(e_ag,))
    e_k1 = v_op(lambda e: e.tensor_max(km4[0:1, 0:8], km4[0:1, 8:16], km4[0:1, 0:8]),
                waits=(e_km,), inc=True)
    e_k2 = v_op(lambda e: e.tensor_max(km4[0:1, 16:24], km4[0:1, 24:32],
                                       km4[0:1, 16:24]), waits=(e_km,), inc=True)
    e_k3 = v_op(lambda e: e.tensor_max(km4[0:1, 0:8], km4[0:1, 0:8], km4[0:1, 16:24]),
                waits=(e_k1, e_k2), inc=True)
    e_ks = s_op(lambda e: e.activation(km4[0:1, 0:8], km4[0:1, 0:8], AF.Sqrt),
                waits=(e_k3,), inc=True)
    e_qs = s_op(lambda e: e.activation(qmx[0:1, :], qmx[0:1, :], AF.Sqrt),
                waits=(e_qmx,), inc=True)
    mrow3 = mrow[0:1, :].rearrange("p (h a) -> p h a", a=4)
    km3 = km4[0:1, 0:8].rearrange("p (h a) -> p h a", a=1)
    for sub in range(4):
        v_op(lambda e, sub=sub: e.tensor_copy(mrow3[:, :, sub:sub + 1], km3),
             waits=(e_ks,))
    e_mr = v_op(lambda e: e.scalar_tensor_tensor(mrow[0:1, :], qmx[0:1, :], NBOUND,
                                                 mrow[0:1, :], ALU.mult, ALU.mult),
                waits=(e_qs,), inc=True)
    e_mrb = s_op(lambda e: e.activation(mrow_bf[0:1, :], mrow[0:1, :], AF.Copy),
                 waits=(e_mr,), inc=True)
    with nc.psum_tensor("p_mhc", [128, 32], F32) as p_mhc:
        t_op(lambda e: e.matmul(p_mhc[:, :], ones_bf[0:1, :], mrow_bf[0:1, :],
                                start=True, stop=True),
             waits=(e_mrb, e_p1_s, e_p1_v), inc=True)
        e_mh1 = now(st_)
        e_mhc = s_op(lambda e: e.activation(MHC[:, :], p_mhc[:, :], AF.Copy),
                     waits=(e_mh1,), inc=True)

    sAO = ExitStack()
    AO_fm = sb(sAO, "AO_fm", [128, H * OWN], BF16)
    p3 = ExitStack()
    K_h = [sb(p3, f"K_h{i}", [128, 4 * OWN], BF16) for i in range(2)]
    V_h = [sb(p3, f"V_h{i}", [128, 32 * VST], BF16) for i in range(2)]
    Q_h = [sb(p3, f"Q_h{i}", [128, OWN], BF16) for i in range(2)]
    PT = sb(p3, "PT", [128, NT * 512], BF16)
    AOq = [sb(p3, f"AOq{i}", [128, 128], BF16) for i in range(2)]
    rl_sb = [sb(p3, f"rl_sb{i}", [128, 1], F32) for i in range(2)]
    p_S = [ps(p3, f"p_S{i}", [128, 1024]) for i in range(2)]
    p_AV = [ps(p3, f"p_AV{i}", [128, 129]) for i in range(2)]
    p_AOT = [ps(p3, f"p_AOT{i}", [128, 128], BF16) for i in range(2)]

    ev_vones = []
    for i in range(2):
        vh3 = V_h[i][:, :].rearrange("p (m c) -> p m c", c=VST)
        vw0 = () if i == 0 else (e_p1_t, e_p1_s, now(swq4[0]), now(swq4[1]),
                                 now(swq4[2]), now(swq4[3]), now(sx2[0]), now(sx2[1]))
        ev_vones.append(v_op(lambda e, vh3=vh3: e.memset(vh3[:, :, 128:129], 1.0),
                             waits=vw0, inc=True))

    ev_kh_free = [None, None]
    ev_vh_free = [None, None]
    ev_qh_free = [None, None]
    pend_tail = [None]
    exp_ev = {0: None, 1: None}
    av_war = {0: None, 1: None}
    aot_war = {0: None, 1: None}
    aoq_war = {0: None, 1: None}
    rl_war = {0: None, 1: None}
    for kvh in range(KVH):
        ks = kvh % 2
        e_kh = e_vh = None
        for rr in range(4):
            kw = (e_ag, ev_kh_free[ks], e_p1_t, e_p1_s, e_p1_v)
            e_kh = dy(skh2[ks], K_h[ks][:, rr * OWN:(rr + 1) * OWN],
                      kvg_d[rr * KVROWS + kvh * 128: rr * KVROWS + (kvh + 1) * 128, :],
                      waits=kw)
            vdst = V_h[ks][:, :].rearrange("p (m c) -> p m c", c=VST)
            vsrc = kvg_d[rr * KVROWS + 1024: rr * KVROWS + 2048,
                         kvh * 128:(kvh + 1) * 128].rearrange(
                             "(jj s) e -> s jj e", s=128)
            vw = (e_ag, ev_vh_free[ks], ev_vones[ks], e_p1_t, e_p1_s, e_p1_v)
            e_vh = dy(svh2[ks], vdst[:, rr * 8:(rr + 1) * 8, 0:128], vsrc,
                      waits=vw)
        for qh4 in range(4):
            h = kvh * 4 + qh4
            qs = h % 2
            e_qh = dy(sqr2[qs], Q_h[qs][:, :], qspill_d[h * 128:(h + 1) * 128, :],
                      waits=(now(ssp[0]), now(ssp[1]), ev_qh_free[qs]))
            last_t_of_head = None
            for j in range(NT):
                avp = j % 2
                ready = {}
                nch = (j + 2) // 2

                def emit_Sc(c, j=j, h=h, ks=ks, qs=qs):
                    cpar = c % 2
                    wins = [w for w in (2 * c, 2 * c + 1) if w <= j]
                    for w in wins:
                        off = (w - 2 * c) * 512
                        for rr in range(4):
                            t_op(lambda e, w=w, rr=rr, cpar=cpar, off=off: e.matmul(
                                    p_S[cpar][:, off + rr * 128: off + rr * 128 + 128],
                                    K_h[ks][:, rr * OWN + w * 128: rr * OWN + w * 128 + 128],
                                    Q_h[qs][:, j * 128:(j + 1) * 128],
                                    start=True, stop=True),
                                 waits=(e_kh, e_qh, exp_ev[cpar], e_mhc),
                                 inc=(w == wins[-1] and rr == 3))
                    e_sc = now(st_)
                    nw = len(wins)
                    e_exp = s_op(lambda e, c=c, cpar=cpar, nw=nw: e.activation(
                                    PT[:, 2 * c * 512: (2 * c + nw) * 512],
                                    p_S[cpar][:, 0:nw * 512],
                                    AF.Exp, bias=MHC[:, h:h + 1], scale=ISQHD),
                                 waits=(e_sc, e_mhc), inc=True)
                    exp_ev[cpar] = e_exp
                    for w in wins:
                        if w == j:
                            ready[w] = v_op(lambda e, w=w: e.tensor_mul(
                                            PT[:, w * 512:(w + 1) * 512],
                                            PT[:, w * 512:(w + 1) * 512], dmask[:, :]),
                                            waits=(e_exp,), inc=True)
                        else:
                            ready[w] = e_exp

                def emit_AV(i, j=j, ks=ks, avp=avp):
                    for rr in range(4):
                        t_op(lambda e, i=i, rr=rr: e.matmul(
                                p_AV[avp][:, :],
                                PT[:, i * 512 + rr * 128: i * 512 + rr * 128 + 128],
                                V_h[ks][:, (rr * 8 + i) * VST: (rr * 8 + i) * VST + 129],
                                start=(i == 0 and rr == 0), stop=(i == j and rr == 3)),
                             waits=(ready[i], e_vh, av_war[avp]),
                             inc=(i == j and rr == 3))

                emit_Sc(0)
                emitted_tail = False
                for c in range(nch):
                    if c + 1 < nch:
                        emit_Sc(c + 1)
                    if not emitted_tail and pend_tail[0] is not None and c >= 1:
                        pend_tail[0]()
                        pend_tail[0] = None
                        emitted_tail = True
                    for w in (2 * c, 2 * c + 1):
                        if w <= j:
                            emit_AV(w)
                e_avd = now(st_)
                last_t_of_head = e_avd
                e_rl = v_op(lambda e, avp=avp: e.reciprocal(rl_sb[avp][:, :],
                                                            p_AV[avp][:, 128:129]),
                            waits=(e_avd, rl_war[avp]), inc=True)
                e_ao = s_op(lambda e, avp=avp: e.activation(
                                AOq[avp][:, :], p_AV[avp][:, 0:128], AF.Copy,
                                scale=rl_sb[avp][:, :]),
                            waits=(e_rl, aoq_war[avp]), inc=True)
                av_war[avp] = e_ao
                rl_war[avp] = e_ao

                def mk_tail(avp=avp, h=h, j=j, e_ao=e_ao):
                    def tail():
                        t_op(lambda e: e.transpose(p_AOT[avp][:, :], AOq[avp][:, :],
                                                   id_bf[:, :]),
                             waits=(e_ao, aot_war[avp]), inc=True)
                        aoq_war[avp] = now(st_)
                        e_fm = v_op(lambda e: e.tensor_copy(
                                        AO_fm[:, h * OWN + j * 128:
                                              h * OWN + j * 128 + 128],
                                        p_AOT[avp][:, :]),
                                    waits=(now(st_),), inc=True)
                        aot_war[avp] = e_fm
                    return tail
                if pend_tail[0] is not None:
                    pend_tail[0]()
                pend_tail[0] = mk_tail()
            ev_qh_free[qs] = last_t_of_head
        ev_kh_free[ks] = last_t_of_head
        ev_vh_free[ks] = last_t_of_head
    if pend_tail[0] is not None:
        pend_tail[0]()
        pend_tail[0] = None
    e_attn_done = now(sv_)
    e_p3_t = now(st_)
    e_p3_s = now(ss_)
    p3.close()

    p4 = ExitStack()
    wosl = [sb(p4, "wosl0", [128, NFC * 512], BF16, side="right"),
            sb(p4, "wosl1", [128, NFC * 512], BF16)]
    aost = [sb(p4, f"aost{i}", [128, 512], F32) for i in range(2)]
    sink4 = sb(p4, "sink4", [128, 512], F32)
    p_o2 = [ps(p4, f"p_o2{i}", [128, 512]) for i in range(2)]
    wo_r = wo_d.rearrange("(fc p) n -> p fc n", p=128)
    ev_wo_free = [None, None]
    ev_o2_free = [None, None]
    ev_aost_free = [None, None]
    acc_war = [None, None]
    for oc in range(8):
        ws = oc % 2
        w_extra = (e_p1_t, e_p1_s, e_p1_v) if ws == 0 else (e_p3_t, e_p3_s,
                                                            e_attn_done)
        e_wo = dg_split(swo2[ws], wosl[ws][:, :].rearrange("p (c m) -> p c m", c=NFC),
                  wo_r[:, :, oc * 512:(oc + 1) * 512],
                  waits=(ev_wo_free[ws],) + w_extra)
        last = None
        for j in range(NT):
            par = j % 2
            stt_t = stats[par]
            for fc in range(NFC):
                t_op(lambda e, fc=fc, par=par, j=j, ws=ws: e.matmul(
                        p_o2[par][:, :],
                        AO_fm[:, fc * OWN + j * 128: fc * OWN + j * 128 + 128],
                        wosl[ws][:, fc * 512:(fc + 1) * 512],
                        start=(fc == 0), stop=(fc == NFC - 1)),
                     waits=(e_wo, e_attn_done, ev_o2_free[par]), inc=(fc == NFC - 1))
            e_mm = now(st_)
            last = e_mm
            e_ev = s_op(lambda e, par=par, stt_t=stt_t: e.activation(
                            aost[par][:, :], p_o2[par][:, :], AF.Copy,
                            accum_out=stt_t[:, 9:10]),
                        waits=(e_mm, ev_aost_free[par], acc_war[par]), inc=True)
            ev_o2_free[par] = e_ev
            if oc == 0:
                v_op(lambda e, j=j, stt_t=stt_t: e.tensor_copy(
                    sums_ao[:, j:j + 1], stt_t[:, 9:10]), waits=(e_ev,))
            else:
                v_op(lambda e, j=j, stt_t=stt_t: e.tensor_add(
                    sums_ao[:, j:j + 1], sums_ao[:, j:j + 1], stt_t[:, 9:10]),
                    waits=(e_ev,))
            e_sq = s_op(lambda e, par=par, stt_t=stt_t: e.activation(
                            sink4[:, :], aost[par][:, :], AF.Square,
                            accum_out=stt_t[:, 10:11]), inc=True)
            if oc == 0:
                e_ac = v_op(lambda e, j=j, stt_t=stt_t: e.tensor_copy(
                    sumsq_ao[:, j:j + 1], stt_t[:, 10:11]), waits=(e_sq,), inc=True)
            else:
                e_ac = v_op(lambda e, j=j, stt_t=stt_t: e.tensor_add(
                    sumsq_ao[:, j:j + 1], sumsq_ao[:, j:j + 1], stt_t[:, 10:11]),
                    waits=(e_sq,), inc=True)
            acc_war[par] = e_ac
            e_w = dy(sat[par], ao_d[j * 128:(j + 1) * 128, oc * 512:(oc + 1) * 512],
                     aost[par][:, :], waits=(e_sq,))
            ev_aost_free[par] = e_w
        ev_wo_free[ws] = last
    e_p4_v = now(sv_)
    e_p4_t = now(st_)
    e_p4_s = now(ss_)
    p4.close()
    sAO.close()

    sHF = ExitStack()
    h_fm = sb(sHF, "h_fm", [128, NFC * OWN], BF16, side="right")
    p5 = ExitStack()
    xp_sb = sb(p5, "xp_sb", [128, D], F32)
    x2_sb = [sb(p5, "x2_sb0", [128, D], F32)]
    ao_rd = sb(p5, "ao_rd", [128, D], F32)
    adj_sb = sb(p5, "adj_sb", [128, D], F32)
    ln2_tm = sb(p5, "ln2_tm", [128, D], BF16)
    p_t5 = [ps(p5, f"p_t5{i}", [128, 512], BF16) for i in range(2)]
    a1 = ExitStack()
    wu1c = [sb(a1, f"wu1c{i}", [128, NFC * 128], BF16) for i in range(2)]
    gg = sb(a1, "gg", [128, 8 * OWN], BF16)
    gb_s = sb(a1, "gb_s", [128, 512], F32)
    x3g = sb(a1, "x3g", [128, 512], F32)
    thg = sb(a1, "thg", [128, 512], F32)
    p_g = [ps(a1, f"p_g{i}", [128, 512]) for i in range(2)]
    p_le = [ps(a1, f"p_le{i}", [1, 512]) for i in range(2)]

    ev_x2_free = [None]
    ev_ao_free = None
    ev_tr5 = {0: None, 1: None}
    ev_hfm = None
    ev_xp_w = None
    last_stats5 = [None, None]
    ev_hfm_half = {0: None, 1: None}
    last_tr5 = None
    prev_xp = None
    prev_ln2 = None
    for j in range(NT):
        sl = j % 2
        stt_t = stats[sl]
        e_x = dy(sx2[0], x2_sb[0][:, :], x_d[j * 128:(j + 1) * 128, :],
                 waits=(ev_x2_free[0], e_p4_t, e_p4_s, e_p4_v))
        e_a = dy(sao, ao_rd[:, :], ao_d[j * 128:(j + 1) * 128, :],
                 waits=(now(sat[0]), now(sat[1]), ev_ao_free, e_p4_t, e_p4_s,
                        e_p4_v))
        v_op(lambda e, j=j, stt_t=stt_t: e.tensor_scalar(
            stt_t[:, 2:3], sums_ao[:, j:j + 1], 1.0 / D, None, ALU.mult),
            waits=(e_p4_v, last_stats5[sl]))
        v_op(lambda e, j=j, stt_t=stt_t: e.tensor_scalar(
            stt_t[:, 3:4], sumsq_ao[:, j:j + 1], 1.0 / D, None, ALU.mult))
        v_op(lambda e, stt_t=stt_t: e.tensor_tensor(stt_t[:, 4:5], stt_t[:, 2:3],
                                                    stt_t[:, 2:3], ALU.mult))
        e_var = v_op(lambda e, stt_t=stt_t: e.tensor_sub(stt_t[:, 5:6], stt_t[:, 3:4],
                                                         stt_t[:, 4:5]), inc=True)
        s_op(lambda e, stt_t=stt_t: e.activation(stt_t[:, 6:7], stt_t[:, 5:6], AF.Ln,
                                                 bias=EPS),
             waits=(e_var, last_stats5[sl]))
        e_rstd = s_op(lambda e, stt_t=stt_t: e.activation(stt_t[:, 7:8], stt_t[:, 6:7],
                                                          AF.Exp, scale=-0.5), inc=True)
        e_nmr = v_op(lambda e, stt_t=stt_t: e.tensor_scalar(
            stt_t[:, 8:9], stt_t[:, 2:3], stt_t[:, 7:8], -1.0, ALU.mult, ALU.mult),
            waits=(e_rstd,), inc=True)
        e_adj = s_op(lambda e, stt_t=stt_t: e.activation(
            adj_sb[:, :], x2_sb[0][:, :], AF.Identity, bias=stt_t[:, 8:9]),
            waits=(e_x, e_nmr, prev_xp), inc=True)
        e_xp = v_op(lambda e, stt_t=stt_t: e.scalar_tensor_tensor(
            xp_sb[:, :], ao_rd[:, :], stt_t[:, 7:8], adj_sb[:, :],
            ALU.mult, ALU.add, accum_out=stt_t[:, 11:12]),
            waits=(e_a, e_adj, prev_ln2, ev_xp_w), inc=True)
        prev_xp = e_xp
        ev_x2_free[0] = e_adj
        ev_ao_free = e_xp
        ev_xp_w = dy(sxp, xp_d[j * 128:(j + 1) * 128, :], xp_sb[:, :], waits=(e_xp,))
        e_sq2 = s_op(lambda e, stt_t=stt_t: e.activation(
            ln2_tm[:, :], xp_sb[:, :], AF.Square, accum_out=stt_t[:, 12:13]),
            waits=(e_xp, last_tr5), inc=True)
        v_op(lambda e, stt_t=stt_t: e.tensor_scalar(stt_t[:, 2:3], stt_t[:, 11:12],
                                                    1.0 / D, None, ALU.mult),
             waits=(e_xp,))
        v_op(lambda e, stt_t=stt_t: e.tensor_scalar(stt_t[:, 3:4], stt_t[:, 12:13],
                                                    1.0 / D, None, ALU.mult),
             waits=(e_sq2,))
        v_op(lambda e, stt_t=stt_t: e.tensor_tensor(stt_t[:, 4:5], stt_t[:, 2:3],
                                                    stt_t[:, 2:3], ALU.mult))
        e_var2 = v_op(lambda e, stt_t=stt_t: e.tensor_sub(stt_t[:, 5:6], stt_t[:, 3:4],
                                                          stt_t[:, 4:5]), inc=True)
        s_op(lambda e, stt_t=stt_t: e.activation(stt_t[:, 6:7], stt_t[:, 5:6], AF.Ln,
                                                 bias=EPS), waits=(e_var2,))
        e_rstd2 = s_op(lambda e, j=j, stt_t=stt_t: e.activation(
            rstd2_k[:, j:j + 1], stt_t[:, 6:7], AF.Exp, scale=-0.5), inc=True)
        e_nmr2 = v_op(lambda e, j=j, stt_t=stt_t: e.tensor_scalar(
            nmr2_k[:, j:j + 1], stt_t[:, 2:3], rstd2_k[:, j:j + 1], -1.0,
            ALU.mult, ALU.mult), waits=(e_rstd2,), inc=True)
        e_ln2 = s_op(lambda e, j=j: e.activation(ln2_tm[:, :], xp_sb[:, :], AF.Identity,
                                                 bias=nmr2_k[:, j:j + 1],
                                                 scale=rstd2_k[:, j:j + 1]),
                     waits=(e_nmr2,), inc=True)
        prev_ln2 = e_ln2
        v_op(lambda e, stt_t=stt_t: e.tensor_scalar(stt_t[:, 13:14], stt_t[:, 5:6],
                                                    EPS, None, ALU.add),
             waits=(e_var2,))
        e_rec = v_op(lambda e, stt_t=stt_t: e.reciprocal(stt_t[:, 14:15],
                                                         stt_t[:, 13:14]), inc=True)
        e_vh2 = v_op(lambda e, stt_t=stt_t: e.tensor_tensor(
            stt_t[:, 15:16], stt_t[:, 5:6], stt_t[:, 14:15], ALU.mult),
            waits=(e_rec,), inc=True)
        e_lv = s_op(lambda e, j=j, stt_t=stt_t: e.activation(
            lv_all[:, j:j + 1], stt_t[:, 15:16], AF.Ln, bias=1.0),
            waits=(e_vh2,), inc=True)
        last_stats5[sl] = e_lv
        hfm3 = h_fm[:, :].rearrange("p (fc t) -> p fc t", fc=NFC)
        for g in range(8):
            par = g % 2
            for k in range(4):
                fc = g * 4 + k
                t_op(lambda e, fc=fc, par=par, k=k: e.transpose(
                        p_t5[par][:, k * 128:(k + 1) * 128],
                        ln2_tm[:, fc * 128:(fc + 1) * 128], id_bf[:, :]),
                     waits=(e_ln2, ev_tr5[par]), inc=(k == 3))
            e_trg = now(st_)
            e_ev = v_op(lambda e, g=g, j=j, par=par, hfm3=hfm3: e.tensor_copy(
                            hfm3[:, g * 4:(g + 1) * 4, j * 128:(j + 1) * 128],
                            p_t5[par][:, :].rearrange("p (k t) -> p k t", k=4)),
                        waits=(e_trg,), inc=True)
            ev_tr5[par] = e_ev
            ev_hfm = e_ev
        if j == 3:
            ev_hfm_half[0] = e_ev
        if j == 7:
            ev_hfm_half[1] = e_ev
        last_tr5 = now(st_)
    e_lv_done = now(ss_)


    wu1_r = Wu1T_d.rearrange("(fc p) n -> p fc n", p=128)
    ev_wu_free = [None, None]
    ev_g_free = [None, None]
    prev_ggv = None
    prev_x3 = None
    for uc in range(8):
        ws = uc % 2
        e_wu = dg_split(swu2[ws], wu1c[ws][:, :].rearrange("p (c m) -> p c m", c=NFC),
                  wu1_r[:, :, uc * 128:(uc + 1) * 128],
                  waits=(ev_wu_free[ws], e_p4_t, e_p4_s, e_p4_v,
                         now(sat[0]), now(sat[1])))
        last = None
        for half in range(2):
            par = half
            for fc in range(NFC):
                t_op(lambda e, fc=fc, par=par, ws=ws, half=half: e.matmul(
                        p_g[par][:, :], wu1c[ws][:, fc * 128:(fc + 1) * 128],
                        h_fm[:, fc * OWN + half * 512: fc * OWN + half * 512 + 512],
                        start=(fc == 0), stop=(fc == NFC - 1)),
                     waits=(e_wu, ev_hfm_half[half], ev_g_free[par]),
                     inc=(fc == NFC - 1))
            e_mm = now(st_)
            last = e_mm
            e_gb = s_op(lambda e, par=par, uc=uc: e.activation(
                gb_s[:, :], p_g[par][:, :], AF.Identity, bias=bu1c[:, uc:uc + 1]),
                waits=(e_mm, e_cst, prev_ggv), inc=True)
            ev_g_free[par] = e_gb
            e_sqg = s_op(lambda e: e.activation(x3g[:, :], gb_s[:, :], AF.Square),
                         waits=(prev_x3,), inc=True)
            e_x3 = v_op(lambda e: e.tensor_mul(x3g[:, :], x3g[:, :], gb_s[:, :]),
                        waits=(e_sqg,), inc=True)
            prev_x3 = e_x3
            e_ti = v_op(lambda e: e.scalar_tensor_tensor(thg[:, :], x3g[:, :], GC2,
                                                         gb_s[:, :], ALU.mult, ALU.add),
                        waits=(e_x3,), inc=True)
            e_th = s_op(lambda e: e.activation(thg[:, :], thg[:, :], AF.Tanh, scale=GC1),
                        waits=(e_ti,), inc=True)
            e_t5v = v_op(lambda e: e.tensor_scalar(thg[:, :], thg[:, :], 0.5, 0.5,
                                                   ALU.mult, ALU.add),
                         waits=(e_th,), inc=True)
            e_ggv = v_op(lambda e, uc=uc, half=half: e.tensor_mul(
                            gg[:, uc * OWN + half * 512: uc * OWN + half * 512 + 512],
                            thg[:, :], gb_s[:, :]),
                         waits=(e_t5v,), inc=True)
            prev_ggv = e_ggv
            t_op(lambda e, uc=uc, half=half: e.matmul(
                    p_le[half][:, :], wu2c[:, uc:uc + 1],
                    gg[:, uc * OWN + half * 512: uc * OWN + half * 512 + 512],
                    start=(uc == 0), stop=(uc == 7)),
                 waits=(e_ggv, e_wu2), inc=(uc == 7))
        ev_wu_free[ws] = last
    e_led = now(st_)
    for half in range(2):
        s_op(lambda e, half=half: e.activation(
            sigrow[0:1, half * 512:(half + 1) * 512], p_le[half][0:1, :],
            AF.Sigmoid, bias=bu2_sb[0:1, :]),
            waits=(e_led, e_cst), inc=(half == 1))
    e_sig = now(ss_)

    with nc.psum_tensor("p_lvT", [1, 128], F32) as p_lvT:
        e_lvc = None
        for j in range(NT):
            t_op(lambda e, j=j: e.transpose(p_lvT[:, :], lv_all[:, j:j + 1],
                                            id_f32[:, :]),
                 waits=(e_lv_done, e_lvc), inc=True)
            e_lvc = v_op(lambda e, j=j: e.tensor_copy(
                lvrow[0:1, j * 128:(j + 1) * 128], p_lvT[0:1, :]),
                waits=(now(st_),), inc=True)
    v_op(lambda e: e.memset(arsb[0:1, :], 0.0), waits=(e_lvc,))
    e_sumlv = v_op(lambda e: e.reduce_sum(arsb[0:1, 0:1], lvrow[0:1, :], axis=AX.X),
                   inc=True)
    dy(sar, arin_d, arsb[0:1, :], waits=(e_sumlv,))

    e_a1_t = now(st_)
    e_a1_v = now(sv_)
    e_p5_t = e_a1_t
    e_p5_s = now(ss_)
    e_p5_v = e_a1_v
    a1.close()
    p5.close()

    sA2 = ExitStack()
    wT = sb(sA2, "wT", [128, 8 * OWN], BF16)
    w_bf = sb(sA2, "w_bf", [128, 1024], BF16)
    srf = sb(sA2, "srf", [128, 1024], F32)
    mask_u8 = sb(sA2, "mask_u8", [128, 1024], mybir.dt.uint8)
    alm = sb(sA2, "alm", [128, 1024], F32)
    top8 = sb(sA2, "top8", [128, 8], F32)
    a2p = ExitStack()
    p_qy = ps(a2p, "p_qy", [64, 512])
    p_rq = ps(a2p, "p_rq", [48, 512])
    p_sr = ps(a2p, "p_sr", [128, 1024])
    p_al = ps(a2p, "p_al", [128, 1024])
    p_wt = [ps(a2p, f"p_wt{i}", [128, 512], BF16) for i in range(2)]

    e_qy_ev = None
    e_rq_ev = None
    ev_sr_free = None
    ev_al_free = None
    ev_wt_war = {0: None, 1: None}
    prev_wx = None
    prev_sel = None
    last_wtr = None
    for half in range(2):
        for fc in range(NFC):
            t_op(lambda e, fc=fc, half=half: e.matmul(
                    p_qy[:, :], WqT[:, fc * 64:(fc + 1) * 64],
                    h_fm[:, fc * OWN + half * 512: fc * OWN + half * 512 + 512],
                    start=(fc == 0), stop=(fc == NFC - 1)),
                 waits=(e_cst, ev_hfm, e_qy_ev, e_sig, e_a1_v), inc=(fc == NFC - 1))
        e_qmm = now(st_)
        e_qy_ev = s_op(lambda e, half=half: e.activation(
            q_aug[0:64, half * 512:(half + 1) * 512], p_qy[:, :], AF.Copy),
            waits=(e_qmm, e_ms1), inc=True)
        t_op(lambda e, half=half: e.matmul(p_rq[:, :], WrT[:, :],
                                           q_aug[0:64, half * 512:(half + 1) * 512],
                                           start=True, stop=True),
             waits=(e_qy_ev, e_rq_ev), inc=True)
        e_rqm = now(st_)
        e_rq_ev = s_op(lambda e, half=half: e.activation(
            rq_aug[0:48, half * 512:(half + 1) * 512], p_rq[:, :], AF.Copy),
            waits=(e_rqm,), inc=True)
        for jj in range(4):
            j = half * 4 + jj
            for hh in range(2):
                t_op(lambda e, jj=jj, half=half, hh=hh: e.matmul(
                        p_sr[:, hh * 512:(hh + 1) * 512],
                        rq_aug[0:49, half * 512 + jj * 128: half * 512 + jj * 128 + 128],
                        rkT_aug[0:49, hh * 512:(hh + 1) * 512],
                        start=True, stop=True),
                     waits=(e_rq_ev, e_rk, ev_sr_free), inc=(hh == 1))
            e_srm = now(st_)
            e_srf = s_op(lambda e: e.activation(srf[:, :], p_sr[:, :], AF.Copy),
                         waits=(e_srm, prev_sel), inc=True)
            ev_sr_free = e_srf
            e_t8 = v_op(lambda e: e.max(out=top8[:, :], in_=srf[:, :]),
                        waits=(e_srf,), inc=True)
            e_msk = v_op(lambda e: e.tensor_scalar(mask_u8[:, :], srf[:, :],
                                                   top8[:, 7:8], None, ALU.is_ge),
                         waits=(e_t8,), inc=True)
            for hh in range(2):
                t_op(lambda e, jj=jj, half=half, hh=hh: e.matmul(
                        p_al[:, hh * 512:(hh + 1) * 512],
                        q_aug[0:65, half * 512 + jj * 128: half * 512 + jj * 128 + 128],
                        akT_aug[0:65, hh * 512:(hh + 1) * 512],
                        start=True, stop=True),
                     waits=(e_qy_ev, e_ak, ev_al_free), inc=(hh == 1))
            e_almm = now(st_)
            e_sel = v_op(lambda e: e.select(alm[:, :], mask_u8[:, :], p_al[:, :],
                                            negbig[:, :].to_broadcast([128, 1024])),
                         waits=(e_almm, e_msk, e_ms2, prev_wx), inc=True)
            prev_sel = e_sel
            ev_al_free = e_sel
            e_nm = v_op(lambda e: e.reduce_max(stats[0][:, 9:10], alm[:, :], axis=AX.X,
                                               negate=True),
                        waits=(e_sel,), inc=True)
            e_wx = s_op(lambda e, j=j: e.activation(w_bf[:, :], alm[:, :], AF.Exp,
                                                    bias=stats[0][:, 9:10],
                                                    accum_out=lffn_all[:, j:j + 1]),
                        waits=(e_nm, last_wtr), inc=True)
            prev_wx = e_wx
            wt3 = wT[:, :].rearrange("p (sc t) -> p sc t", sc=8)
            for g in range(2):
                par = g
                for k in range(4):
                    sc = g * 4 + k
                    t_op(lambda e, sc=sc, par=par, k=k: e.transpose(
                            p_wt[par][:, k * 128:(k + 1) * 128],
                            w_bf[:, sc * 128:(sc + 1) * 128], id_bf[:, :]),
                         waits=(e_wx, ev_wt_war[par]), inc=(k == 3))
                e_trg = now(st_)
                e_evw = v_op(lambda e, g=g, j=j, par=par, wt3=wt3: e.tensor_copy(
                                wt3[:, g * 4:(g + 1) * 4, j * 128:(j + 1) * 128],
                                p_wt[par][:, :].rearrange("p (k t) -> p k t", k=4)),
                             waits=(e_trg,), inc=True)
                ev_wt_war[par] = e_evw
            last_wtr = now(st_)
    e_wT_done = now(sv_)
    e_a2_s = now(ss_)
    e_a2_t = now(st_)
    a2p.close()
    sHF.close()

    sKeep = ExitStack()
    WvuT = sb(sKeep, "WvuT", [128, 2 * D], BF16, side="right")
    a2_all = sb(sKeep, "a2_all", [128, 2048], BF16, side="right")
    a3 = ExitStack()
    av_sb = sb(a3, "av_sb", [128, 8 * D], BF16)
    WvdT = sb(a3, "WvdT", [128, NFC * 256], BF16)
    aux1 = sb(a3, "aux1", [128, NFC * 512], BF16)
    a3p = ExitStack()
    p_x1 = [ps(a3p, f"p_x1{i}", [128, 512]) for i in range(2)]
    p_a2 = [ps(a3p, f"p_a2{i}", [128, 512]) for i in range(2)]

    av_r = av_d.rearrange("(sc p) n -> p sc n", p=128)
    for sc in range(8):
        dg(sav, av_sb[:, sc * D:(sc + 1) * D], av_r[:, sc, :],
           waits=(e_a2_t, e_a2_s, e_wT_done))
    e_wvd = dg_split(swvd, WvdT[:, :].rearrange("p (c m) -> p c m", c=NFC),
               WvdT_d.rearrange("(c p) n -> p c n", p=128))
    e_wvu = dg(swvu, WvuT[:, :].rearrange("p (c m) -> p c m", c=2),
               WvuT_d.rearrange("(c p) n -> p c n", p=128))

    ev_x1_free = [None, None]
    ev_a2_free = [None, None]
    for half in range(2):
        for hc in range(NFC):
            par = hc % 2
            for sc in range(8):
                t_op(lambda e, hc=hc, sc=sc, half=half, par=par: e.matmul(
                        p_x1[par][:, :],
                        av_sb[:, sc * D + hc * 128: sc * D + hc * 128 + 128],
                        wT[:, sc * OWN + half * 512: sc * OWN + half * 512 + 512],
                        start=(sc == 0), stop=(sc == 7)),
                     waits=(now(sav), e_wT_done, ev_x1_free[par]), inc=(sc == 7))
            e_mm = now(st_)
            e_ev = s_op(lambda e, hc=hc, par=par: e.activation(
                            aux1[:, hc * 512:(hc + 1) * 512], p_x1[par][:, :], AF.Copy),
                        waits=(e_mm,), inc=True)
            ev_x1_free[par] = e_ev
        e_aux1 = now(ss_)
        for vbc in range(2):
            par = vbc
            for hc in range(NFC):
                t_op(lambda e, vbc=vbc, hc=hc, par=par: e.matmul(
                        p_a2[par][:, :],
                        WvdT[:, hc * 256 + vbc * 128: hc * 256 + vbc * 128 + 128],
                        aux1[:, hc * 512:(hc + 1) * 512],
                        start=(hc == 0), stop=(hc == NFC - 1)),
                     waits=(e_wvd, e_aux1, ev_a2_free[par]), inc=(hc == NFC - 1))
            e_mm = now(st_)
            e_ev = s_op(lambda e, half=half, vbc=vbc, par=par: e.activation(
                            a2_all[:, half * 1024 + vbc * 512:
                                   half * 1024 + vbc * 512 + 512],
                            p_a2[par][:, :], AF.Copy),
                        waits=(e_mm,), inc=True)
            ev_a2_free[par] = e_ev
    e_a2_done = now(ss_)
    e_a3_t = now(st_)
    a3p.close()
    a3.close()
    sA2.close()

    P.op("g", lambda e: e.collective_compute(
        "AllReduce", ALU.add, replica_groups=[list(range(8))],
        ins=[arin_d.opt()], outs=[arout_d.opt()]),
        waits=(now(sar),), inc=cc_)
    e_arr = now(cc_)
    e_arl = dg(sar, arsb[0:1, :], arout_d, waits=(e_arr,))
    v_op(lambda e: e.tensor_scalar(growA[0:1, 0:1], arsb[0:1, 0:1], 0.008192, None,
                                   ALU.add), waits=(e_arl,))
    v_op(lambda e: e.reciprocal(growA[0:1, 1:2], growA[0:1, 0:1]))
    e_hi = v_op(lambda e: e.tensor_scalar(growA[0:1, 2:3], growA[0:1, 1:2], 4096.0,
                                          None, ALU.mult), inc=True)
    v_op(lambda e: e.tensor_scalar(growB[0:1, :], sigrow[0:1, :], 2.5, None, ALU.mult),
         waits=(e_sig,))
    v_op(lambda e: e.scalar_tensor_tensor(growA[0:1, :], lvrow[0:1, :], growA[0:1, 2:3],
                                          growB[0:1, :], ALU.mult, ALU.add),
         waits=(e_hi, e_sumlv))
    v_op(lambda e: e.tensor_scalar_max(growA[0:1, :], growA[0:1, :], 0.0))
    v_op(lambda e: e.tensor_scalar_min(growA[0:1, :], growA[0:1, :], 5.0))
    v_op(lambda e: e.tensor_scalar(growA[0:1, :], growA[0:1, :], 2.0 / 3.0, -1.0 / 3.0,
                                   ALU.mult, ALU.add))
    v_op(lambda e: e.tensor_scalar_max(growA[0:1, :], growA[0:1, :], 0.0))
    v_op(lambda e: e.tensor_scalar_min(growA[0:1, :], growA[0:1, :], 1.0))
    e_tg = v_op(lambda e: e.tensor_scalar(growA[0:1, :], growA[0:1, :], gw1_sb[0:1, :],
                                          gb_sb[0:1, :], ALU.mult, ALU.add),
                waits=(e_cst,), inc=True)
    e_gt = s_op(lambda e: e.activation(growB[0:1, :], growA[0:1, :], AF.Sigmoid),
                waits=(e_tg, e_a2_done), inc=True)
    v_op(lambda e: e.tensor_scalar(growA[0:1, :], growB[0:1, :], 0.05, None, ALU.is_ge),
         waits=(e_gt,))
    e_gate = v_op(lambda e: e.tensor_mul(growB[0:1, :], growB[0:1, :], growA[0:1, :]),
                  inc=True)
    with nc.psum_tensor("p_gT", [128, 1], F32) as p_gT:
        ev_gTc = None
        for j in range(NT):
            t_op(lambda e, j=j: e.transpose(p_gT[:, :],
                                            growB[0:1, j * 128:(j + 1) * 128],
                                            id_f32[0:1, 0:1]),
                 waits=(e_gate, ev_gTc, e_a3_t, e_a2_done), inc=True)
            e_gtt = now(st_)
            e_gc = v_op(lambda e: e.tensor_copy(gcol[:, :], p_gT[:, :]),
                        waits=(e_gtt,), inc=True)
            ev_gTc = e_gc
            v_op(lambda e, j=j: e.reciprocal(rcol[:, :], lffn_all[:, j:j + 1]),
                 waits=(e_a2_s,))
            v_op(lambda e, j=j: e.tensor_tensor(gs_all[:, j:j + 1], gcol[:, :],
                                                rcol[:, :], ALU.mult))
    e_gs = now(sv_)

    a4 = ExitStack()
    xp2 = [sb(a4, f"xp2{i}", [128, D], F32) for i in range(2)]
    hh_sb = sb(a4, "hh_sb", [128, D], F32)
    ffn_sb = sb(a4, "ffn_sb", [128, D], F32)
    adj2 = sb(a4, "adj2", [128, D], F32)
    outsb = [sb(a4, f"outsb{i}", [128, D], F32) for i in range(2)]
    sink2 = sb(a4, "sink2", [128, D], BF16)
    a4p = ExitStack()
    p_ao4 = [ps(a4p, f"p_ao4{i}", [128, 512]) for i in range(2)]

    ev_xp2_free = [None, None]
    ev_out_free = [None, None]
    ev_ao4_free = [None, None]
    prev_hh_users = None
    prev_sq3 = None
    prev_fin = None
    last_stats4 = [None, None]
    for j in range(NT):
        sl = j % 2
        half = j // 4
        jj = j % 4
        stt_t = stats[sl]
        e_xl = dy(sxr2[sl], xp2[sl][:, :], xp_d[j * 128:(j + 1) * 128, :],
                  waits=(now(sxp), ev_xp2_free[sl], e_a3_t, e_a2_done, e_gs))
        e_hh = s_op(lambda e, sl=sl, j=j: e.activation(
            hh_sb[:, :], xp2[sl][:, :], AF.Identity, bias=nmr2_k[:, j:j + 1],
            scale=rstd2_k[:, j:j + 1]),
            waits=(e_xl, e_p5_v, prev_hh_users), inc=True)
        cmb_evs = []
        for hc in range(8):
            par = hc % 2
            for vbc in range(2):
                t_op(lambda e, half=half, vbc=vbc, hc=hc, par=par, jj=jj: e.matmul(
                        p_ao4[par][:, :],
                        a2_all[:, half * 1024 + vbc * 512 + jj * 128:
                               half * 1024 + vbc * 512 + jj * 128 + 128],
                        WvuT[:, vbc * D + hc * 512: vbc * D + hc * 512 + 512],
                        start=(vbc == 0), stop=(vbc == 1)),
                     waits=(e_a2_done, e_wvu, ev_ao4_free[par], e_gs),
                     inc=(vbc == 1))
            e_mm = now(st_)
            e_cmb = v_op(lambda e, hc=hc, par=par, j=j, stt_t=stt_t: e.scalar_tensor_tensor(
                            ffn_sb[:, hc * 512:(hc + 1) * 512], p_ao4[par][:, :],
                            gs_all[:, j:j + 1], hh_sb[:, hc * 512:(hc + 1) * 512],
                            ALU.mult, ALU.add,
                            accum_out=stt_t[:, 9 + par:10 + par]),
                         waits=(e_mm, e_hh, e_gs, prev_sq3, last_stats4[sl]),
                         inc=True)
            ev_ao4_free[par] = e_cmb
            cmb_evs.append(e_cmb)
            if hc == 0:
                v_op(lambda e, stt_t=stt_t: e.tensor_copy(stt_t[:, 1:2],
                                                          stt_t[:, 9:10]))
            else:
                v_op(lambda e, par=par, stt_t=stt_t: e.tensor_add(
                    stt_t[:, 1:2], stt_t[:, 1:2], stt_t[:, 9 + par:10 + par]))
        e_ffull = now(sv_)
        e_sq3 = s_op(lambda e, stt_t=stt_t: e.activation(
            sink2[:, :], ffn_sb[:, :], AF.Square, accum_out=stt_t[:, 2:3]),
            waits=(e_ffull,), inc=True)
        prev_sq3 = e_sq3
        v_op(lambda e, stt_t=stt_t: e.tensor_scalar(stt_t[:, 3:4], stt_t[:, 1:2],
                                                    1.0 / D, None, ALU.mult))
        v_op(lambda e, stt_t=stt_t: e.tensor_scalar(stt_t[:, 4:5], stt_t[:, 2:3],
                                                    1.0 / D, None, ALU.mult),
             waits=(e_sq3,))
        v_op(lambda e, stt_t=stt_t: e.tensor_tensor(stt_t[:, 5:6], stt_t[:, 3:4],
                                                    stt_t[:, 3:4], ALU.mult))
        e_var3 = v_op(lambda e, stt_t=stt_t: e.tensor_sub(stt_t[:, 6:7], stt_t[:, 4:5],
                                                          stt_t[:, 5:6]), inc=True)
        s_op(lambda e, stt_t=stt_t: e.activation(stt_t[:, 7:8], stt_t[:, 6:7], AF.Ln,
                                                 bias=EPS), waits=(e_var3,))
        e_rstd3 = s_op(lambda e, stt_t=stt_t: e.activation(stt_t[:, 7:8], stt_t[:, 7:8],
                                                           AF.Exp, scale=-0.5), inc=True)
        e_nmr3 = v_op(lambda e, stt_t=stt_t: e.tensor_scalar(
            stt_t[:, 8:9], stt_t[:, 3:4], stt_t[:, 7:8], -1.0, ALU.mult, ALU.mult),
            waits=(e_rstd3,), inc=True)
        e_adj3 = s_op(lambda e, sl=sl, stt_t=stt_t: e.activation(
            adj2[:, :], xp2[sl][:, :], AF.Identity, bias=stt_t[:, 8:9]),
            waits=(e_nmr3, prev_fin), inc=True)
        ev_xp2_free[sl] = e_adj3
        e_fin = v_op(lambda e, sl=sl, stt_t=stt_t: e.scalar_tensor_tensor(
            outsb[sl][:, :], ffn_sb[:, :], stt_t[:, 7:8], adj2[:, :],
            ALU.mult, ALU.add),
            waits=(e_adj3, ev_out_free[sl]), inc=True)
        prev_fin = e_fin
        prev_hh_users = e_fin
        last_stats4[sl] = e_fin
        e_ow = dy(soo[sl], out_d[j * 128:(j + 1) * 128, :], outsb[sl][:, :],
                  waits=(e_fin,))
        ev_out_free[sl] = e_ow
    if debug:
        dq = nc.dram_tensor("dbg_qspill", [D, OWN], BF16, kind="ExternalOutput").ap()
        dk = nc.dram_tensor("dbg_kvin", [KVROWS, OWN], BF16, kind="ExternalOutput").ap()
        da = nc.dram_tensor("dbg_ao", [OWN, D], F32, kind="ExternalOutput").ap()
        dx = nc.dram_tensor("dbg_xp", [OWN, D], F32, kind="ExternalOutput").ap()
        dr = nc.dram_tensor("dbg_ar", [2, 16], F32, kind="ExternalOutput").ap()
        dy(sout, dq, qspill_d, waits=(now(soo[0]), now(soo[1]),))
        dy(sout, dk, kvin_d)
        dy(sout, da, ao_d)
        dy(sout, dx, xp_d)
        dy(sout, dr[0:1, :], arin_d)
        dy(sout, dr[1:2, :], arout_d)
        ds = nc.dram_tensor("dbg_stats", [128, 64], F32, kind="ExternalOutput").ap()
        dy(sout, ds[:, 0:8], sums_ao[:, :])
        dy(sout, ds[:, 8:16], sumsq_ao[:, :])
        dy(sout, ds[:, 16:24], lv_all[:, :])
        dy(sout, ds[:, 24:32], rstd2_k[:, :])
        dy(sout, ds[:, 32:40], nmr2_k[:, :])
        dy(sout, ds[:, 40:48], lffn_all[:, :])
        dy(sout, ds[:, 48:56], gs_all[:, :])
        dy(sout, ds[:, 56:64], stats[0][:, 0:8])

    P.op("y", lambda e: e.nop(), waits=(now(soo[0]), now(soo[1]), now(sout)))
    a4p.close()
    a4.close()
    sKeep.close()

    with nc.Block() as block:
        P.emit(nc, block)
    ctx.close()
    return nc


_NC_CACHE = [None]


def prepare_in_maps(L0_attn_x, L0_attn_wq, L0_attn_wk, L0_attn_wv, L0_attn_wo,
                    L0_ffn_Wq, L0_ffn_Wr, L0_ffn_aux_keys, L0_ffn_aux_values,
                    L0_ffn_Wvd, L0_ffn_Wvu, L0_ffn_Wu1, L0_ffn_bu1, L0_ffn_Wu2,
                    L0_ffn_bu2, L0_ffn_gate_w1, L0_ffn_gate_bias, L0_ffn_log_rel):
    f = np.float32
    x = np.asarray(L0_attn_x, f)
    shared = dict(
        i_wq=np.ascontiguousarray(np.asarray(L0_attn_wq, f)),
        i_wk=np.ascontiguousarray(np.asarray(L0_attn_wk, f)),
        i_wv=np.ascontiguousarray(np.asarray(L0_attn_wv, f)),
        i_wo=np.ascontiguousarray(np.asarray(L0_attn_wo, f)),
        i_fWqT=np.ascontiguousarray(np.asarray(L0_ffn_Wq, f).T),
        i_fWrT=np.ascontiguousarray(np.asarray(L0_ffn_Wr, f).T),
        i_akT=np.ascontiguousarray(np.asarray(L0_ffn_aux_keys, f).T),
        i_av=np.ascontiguousarray(np.asarray(L0_ffn_aux_values, f)),
        i_WvdT=np.ascontiguousarray(np.asarray(L0_ffn_Wvd, f).T),
        i_WvuT=np.ascontiguousarray(np.asarray(L0_ffn_Wvu, f).T),
        i_Wu1T=np.ascontiguousarray(np.asarray(L0_ffn_Wu1, f).T),
        i_bu1c=np.ascontiguousarray(np.asarray(L0_ffn_bu1, f).reshape(8, 128).T),
        i_wu2c=np.ascontiguousarray(np.asarray(L0_ffn_Wu2, f).reshape(8, 128).T),
        i_bu2=np.asarray(L0_ffn_bu2, f).reshape(1, 1),
        i_gw1=np.asarray(L0_ffn_gate_w1, f).reshape(1, 1),
        i_gb=np.asarray(L0_ffn_gate_bias, f).reshape(1, 1),
        i_lrel=np.ascontiguousarray(np.asarray(L0_ffn_log_rel, f).reshape(1, 1024)),
        c_scal=np.ascontiguousarray(np.tile(np.array(
            [EPS, 1.0, -0.5, 0.125, 1.0 / math.sqrt(48.0), ISQHD, GC1, 0.0],
            dtype=np.float32), (128, 1))),
        c_idbf=np.eye(128, dtype=ml_dtypes.bfloat16),
        c_idf32=np.eye(128, dtype=np.float32),
        c_ones=np.ones((128, 128), dtype=ml_dtypes.bfloat16),
        c_rot=_rot_matrix(),
    )
    in_maps = []
    idx_by_r = {}
    for c in range(8):
        b, r = c // 4, c % 4
        if r not in idx_by_r:
            j = np.arange(NT)[:, None]
            s = np.arange(TS)[None, :]
            idx_by_r[r] = (512 * j + 4 * s + r).reshape(-1)
        idx = idx_by_r[r]
        cos, sin, dm = _host_tables(r)
        m = dict(shared)
        m["x_own"] = np.ascontiguousarray(x[b][idx])
        m["c_cos"] = cos
        m["c_sin"] = sin
        m["c_dmask"] = dm
        in_maps.append(m)
    return in_maps, idx_by_r


def kernel(**inputs):
    in_maps, idx_by_r = prepare_in_maps(**inputs)
    if _NC_CACHE[0] is None:
        _NC_CACHE[0] = build_program()
    nc = _NC_CACHE[0]
    res = run_bass_kernel_spmd(nc, in_maps, core_ids=list(range(8)))
    out = np.empty((B, T, D), dtype=np.float32)
    for c in range(8):
        b, r = c // 4, c % 4
        out[b][idx_by_r[r]] = np.asarray(res.results[c]["o_out"], np.float32)
    return out
